# version 03
# baseline (speedup 1.0000x reference)
import numpy as np
import ml_dtypes
from contextlib import ExitStack
import concourse.bass as bass
import concourse.mybir as mybir
from concourse.alu_op_type import AluOpType as ALU

F32 = mybir.dt.float32
BF16 = mybir.dt.bfloat16
U16 = mybir.dt.uint16
U8 = mybir.dt.uint8
FP8 = mybir.dt.float8e4
AF = mybir.ActivationFunctionType
BF = ml_dtypes.bfloat16

D = 2048
TC = 1024
NCH = 16


def perm_evens_odds(n):
    p = []
    for b in range(0, n, 256):
        p.extend(range(b, b + 256, 2))
        p.extend(range(b + 1, b + 256, 2))
    return np.array(p)



_PHASE_N = [0]


def _pn(name):
    return f"{name}_ph{_PHASE_N[0]}"


def _next_phase():
    _PHASE_N[0] += 1


class Sem:
    def __init__(self, h):
        self.h = h
        self.n = 0

    def inc(self, k=1):
        self.n += k
        return self.n


class P:
    ENGINES = ("sync", "gpsimd", "vector", "scalar", "tensor")

    def __init__(self):
        self.q = {e: [] for e in self.ENGINES}

    def add(self, eng, fn):
        self.q[eng].append(fn)

    def sy(self, fn): self.add("sync", fn)
    def g(self, fn): self.add("gpsimd", fn)
    def v(self, fn): self.add("vector", fn)
    def s(self, fn): self.add("scalar", fn)
    def t(self, fn): self.add("tensor", fn)

    def emit(self, blk):
        for eng in self.ENGINES:
            fns = self.q[eng]
            if not fns:
                continue
            def body(e, fns=fns):
                for f in fns:
                    f(e)
            getattr(blk, eng)(body)


def ln_emit(p, sems, xsb, hsb, pp, C, make_rs=False, rs_sb=None, tc=TC, nch=NCH):
    v, t, s = sems["v"], sems["t"], sems["s"]
    half = tc // 512

    def sq(sc):
        li = None
        for c in range(nch):
            li = sc.square(hsb[:, c * tc:(c + 1) * tc], xsb[:, c * tc:(c + 1) * tc])
        li.then_inc(s.h, 1)
    p.s(sq); s.inc()
    s_sq = s.n

    b_s1 = pp[0:1, 4 * 512: 4 * 512 + tc]
    b_s2 = pp[0:1, 6 * 512: 6 * 512 + tc]

    def stats(te, s_sq=s_sq):
        te.wait_ge(s.h, s_sq)
        li = None
        for th in range(half):
            for c in range(nch):
                li = te.matmul(b_s1[:, th * 512:(th + 1) * 512], C["ones_c_f32"],
                               xsb[:, c * tc + th * 512: c * tc + (th + 1) * 512],
                               start=(c == 0), stop=(c == nch - 1))
        for th in range(half):
            for c in range(nch):
                li = te.matmul(b_s2[:, th * 512:(th + 1) * 512], C["ones_c_bf"],
                               hsb[:, c * tc + th * 512: c * tc + (th + 1) * 512],
                               start=(c == 0), stop=(c == nch - 1))
        li.then_inc(t.h, 1)
    p.t(stats); t.inc()
    t_stats = t.n

    mrow, rrow, vrow, srow = C["mrow"], C["rrow"], C["vrow"], C["srow"]

    def rows(vv, t_stats=t_stats):
        vv.wait_ge(t.h, t_stats)
        vv.tensor_scalar_mul(mrow[0:1, 0:tc], b_s1, 1.0 / D)
        vv.tensor_tensor(vrow[0:1, 0:tc], mrow[0:1, 0:tc], mrow[0:1, 0:tc], ALU.mult)
        vv.scalar_tensor_tensor(vrow[0:1, 0:tc], b_s2, 1.0 / D, vrow[0:1, 0:tc],
                                ALU.mult, ALU.subtract).then_inc(v.h, 1)
    p.v(rows); v.inc()
    v_rows = v.n

    def dosqrt(sc, v_rows=v_rows):
        sc.wait_ge(v.h, v_rows)
        sc.activation(srow[0:1, 0:tc], vrow[0:1, 0:tc], AF.Sqrt,
                      bias=C["eps"], scale=1.0).then_inc(s.h, 1)
    p.s(dosqrt); s.inc()
    s_sqrt = s.n

    def dorecip(vv, s_sqrt=s_sqrt):
        vv.wait_ge(s.h, s_sqrt)
        vv.reciprocal(rrow[0:1, 0:tc], srow[0:1, 0:tc]).then_inc(v.h, 1)
    p.v(dorecip); v.inc()
    v_recip = v.n

    b_mb = pp[:, 4 * 512: 4 * 512 + tc]
    b_rb = pp[:, 6 * 512: 6 * 512 + tc]

    def bcast(te, v_recip=v_recip):
        te.wait_ge(v.h, v_recip)
        li = None
        for th in range(half):
            te.matmul(b_mb[:, th * 512:(th + 1) * 512], C["ones_r_f32"],
                      mrow[0:1, th * 512:(th + 1) * 512], start=True, stop=True)
            li = te.matmul(b_rb[:, th * 512:(th + 1) * 512], C["ones_r_f32"],
                           rrow[0:1, th * 512:(th + 1) * 512], start=True, stop=True)
        li.then_inc(t.h, 1)
    p.t(bcast); t.inc()
    t_bcast = t.n

    def mk_h(vv, t_bcast=t_bcast):
        vv.wait_ge(t.h, t_bcast)
        li = None
        for c in range(nch):
            vv.tensor_tensor(C["ftmp"][:, 0:tc], xsb[:, c * tc:(c + 1) * tc], b_mb, ALU.subtract)
            li = vv.tensor_tensor(hsb[:, c * tc:(c + 1) * tc], C["ftmp"][:, 0:tc], b_rb, ALU.mult)
        li.then_inc(v.h, 1)
    p.v(mk_h); v.inc()
    v_h = v.n

    if make_rs:
        b_rs = pp[0:1, 4 * 512: 4 * 512 + tc]

        def rs_mm(te, v_h=v_h):
            te.wait_ge(v.h, v_h)
            li = None
            for th in range(half):
                for c in range(nch):
                    li = te.matmul(b_rs[:, th * 512:(th + 1) * 512], C["ones_c_bf"],
                                   hsb[:, c * tc + th * 512: c * tc + (th + 1) * 512],
                                   start=(c == 0), stop=(c == nch - 1))
            li.then_inc(t.h, 1)
        p.t(rs_mm); t.inc()
        t_rs = t.n

        def rs_cp(vv, t_rs=t_rs):
            vv.wait_ge(t.h, t_rs)
            vv.tensor_copy(rs_sb[0:1, 0:tc], b_rs).then_inc(v.h, 1)
        p.v(rs_cp); v.inc()
    return v_h


def alloc_common(ctx, nc, roww=TC, ftmpw=TC):
    C = {}
    for name, shape, dt in [("ones_c_f32", [128, 1], F32), ("ones_c_bf", [128, 1], BF16),
                            ("ones_r_f32", [1, 128], F32), ("c128", [1, 128], BF16),
                            ("eps", [1, 1], F32), ("mrow", [1, roww], F32), ("rrow", [1, roww], F32),
                            ("vrow", [1, roww], F32), ("srow", [1, roww], F32), ("ftmp", [128, ftmpw], F32)]:
        h = ctx.enter_context(nc.sbuf_tensor(_pn(name), shape, dt))
        C[name] = h[:] if name in ("ones_c_f32", "ones_c_bf", "ones_r_f32", "eps") else h
    return C


def init_consts_emit(p, sems, C):
    v = sems["v"]

    def ic(vv):
        vv.memset(C["ones_c_f32"], 1.0)
        vv.memset(C["ones_c_bf"], 1.0)
        vv.memset(C["ones_r_f32"], 1.0)
        vv.memset(C["c128"][:], -128.0)
        vv.memset(C["eps"], 1e-5).then_inc(v.h, 1)
    p.v(ic); v.inc()


def build_l1ffn(stages=99):
    nc = bass.Bass("TRN2", target_bir_lowering=False)
    xT = nc.dram_tensor("xT", [D, TC], F32, kind="ExternalInput").ap()
    wg8 = nc.dram_tensor("wg8", [D, 8192], U8, kind="ExternalInput").ap()
    wu8 = nc.dram_tensor("wu8", [D, 8192], U8, kind="ExternalInput").ap()
    wd8 = nc.dram_tensor("wd8", [8192, D], U8, kind="ExternalInput").ap()
    gsc = nc.dram_tensor("gsc", [128, 64], F32, kind="ExternalInput").ap()
    usc = nc.dram_tensor("usc", [128, 64], F32, kind="ExternalInput").ap()
    dsc = nc.dram_tensor("dsc", [128, 16], F32, kind="ExternalInput").ap()
    xo = nc.dram_tensor("xo", [D, TC], F32, kind="ExternalOutput").ap()
    with ExitStack() as octx:
        S = _mk_sems_fused(octx, nc)
        blk = octx.enter_context(nc.Block())
        emit_l1ffn(nc, blk, S, xT, wg8, wu8, wd8, gsc, usc, dsc, xo)
    return nc


def emit_l1ffn(nc, blk, S, xT, wg8, wu8, wd8, gsc, usc, dsc, xo, stages=99):
    _next_phase()

    I = 8192
    NIC = I // 128
    GRP = 16
    NGRP = NIC // GRP
    NIS = 4

    with ExitStack() as ctx:
        xsb = ctx.enter_context(nc.sbuf_tensor(_pn("xsb"), [128, NCH * TC], F32))
        hsb = ctx.enter_context(nc.sbuf_tensor(_pn("hsb"), [128, NCH * TC], BF16))
        inter = ctx.enter_context(nc.sbuf_tensor(_pn("inter"), [128, GRP * TC], BF16))
        g_sb = ctx.enter_context(nc.sbuf_tensor(_pn("g_sb"), [128, 4 * TC], BF16))
        sig = ctx.enter_context(nc.sbuf_tensor(_pn("sig"), [128, 4 * TC], BF16))
        wgt = ctx.enter_context(nc.sbuf_tensor(_pn("wgt"), [128, 2 * 512], BF16))
        wut = ctx.enter_context(nc.sbuf_tensor(_pn("wut"), [128, 2 * 512], BF16))
        wdt = ctx.enter_context(nc.sbuf_tensor(_pn("wdt"), [128, 2 * 512], BF16))
        gsc_sb = ctx.enter_context(nc.sbuf_tensor(_pn("gsc_sb"), [128, 64], F32))
        usc_sb = ctx.enter_context(nc.sbuf_tensor(_pn("usc_sb"), [128, 64], F32))
        dsc_sb = ctx.enter_context(nc.sbuf_tensor(_pn("dsc_sb"), [128, 16], F32))
        rs_sb = ctx.enter_context(nc.sbuf_tensor(_pn("rs_sb"), [1, TC], BF16))
        rs2_sb = ctx.enter_context(nc.sbuf_tensor(_pn("rs2_sb"), [1, TC], BF16))
        btmp = ctx.enter_context(nc.sbuf_tensor(_pn("btmp"), [128, 512], BF16))
        C = alloc_common(ctx, nc)
        pp = ctx.enter_context(nc.psum_tensor(_pn("pp"), [128, 8 * 512], F32))

        p = P()
        ld, w, v, t, s, st = S["ld"], S["w"], S["v"], S["t"], S["s"], S["st"]
        s_ld, s_w, s_v, s_t, s_s, s_st = ld.h, w.h, v.h, t.h, s.h, st.h
        sems = {"v": v, "t": t, "s": s}

        init_consts_emit(p, sems, C)
        v_c0 = v.n
        p.t(lambda te, v_c0=v_c0: te.wait_ge(s_v, v_c0))
        p.s(lambda sc, v_c0=v_c0: sc.wait_ge(s_v, v_c0))

        def loads(sy):
            xT_ap = bass.AP(xT.tensor, 0, [[TC, 128], [128 * TC, NCH], [1, TC]])
            sy.dma_start(xsb[:], xT_ap).then_inc(s_ld, 16)
            sy.dma_start(gsc_sb[:], gsc).then_inc(s_ld, 16)
            sy.dma_start(usc_sb[:], usc).then_inc(s_ld, 16)
            sy.dma_start(dsc_sb[:], dsc).then_inc(s_ld, 16)
        p.sy(loads); ld.inc(64)
        ld_x = ld.n

        p.s(lambda sc: sc.wait_ge(s_ld, ld_x))
        p.v(lambda vv: vv.wait_ge(s_ld, ld_x))
        p.t(lambda te: te.wait_ge(s_ld, ld_x))

        if stages >= 2:
            ln_emit(p, sems, xsb, hsb, pp, C, make_rs=True, rs_sb=rs_sb)
            v_ln = v.n
            p.t(lambda te, v_ln=v_ln: te.wait_ge(s_v, v_ln))

        s_sig_cur = [0]
        for grp in range(NGRP if stages >= 3 else 0):
            for isl in range(NIS):
                isg = grp * NIS + isl
                for mat in range(2):
                    wt = wgt if mat == 0 else wut
                    w8 = wg8 if mat == 0 else wu8
                    for kc in range(NCH):
                        buf = kc % 2
                        war = t.n - 1 if t.n >= 1 else None
                        def wl(g, kc=kc, isg=isg, wt=wt, w8=w8, buf=buf, war=war):
                            if war is not None:
                                g.wait_ge(s_t, war)
                            g.dma_start(wt[:, buf * 512:(buf + 1) * 512],
                                        w8[kc * 128:(kc + 1) * 128, isg * 512:(isg + 1) * 512]
                                        ).then_inc(s_w, 16)
                        p.g(wl); w.inc(16)
                        w_tile = w.n

                        def mm(te, kc=kc, buf=buf, w_tile=w_tile, wt=wt):
                            te.wait_ge(s_w, w_tile)
                            li = None
                            for ic in range(4):
                                for th in range(2):
                                    li = te.matmul(pp[:, (ic * 2 + th) * 512:(ic * 2 + th + 1) * 512],
                                                   wt[:, buf * 512 + ic * 128: buf * 512 + ic * 128 + 128],
                                                   hsb[:, kc * TC + th * 512: kc * TC + th * 512 + 512],
                                                   start=(kc == 0), stop=False)
                            li.then_inc(s_t, 1)
                        p.t(mm); t.inc()

                    def corr(te):
                        li = None
                        for ic in range(4):
                            for th in range(2):
                                li = te.matmul(pp[:, (ic * 2 + th) * 512:(ic * 2 + th + 1) * 512],
                                               C["c128"][:], rs_sb[0:1, th * 512:(th + 1) * 512],
                                               start=False, stop=True)
                        li.then_inc(s_t, 1)
                    p.t(corr); t.inc()
                    t_mat = t.n

                    if mat == 0:
                        def evac_g(vv, isg=isg, t_mat=t_mat):
                            vv.wait_ge(s_t, t_mat)
                            li = None
                            for ic in range(4):
                                gic = isg * 4 + ic
                                for th in range(2):
                                    li = vv.tensor_scalar(
                                        g_sb[:, ic * TC + th * 512: ic * TC + th * 512 + 512],
                                        pp[:, (ic * 2 + th) * 512:(ic * 2 + th + 1) * 512],
                                        gsc_sb[:, gic:gic + 1], None, ALU.mult)
                            li.then_inc(s_v, 1)
                        p.v(evac_g); v.inc()
                        v_g = v.n

                        def mk_sig(sc, v_g=v_g):
                            sc.wait_ge(s_v, v_g)
                            li = None
                            for ic in range(4):
                                li = sc.activation(sig[:, ic * TC:(ic + 1) * TC],
                                                   g_sb[:, ic * TC:(ic + 1) * TC],
                                                   AF.Sigmoid, scale=1.0)
                            li.then_inc(s_s, 1)
                        p.s(mk_sig); s.inc()
                        s_sig_cur[0] = s.n
                        p.t(lambda te, v_g=v_g: te.wait_ge(s_v, v_g))
                    else:
                        def evac_u(vv, isg=isg, isl=isl, t_mat=t_mat, s_sig=s_sig_cur[0]):
                            vv.wait_ge(s_t, t_mat)
                            vv.wait_ge(s_s, s_sig)
                            li = None
                            for ic in range(4):
                                gic = isg * 4 + ic
                                iin = isl * 4 + ic
                                for th in range(2):
                                    vv.tensor_scalar(C["ftmp"][:, 0:512],
                                                     pp[:, (ic * 2 + th) * 512:(ic * 2 + th + 1) * 512],
                                                     usc_sb[:, gic:gic + 1], None, ALU.mult)
                                    vv.tensor_tensor(btmp[:, 0:512],
                                                     sig[:, ic * TC + th * 512: ic * TC + th * 512 + 512],
                                                     g_sb[:, ic * TC + th * 512: ic * TC + th * 512 + 512],
                                                     ALU.mult)
                                    li = vv.tensor_tensor(
                                        inter[:, iin * TC + th * 512: iin * TC + th * 512 + 512],
                                        btmp[:, 0:512], C["ftmp"][:, 0:512], ALU.mult)
                            li.then_inc(s_v, 1)
                        p.v(evac_u); v.inc()
                        v_u = v.n
                        p.t(lambda te, v_u=v_u: te.wait_ge(s_v, v_u))

            def rs2_mm(te):
                li = None
                for th in range(2):
                    for iin in range(GRP):
                        li = te.matmul(pp[0:1, th * 512:(th + 1) * 512], C["ones_c_bf"],
                                       inter[:, iin * TC + th * 512: iin * TC + th * 512 + 512],
                                       start=(iin == 0), stop=(iin == GRP - 1))
                li.then_inc(s_t, 1)
            p.t(rs2_mm); t.inc()
            t_rs2 = t.n

            def rs2_cp(vv, t_rs2=t_rs2):
                vv.wait_ge(s_t, t_rs2)
                vv.tensor_copy(rs2_sb[0:1, :], pp[0:1, 0:TC]).then_inc(s_v, 1)
            p.v(rs2_cp); v.inc()
            v_rs2 = v.n
            p.t(lambda te, v_rs2=v_rs2: te.wait_ge(s_v, v_rs2))

            for hg in range(4):
                for ic in range(GRP):
                    nlin = (grp * 4 + hg) * GRP + ic
                    buf = nlin % 2
                    war = t.n - 1 if t.n >= 1 else None

                    def wdl(g, grp=grp, hg=hg, ic=ic, buf=buf, war=war):
                        if war is not None:
                            g.wait_ge(s_t, war)
                        g.dma_start(wdt[:, buf * 512:(buf + 1) * 512],
                                    wd8[(grp * GRP + ic) * 128:(grp * GRP + ic + 1) * 128,
                                        hg * 512:(hg + 1) * 512]).then_inc(s_w, 16)
                    p.g(wdl); w.inc(16)
                    w_tile = w.n

                    def dmm(te, ic=ic, buf=buf, w_tile=w_tile):
                        te.wait_ge(s_w, w_tile)
                        li = None
                        for hc in range(4):
                            for th in range(2):
                                li = te.matmul(pp[:, (hc * 2 + th) * 512:(hc * 2 + th + 1) * 512],
                                               wdt[:, buf * 512 + hc * 128: buf * 512 + hc * 128 + 128],
                                               inter[:, ic * TC + th * 512: ic * TC + th * 512 + 512],
                                               start=(ic == 0), stop=False)
                        li.then_inc(s_t, 1)
                    p.t(dmm); t.inc()

                def dcorr(te):
                    li = None
                    for hc in range(4):
                        for th in range(2):
                            li = te.matmul(pp[:, (hc * 2 + th) * 512:(hc * 2 + th + 1) * 512],
                                           C["c128"][:], rs2_sb[0:1, th * 512:(th + 1) * 512],
                                           start=False, stop=True)
                    li.then_inc(s_t, 1)
                p.t(dcorr); t.inc()
                t_down = t.n

                def xacc(vv, hg=hg, t_down=t_down):
                    vv.wait_ge(s_t, t_down)
                    li = None
                    for hc in range(4):
                        h = hg * 4 + hc
                        for th in range(2):
                            li = vv.scalar_tensor_tensor(
                                xsb[:, h * TC + th * 512: h * TC + th * 512 + 512],
                                pp[:, (hc * 2 + th) * 512:(hc * 2 + th + 1) * 512],
                                dsc_sb[:, h:h + 1],
                                xsb[:, h * TC + th * 512: h * TC + th * 512 + 512],
                                ALU.mult, ALU.add)
                    li.then_inc(s_v, 1)
                p.v(xacc); v.inc()
                v_x = v.n
                p.t(lambda te, v_x=v_x: te.wait_ge(s_v, v_x))

        v_done = v.n

        stv = st.n + 16

        def store(sy, v_done=v_done, stv=stv):
            sy.wait_ge(s_v, v_done)
            xo_ap = bass.AP(xo.tensor, 0, [[TC, 128], [128 * TC, NCH], [1, TC]])
            sy.dma_start(xo_ap, xsb[:]).then_inc(s_st, 16)
            sy.wait_ge(s_st, stv)
        p.sy(store); st.inc(16)

        p.emit(blk)


def build_l0ffn(stages=99):
    nc = bass.Bass("TRN2", target_bir_lowering=False)
    xT = nc.dram_tensor("xT", [D, TC], F32, kind="ExternalInput").ap()
    pg8 = nc.dram_tensor("pg8", [1024, 16384], U8, kind="ExternalInput").ap()
    pu8 = nc.dram_tensor("pu8", [1024, 16384], U8, kind="ExternalInput").ap()
    pd8 = nc.dram_tensor("pd8", [8192, D], U8, kind="ExternalInput").ap()
    gscT = nc.dram_tensor("gscT", [64, 16384], F32, kind="ExternalInput").ap()
    uscT = nc.dram_tensor("uscT", [64, 16384], F32, kind="ExternalInput").ap()
    dscT = nc.dram_tensor("dscT", [512, D], F32, kind="ExternalInput").ap()
    xo = nc.dram_tensor("xo", [D, TC], F32, kind="ExternalOutput").ap()
    with ExitStack() as octx:
        S = _mk_sems_fused(octx, nc)
        blk = octx.enter_context(nc.Block())
        emit_l0ffn(nc, blk, S, xT, pg8, pu8, pd8, gscT, uscT, dscT, xo)
    return nc


def emit_l0ffn(nc, blk, S, xT, pg8, pu8, pd8, gscT, uscT, dscT, xo, stages=99):
    _next_phase()

    I = 16384
    NIC = I // 128
    GRP = 16
    NGRP = NIC // GRP
    NIS = 4
    GELU_S = 1.5957691216057308

    with ExitStack() as ctx:
        xsb = ctx.enter_context(nc.sbuf_tensor(_pn("xsb"), [128, NCH * TC], F32))
        hsb = ctx.enter_context(nc.sbuf_tensor(_pn("hsb"), [128, NCH * TC], BF16))
        inter = ctx.enter_context(nc.sbuf_tensor(_pn("inter"), [128, GRP * TC], BF16))
        g_sb = ctx.enter_context(nc.sbuf_tensor(_pn("g_sb"), [128, 4 * TC], BF16))
        sig = ctx.enter_context(nc.sbuf_tensor(_pn("sig"), [128, 4 * TC], BF16))
        wgb = ctx.enter_context(nc.sbuf_tensor(_pn("wgb"), [128, 8 * 512], BF16))
        wub = ctx.enter_context(nc.sbuf_tensor(_pn("wub"), [128, 8 * 512], BF16))
        wdb = ctx.enter_context(nc.sbuf_tensor(_pn("wdb"), [128, 8 * 512], BF16))
        pk = ctx.enter_context(nc.sbuf_tensor(_pn("pk"), [128, 3 * 512], U8))
        f8a = ctx.enter_context(nc.sbuf_tensor(_pn("f8a"), [128, 256], U16))
        f8b = ctx.enter_context(nc.sbuf_tensor(_pn("f8b"), [128, 256], U16))
        f8lo = ctx.enter_context(nc.sbuf_tensor(_pn("f8lo"), [128, 512], U8))
        f8hi = ctx.enter_context(nc.sbuf_tensor(_pn("f8hi"), [128, 512], U8))
        srg = ctx.enter_context(nc.sbuf_tensor(_pn("srg"), [128, 512], F32))
        sru = ctx.enter_context(nc.sbuf_tensor(_pn("sru"), [128, 512], F32))
        btmp = ctx.enter_context(nc.sbuf_tensor(_pn("btmp"), [128, TC], BF16))
        C = alloc_common(ctx, nc)
        pp = ctx.enter_context(nc.psum_tensor(_pn("pp"), [128, 8 * 512], F32))

        p = P()
        ld, w, v, t, s, st = S["ld"], S["w"], S["v"], S["t"], S["s"], S["st"]
        s_ld, s_w, s_v, s_t, s_s, s_st = ld.h, w.h, v.h, t.h, s.h, st.h
        sems = {"v": v, "t": t, "s": s}

        init_consts_emit(p, sems, C)
        v_c0 = v.n
        p.t(lambda te, v_c0=v_c0: te.wait_ge(s_v, v_c0))
        p.s(lambda sc, v_c0=v_c0: sc.wait_ge(s_v, v_c0))

        def loads(sy):
            xT_ap = bass.AP(xT.tensor, 0, [[TC, 128], [128 * TC, NCH], [1, TC]])
            sy.dma_start(xsb[:], xT_ap).then_inc(s_ld, 16)
        p.sy(loads); ld.inc(16)
        ld_x = ld.n

        p.s(lambda sc: sc.wait_ge(s_ld, ld_x))
        p.v(lambda vv: vv.wait_ge(s_ld, ld_x))
        p.t(lambda te: te.wait_ge(s_ld, ld_x))

        if stages >= 2:
            ln_emit(p, sems, xsb, hsb, pp, C)
            v_ln = v.n
            p.t(lambda te, v_ln=v_ln: te.wait_ge(s_v, v_ln))

        def emit_dequant(srcp8, scT, bc, col0, wb, wb_col0, pk_off, sr_sb, war_t, war_v):
            war_pk = v.n if v.n > 0 else None

            def pk_l(g, bc=bc, col0=col0, pk_off=pk_off, war_pk=war_pk):
                if war_pk is not None:
                    g.wait_ge(s_v, war_pk)
                g.dma_start(pk[:, pk_off:pk_off + 512],
                            srcp8[bc * 128:(bc + 1) * 128, col0:col0 + 512]).then_inc(s_w, 16)
            p.g(pk_l); w.inc(16)
            w_pk = w.n

            def sr_l(sy, bc=bc, col0=col0, scT=scT, sr_sb=sr_sb, war_pk=war_pk):
                if war_pk is not None:
                    sy.wait_ge(s_v, war_pk)
                src = bass.AP(scT.tensor, bc * 8 * scT.shape[1] + col0,
                              [[scT.shape[1], 8], [0, 16], [1, 512]])
                sy.dma_start(sr_sb[:], src).then_inc(s_w, 16)
            p.sy(sr_l); w.inc(16)
            w_sr = w.n

            def simd(vv, pk_off=pk_off, w_pk=w_pk, w_sr=w_sr, wb=wb, wb_col0=wb_col0,
                     sr_sb=sr_sb, war_t=war_t):
                vv.wait_ge(s_w, w_pk)
                vv.wait_ge(s_w, w_sr)
                if war_t is not None:
                    vv.wait_ge(s_t, war_t)
                p16 = pk[:, pk_off:pk_off + 512].bitcast(U16)
                vv.tensor_scalar(f8a[:], p16, 0x0707, 2, ALU.bitwise_and, ALU.logical_shift_left)
                vv.tensor_scalar(f8b[:], p16, 0x0808, 4, ALU.bitwise_and, ALU.logical_shift_left)
                vv.tensor_tensor(f8lo[:].bitcast(U16), f8a[:], f8b[:], ALU.bitwise_or)
                vv.tensor_scalar(f8a[:], p16, 0x7070, 2, ALU.bitwise_and, ALU.logical_shift_right)
                vv.tensor_scalar(f8b[:], p16, 0x8080, None, ALU.bitwise_and)
                vv.tensor_tensor(f8hi[:].bitcast(U16), f8a[:], f8b[:], ALU.bitwise_or)
                vv.scalar_tensor_tensor(wb[:, wb_col0:wb_col0 + 512], f8lo[:].bitcast(FP8),
                                        64.0, sr_sb[:], ALU.mult, ALU.mult)
                vv.scalar_tensor_tensor(wb[:, wb_col0 + 512:wb_col0 + 1024], f8hi[:].bitcast(FP8),
                                        64.0, sr_sb[:], ALU.mult, ALU.mult).then_inc(s_v, 1)
            p.v(simd); v.inc()

        s_sig_cur = [0]
        for grp in range(NGRP if stages >= 3 else 0):
            for isl in range(NIS):
                isg = grp * NIS + isl
                for mat in range(2):
                    srcp8 = pg8 if mat == 0 else pu8
                    scT = gscT if mat == 0 else uscT
                    wb = wgb if mat == 0 else wub
                    sr_sb = srg if mat == 0 else sru
                    for halfr in range(2):
                        for bcl in range(4):
                            bc = halfr * 4 + bcl
                            war_t = t.n if t.n > 0 else None
                            emit_dequant(srcp8, scT, bc, isg * 512, wb, bcl * 1024,
                                         (0 if mat == 0 else 512), sr_sb, war_t, None)
                        v_deq = v.n

                        def mmat(te, wb=wb, v_deq=v_deq, halfr=halfr):
                            te.wait_ge(s_v, v_deq)
                            li = None
                            for kcl in range(8):
                                kc = halfr * 8 + kcl
                                for ic in range(4):
                                    for th in range(2):
                                        li = te.matmul(pp[:, (ic * 2 + th) * 512:(ic * 2 + th + 1) * 512],
                                                       wb[:, kcl * 512 + ic * 128: kcl * 512 + ic * 128 + 128],
                                                       hsb[:, kc * TC + th * 512: kc * TC + th * 512 + 512],
                                                       start=(kc == 0), stop=(kc == NCH - 1))
                            li.then_inc(s_t, 1)
                        p.t(mmat); t.inc()
                    t_mat = t.n

                    if mat == 0:
                        def evac_g(vv, t_mat=t_mat):
                            vv.wait_ge(s_t, t_mat)
                            li = None
                            for ic in range(4):
                                for th in range(2):
                                    li = vv.tensor_copy(
                                        g_sb[:, ic * TC + th * 512: ic * TC + th * 512 + 512],
                                        pp[:, (ic * 2 + th) * 512:(ic * 2 + th + 1) * 512])
                            li.then_inc(s_v, 1)
                        p.v(evac_g); v.inc()
                        v_g = v.n

                        def gelu1(sc, t_mat=t_mat, v_g=v_g):
                            sc.wait_ge(s_t, t_mat)
                            li = None
                            for ic in range(4):
                                for th in range(2):
                                    li = sc.square(sq_sb[:, th * 512:th * 512 + 512],
                                                   pp[:, (ic * 2 + th) * 512:(ic * 2 + th + 1) * 512]) \
                                        if False else None
                            sc.wait_ge(s_v, v_g)
                            for ic in range(4):
                                li = sc.square(sig[:, ic * TC:(ic + 1) * TC], g_sb[:, ic * TC:(ic + 1) * TC])
                            li.then_inc(s_s, 1)
                        p.s(gelu1); s.inc()
                        s_sq2 = s.n

                        def gelu2(vv, s_sq2=s_sq2):
                            vv.wait_ge(s_s, s_sq2)
                            li = None
                            for ic in range(4):
                                vv.tensor_scalar(sig[:, ic * TC:(ic + 1) * TC],
                                                 sig[:, ic * TC:(ic + 1) * TC],
                                                 0.044715, 1.0, ALU.mult, ALU.add)
                                li = vv.tensor_tensor(sig[:, ic * TC:(ic + 1) * TC],
                                                      sig[:, ic * TC:(ic + 1) * TC],
                                                      g_sb[:, ic * TC:(ic + 1) * TC], ALU.mult)
                            li.then_inc(s_v, 1)
                        p.v(gelu2); v.inc()
                        v_z = v.n

                        def gelu3(sc, v_z=v_z):
                            sc.wait_ge(s_v, v_z)
                            li = None
                            for ic in range(4):
                                li = sc.activation(sig[:, ic * TC:(ic + 1) * TC],
                                                   sig[:, ic * TC:(ic + 1) * TC],
                                                   AF.Sigmoid, scale=GELU_S)
                            li.then_inc(s_s, 1)
                        p.s(gelu3); s.inc()
                        s_sig_cur[0] = s.n
                        p.t(lambda te, v_g=v_g: te.wait_ge(s_v, v_g))
                    else:
                        def evac_u(vv, isl=isl, t_mat=t_mat, s_sig=s_sig_cur[0]):
                            vv.wait_ge(s_t, t_mat)
                            vv.wait_ge(s_s, s_sig)
                            li = None
                            for ic in range(4):
                                iin = isl * 4 + ic
                                for th in range(2):
                                    sl = slice(ic * TC + th * 512, ic * TC + th * 512 + 512)
                                    vv.tensor_tensor(btmp[:, 0:512], sig[:, sl], g_sb[:, sl], ALU.mult)
                                    li = vv.tensor_tensor(
                                        inter[:, iin * TC + th * 512: iin * TC + th * 512 + 512],
                                        btmp[:, 0:512],
                                        pp[:, (ic * 2 + th) * 512:(ic * 2 + th + 1) * 512], ALU.mult)
                            li.then_inc(s_v, 1)
                        p.v(evac_u); v.inc()
                        v_u = v.n
                        p.t(lambda te, v_u=v_u: te.wait_ge(s_v, v_u))

            for hg in range(4):
                for halfr in range(2):
                    for bci in range(4):
                        bcl = halfr * 4 + bci
                        bc_glob = grp * 8 + bcl
                        war_t = t.n if t.n > 0 else None
                        emit_dequant(pd8, dscT, bc_glob, hg * 512, wdb, bci * 1024,
                                     1024, srg, war_t, None)
                    v_deqd = v.n

                    def dmms(te, v_deqd=v_deqd, halfr=halfr):
                        te.wait_ge(s_v, v_deqd)
                        li = None
                        for icll in range(8):
                            icl = halfr * 8 + icll
                            for hc in range(4):
                                for th in range(2):
                                    li = te.matmul(pp[:, (hc * 2 + th) * 512:(hc * 2 + th + 1) * 512],
                                                   wdb[:, icll * 512 + hc * 128: icll * 512 + hc * 128 + 128],
                                                   inter[:, icl * TC + th * 512: icl * TC + th * 512 + 512],
                                                   start=(icl == 0), stop=(icl == GRP - 1))
                        li.then_inc(s_t, 1)
                    p.t(dmms); t.inc()
                t_down = t.n

                def xacc(vv, hg=hg, t_down=t_down):
                    vv.wait_ge(s_t, t_down)
                    li = None
                    for hc in range(4):
                        h = hg * 4 + hc
                        for th in range(2):
                            li = vv.scalar_tensor_tensor(
                                xsb[:, h * TC + th * 512: h * TC + th * 512 + 512],
                                pp[:, (hc * 2 + th) * 512:(hc * 2 + th + 1) * 512],
                                1.0,
                                xsb[:, h * TC + th * 512: h * TC + th * 512 + 512],
                                ALU.mult, ALU.add)
                    li.then_inc(s_v, 1)
                p.v(xacc); v.inc()
                v_x = v.n
                p.t(lambda te, v_x=v_x: te.wait_ge(s_v, v_x))

        v_done = v.n

        stv = st.n + 16

        def store(sy, v_done=v_done, stv=stv):
            sy.wait_ge(s_v, v_done)
            xo_ap = bass.AP(xo.tensor, 0, [[TC, 128], [128 * TC, NCH], [1, TC]])
            sy.dma_start(xo_ap, xsb[:]).then_inc(s_st, 16)
            sy.wait_ge(s_st, stv)
        p.sy(store); st.inc(16)

        p.emit(blk)


def ln_tile_emit(p, sems, xt, ht, pp, C, tcw=512, nch=NCH, xcolstride=None, hcolstride=None,
                 xoff=0, hoff=0):
    v, t, s = sems["v"], sems["t"], sems["s"]
    xs = xcolstride
    hs = hcolstride

    def sq(sc):
        li = None
        for c in range(nch):
            li = sc.square(ht[:, hoff + c * hs: hoff + c * hs + tcw],
                           xt[:, xoff + c * xs: xoff + c * xs + tcw])
        li.then_inc(s.h, 1)
    p.s(sq); s.inc()
    s_sq = s.n

    b_s1 = pp[0:1, 4 * 512: 4 * 512 + tcw]
    b_s2 = pp[0:1, 6 * 512: 6 * 512 + tcw]

    def stats(te, s_sq=s_sq):
        te.wait_ge(s.h, s_sq)
        li = None
        for c in range(nch):
            li = te.matmul(b_s1, C["ones_c_f32"], xt[:, xoff + c * xs: xoff + c * xs + tcw],
                           start=(c == 0), stop=(c == nch - 1))
        for c in range(nch):
            li = te.matmul(b_s2, C["ones_c_bf"], ht[:, hoff + c * hs: hoff + c * hs + tcw],
                           start=(c == 0), stop=(c == nch - 1))
        li.then_inc(t.h, 1)
    p.t(stats); t.inc()
    t_stats = t.n

    mrow, rrow, vrow, srow = C["mrow"], C["rrow"], C["vrow"], C["srow"]

    def rows(vv, t_stats=t_stats):
        vv.wait_ge(t.h, t_stats)
        vv.tensor_scalar_mul(mrow[0:1, 0:tcw], b_s1, 1.0 / D)
        vv.tensor_tensor(vrow[0:1, 0:tcw], mrow[0:1, 0:tcw], mrow[0:1, 0:tcw], ALU.mult)
        vv.scalar_tensor_tensor(vrow[0:1, 0:tcw], b_s2, 1.0 / D, vrow[0:1, 0:tcw],
                                ALU.mult, ALU.subtract).then_inc(v.h, 1)
    p.v(rows); v.inc()
    v_rows = v.n

    def dosqrt(sc, v_rows=v_rows):
        sc.wait_ge(v.h, v_rows)
        sc.activation(srow[0:1, 0:tcw], vrow[0:1, 0:tcw], AF.Sqrt,
                      bias=C["eps"], scale=1.0).then_inc(s.h, 1)
    p.s(dosqrt); s.inc()
    s_sqrt = s.n

    def dorecip(vv, s_sqrt=s_sqrt):
        vv.wait_ge(s.h, s_sqrt)
        vv.reciprocal(rrow[0:1, 0:tcw], srow[0:1, 0:tcw]).then_inc(v.h, 1)
    p.v(dorecip); v.inc()
    v_recip = v.n

    b_mb = pp[:, 4 * 512: 4 * 512 + tcw]
    b_rb = pp[:, 6 * 512: 6 * 512 + tcw]

    def bcast(te, v_recip=v_recip):
        te.wait_ge(v.h, v_recip)
        te.matmul(b_mb, C["ones_r_f32"], mrow[0:1, 0:tcw], start=True, stop=True)
        te.matmul(b_rb, C["ones_r_f32"], rrow[0:1, 0:tcw], start=True, stop=True).then_inc(t.h, 1)
    p.t(bcast); t.inc()
    t_bcast = t.n

    def mk_h(vv, t_bcast=t_bcast):
        vv.wait_ge(t.h, t_bcast)
        li = None
        for c in range(nch):
            vv.tensor_tensor(C["ftmp"][:, 0:tcw], xt[:, xoff + c * xs: xoff + c * xs + tcw],
                             b_mb, ALU.subtract)
            li = vv.tensor_tensor(ht[:, hoff + c * hs: hoff + c * hs + tcw],
                                  C["ftmp"][:, 0:tcw], b_rb, ALU.mult)
        li.then_inc(v.h, 1)
    p.v(mk_h); v.inc()
    return v.n


def build_l1attn(stages=99):
    nc = bass.Bass("TRN2", target_bir_lowering=False)
    xT = nc.dram_tensor("xT", [D, TC], F32, kind="ExternalInput").ap()
    xfT = nc.dram_tensor("xfT", [D, 4096], F32, kind="ExternalInput").ap()
    wq = nc.dram_tensor("wq", [D, D], F32, kind="ExternalInput").ap()
    wk = nc.dram_tensor("wk", [D, D], F32, kind="ExternalInput").ap()
    wv = nc.dram_tensor("wv", [D, D], F32, kind="ExternalInput").ap()
    wo = nc.dram_tensor("wo", [D, D], F32, kind="ExternalInput").ap()
    band_in = nc.dram_tensor("band", [128, 4992], BF16, kind="ExternalInput").ap()
    xo = nc.dram_tensor("xo", [D, TC], F32, kind="ExternalOutput").ap()
    with ExitStack() as octx:
        S = _mk_sems_fused(octx, nc)
        blk = octx.enter_context(nc.Block())
        emit_l1attn(nc, blk, S, xT, xfT, wq, wk, wv, wo, band_in, xo)
    return nc


def emit_l1attn(nc, blk, S, xT, xfT, wq, wk, wv, wo, band_in, xo,
                xful_blocks=False, stages=99):
    _next_phase()
    """xful_blocks: xfT is an AllGather output [4*2048, 1024] (rank-major blocks)
    instead of [2048, 4096]."""
    k1d = nc.dram_tensor("k1d_p2", [D, 4096], BF16, kind="Internal").ap()
    v1d = nc.dram_tensor("v1d_p2", [16 * 4096 * 130], BF16, kind="Internal").ap()

    NH = 16

    with ExitStack() as ctx:
        xt = ctx.enter_context(nc.sbuf_tensor(_pn("xt"), [128, NCH * 512], F32))
        harena = ctx.enter_context(nc.sbuf_tensor(_pn("harena"), [128, NCH * 2048], BF16))
        q1 = ctx.enter_context(nc.sbuf_tensor(_pn("q1"), [128, NH * TC], BF16))
        wsl = ctx.enter_context(nc.sbuf_tensor(_pn("wsl"), [128, NCH * 512], BF16))
        wot = ctx.enter_context(nc.sbuf_tensor(_pn("wot"), [128, 2 * NCH * 128], BF16))
        kslab = ctx.enter_context(nc.sbuf_tensor(_pn("kslab"), [128, 4096], BF16))
        vslab = ctx.enter_context(nc.sbuf_tensor(_pn("vslab"), [128, 32 * 130], BF16))
        band = ctx.enter_context(nc.sbuf_tensor(_pn("band_sb"), [128, 4992], BF16))
        Tbuf = ctx.enter_context(nc.sbuf_tensor(_pn("Tbuf"), [128, 512], F32))
        Pbuf = ctx.enter_context(nc.sbuf_tensor(_pn("Pbuf"), [128, 2 * 512], BF16))

        lrow = ctx.enter_context(nc.sbuf_tensor(_pn("lrow"), [1, 512], F32))
        lb = ctx.enter_context(nc.sbuf_tensor(_pn("lb"), [64, 512], F32))
        m64 = ctx.enter_context(nc.sbuf_tensor(_pn("m64"), [128, 1], F32))
        evb = ctx.enter_context(nc.sbuf_tensor(_pn("evb"), [128, 512], BF16))
        xnew = ctx.enter_context(nc.sbuf_tensor(_pn("xnew"), [128, TC], F32))
        C = alloc_common(ctx, nc, roww=512, ftmpw=512)
        pp = ctx.enter_context(nc.psum_tensor(_pn("pp"), [128, 8 * 512], F32))

        ctx_sb = harena
        p = P()
        ld, w, v, t, s, st, sd = (S[k] for k in ("ld", "w", "v", "t", "s", "st", "d"))
        s_ld, s_w, s_v, s_t, s_s, s_st, s_d = (x.h for x in (ld, w, v, t, s, st, sd))
        sems = {"v": v, "t": t, "s": s}

        init_consts_emit(p, sems, C)

        def ic2(vv):
            vv.memset(m64[:], -64.0).then_inc(s_v, 1)
        p.v(ic2); v.inc()
        v_c0 = v.n
        p.t(lambda te, v_c0=v_c0: te.wait_ge(s_v, v_c0))
        p.s(lambda sc, v_c0=v_c0: sc.wait_ge(s_v, v_c0))

        def ld_band(sy):
            sy.dma_start(band[:], band_in).then_inc(s_ld, 16)
        p.sy(ld_band); ld.inc(16)

        for mt in range(2):
            war_v = v.n
            war_t = t.n
            def xl2(sy, mt=mt, war_v=war_v, war_t=war_t):
                if mt > 0:
                    sy.wait_ge(s_v, war_v)
                    sy.wait_ge(s_t, war_t)
                src = bass.AP(xT.tensor, mt * 512, [[TC, 128], [128 * TC, NCH], [1, 512]])
                sy.dma_start(xt[:], src).then_inc(s_ld, 16)
            p.sy(xl2); ld.inc(16)
            ld_t = ld.n
            p.s(lambda sc, ld_t=ld_t: sc.wait_ge(s_ld, ld_t))
            p.t(lambda te, ld_t=ld_t: te.wait_ge(s_ld, ld_t))
            p.v(lambda vv, ld_t=ld_t: vv.wait_ge(s_ld, ld_t))
            ln_tile_emit(p, sems, xt, harena, pp, C, tcw=512, xcolstride=512,
                         hcolstride=2048, xoff=0, hoff=mt * 512)
        v_hmy = v.n
        for h in range(NH):
            buf = h % 2
            war = t.n if h >= 2 else None
            def wql(g, h=h, buf=buf, war=war):
                if war is not None:
                    g.wait_ge(s_t, war)
                src = bass.AP(wq.tensor, h * 128, [[D, 128], [128 * D, NCH], [1, 128]])
                g.dma_start(wot[:, buf * 2048:(buf + 1) * 2048], src).then_inc(s_w, 16)
            p.g(wql); w.inc(16)
            w_t = w.n

            def qmm(te, h=h, buf=buf, w_t=w_t, v_hmy=v_hmy):
                te.wait_ge(s_w, w_t)
                te.wait_ge(s_v, v_hmy)
                li = None
                for th in range(2):
                    for kc in range(NCH):
                        li = te.matmul(pp[:, th * 512:(th + 1) * 512],
                                       wot[:, buf * 2048 + kc * 128: buf * 2048 + kc * 128 + 128],
                                       harena[:, kc * 2048 + th * 512: kc * 2048 + th * 512 + 512],
                                       start=(kc == 0), stop=(kc == NCH - 1))
                li.then_inc(s_t, 1)
            p.t(qmm); t.inc()
            t_q = t.n

            def qev(vv, h=h, t_q=t_q):
                vv.wait_ge(s_t, t_q)
                li = None
                for th in range(2):
                    li = vv.tensor_copy(q1[:, h * TC + th * 512: h * TC + th * 512 + 512],
                                        pp[:, th * 512:(th + 1) * 512])
                li.then_inc(s_v, 1)
            p.v(qev); v.inc()
            v_q = v.n
            p.t(lambda te, v_q=v_q: te.wait_ge(s_v, v_q))

        def ones_ms(vv):
            vv.memset(harena[:, 16384:16384 + 4160], 1.0).then_inc(s_v, 1)
        p.v(ones_ms); v.inc()
        v_ones = v.n

        def ones_init(sy, v_ones=v_ones):
            sy.wait_ge(s_v, v_ones)
            for h in range(NH):
                dst = bass.AP(v1d.tensor, h * 4096 * 130,
                              [[130, 128], [130 * 128, 32], [1, 130]])
                src = bass.AP(harena, 16384, [[32768, 128], [0, 32], [1, 130]])
                sy.dma_start(dst, src).then_inc(s_d, 16)
        p.sy(ones_init); sd.inc(16 * NH)
        sd_ones = sd.n
        p.s(lambda sc, sd_ones=sd_ones: sc.wait_ge(s_d, sd_ones))
        p.v(lambda vv, sd_ones=sd_ones: vv.wait_ge(s_d, sd_ones))

        for half in range(2):
            for kvt in range(4):
                war_v = v.n
                war_t = t.n
                def xfl(sy, half=half, kvt=kvt, war_v=war_v, war_t=war_t):
                    sy.wait_ge(s_v, war_v)
                    sy.wait_ge(s_t, war_t)
                    kt = half * 4 + kvt
                    if xful_blocks:
                        blkid, j0 = kt // 2, (kt % 2) * 512
                        src = bass.AP(xfT.tensor, blkid * D * TC + j0,
                                      [[TC, 128], [128 * TC, NCH], [1, 512]])
                    else:
                        src = bass.AP(xfT.tensor, kt * 512,
                                      [[4096, 128], [128 * 4096, NCH], [1, 512]])
                    sy.dma_start(xt[:], src).then_inc(s_ld, 16)
                p.sy(xfl); ld.inc(16)
                ld_t = ld.n
                p.s(lambda sc, ld_t=ld_t: sc.wait_ge(s_ld, ld_t))
                p.t(lambda te, ld_t=ld_t: te.wait_ge(s_ld, ld_t))
                p.v(lambda vv, ld_t=ld_t: vv.wait_ge(s_ld, ld_t))
                ln_tile_emit(p, sems, xt, harena, pp, C, tcw=512, xcolstride=512,
                             hcolstride=2048, xoff=0, hoff=kvt * 512)
            v_h = v.n
            for mat in range(2):
                wsrc = wk if mat == 0 else wv
                for ds in range(4):
                    war = t.n
                    def wsl_l(g, ds=ds, wsrc=wsrc, war=war):
                        g.wait_ge(s_t, war)
                        src = bass.AP(wsrc.tensor, ds * 512, [[D, 128], [128 * D, NCH], [1, 512]])
                        g.dma_start(wsl[:], src).then_inc(s_w, 16)
                    p.g(wsl_l); w.inc(16)
                    w_s = w.n
                    p.t(lambda te, w_s=w_s, v_h=v_h: (te.wait_ge(s_w, w_s), te.wait_ge(s_v, v_h)))
                    for kvt in range(4):
                        for blk4 in range(4):
                            def kvmm(te, mat=mat, ds=ds, kvt=kvt, blk4=blk4):
                                li = None
                                for kc in range(NCH):
                                    if mat == 0:
                                        li = te.matmul(pp[:, 0:512],
                                                       wsl[:, kc * 512 + blk4 * 128: kc * 512 + blk4 * 128 + 128],
                                                       harena[:, kc * 2048 + kvt * 512: kc * 2048 + kvt * 512 + 512],
                                                       start=(kc == 0), stop=(kc == NCH - 1))
                                    else:
                                        li = te.matmul(pp[:, 0:512],
                                                       harena[:, kc * 2048 + kvt * 512 + blk4 * 128: kc * 2048 + kvt * 512 + blk4 * 128 + 128],
                                                       wsl[:, kc * 512:(kc + 1) * 512],
                                                       start=(kc == 0), stop=(kc == NCH - 1))
                                li.then_inc(s_t, 1)
                            p.t(kvmm); t.inc()
                            t_kv = t.n

                            def kvev(vv, t_kv=t_kv):
                                vv.wait_ge(s_t, t_kv)
                                vv.tensor_copy(evb[:], pp[:, 0:512]).then_inc(s_v, 1)
                            p.v(kvev); v.inc()
                            v_ev = v.n
                            p.t(lambda te, v_ev=v_ev: te.wait_ge(s_v, v_ev))

                            def kvst(sy, mat=mat, ds=ds, kvt=kvt, blk4=blk4, half=half,
                                     v_ev=v_ev, sd_ones=sd_ones):
                                sy.wait_ge(s_v, v_ev)
                                if mat == 0:
                                    d0 = ds * 512 + blk4 * 128
                                    dst = bass.AP(k1d.tensor, d0 * 4096 + half * 2048 + kvt * 512,
                                                  [[4096, 128], [1, 512]])
                                    sy.dma_start(dst, evb[:]).then_inc(s_d, 16)
                                else:
                                    sy.wait_ge(s_d, sd_ones)
                                    t0 = half * 2048 + kvt * 512 + blk4 * 128
                                    base = (ds * 4) * 4096 * 130 + t0 * 130
                                    for hf in range(2):
                                        dst = bass.AP(v1d.tensor, base + hf * 65,
                                                      [[130, 128], [4096 * 130, 4], [1, 64]])
                                        srcap = bass.AP(evb, hf * 64,
                                                        [[512, 128], [128, 4], [1, 64]])
                                        sy.dma_start(dst, srcap).then_inc(s_d, 16)
                            p.sy(kvst); sd.inc(16 if mat == 0 else 32)
                            sd_cur = sd.n
                            p.v(lambda vv, sd_cur=sd_cur: vv.wait_ge(s_d, sd_cur))

        sd_all = sd.n
        p.sy(lambda sy, sd_all=sd_all: sy.wait_ge(s_d, sd_all))
        ld_b = 16
        for qb in range(2):
            for h in range(NH):
                war = t.n
                def ksl(sy, h=h, war=war):
                    sy.wait_ge(s_t, war)
                    src = bass.AP(k1d.tensor, h * 128 * 4096, [[4096, 128], [1, 4096]])
                    sy.dma_start(kslab[:], src).then_inc(s_ld, 16)
                p.sy(ksl); ld.inc(16)

                def vsl(sy, h=h, war=war):
                    sy.wait_ge(s_t, war)
                    src = bass.AP(v1d.tensor, h * 4096 * 130,
                                  [[130, 128], [130 * 128, 32], [1, 130]])
                    sy.dma_start(vslab[:], src).then_inc(s_ld, 16)
                p.sy(vsl); ld.inc(16)
                ld_slab = ld.n
                p.t(lambda te, ld_slab=ld_slab: te.wait_ge(s_ld, ld_slab))
                p.v(lambda vv, ld_slab=ld_slab: vv.wait_ge(s_ld, ld_slab))

                for tbl in range(32):
                    sb = tbl % 2
                    pb = tbl % 2
                    def qkmm(te, h=h, qb=qb, tbl=tbl, sb=sb):
                        te.matmul(pp[:, sb * 512:(sb + 1) * 512],
                                  kslab[:, tbl * 128:(tbl + 1) * 128],
                                  q1[:, h * TC + qb * 512: h * TC + qb * 512 + 512],
                                  start=True, stop=True).then_inc(s_t, 1)
                    p.t(qkmm); t.inc()
                    t_qk = t.n

                    def pmask(vv, qb=qb, tbl=tbl, sb=sb, t_qk=t_qk):
                        vv.wait_ge(s_t, t_qk)
                        off = 3968 + qb * 512 - tbl * 128
                        vv.tensor_tensor(Tbuf[:], pp[:, sb * 512:(sb + 1) * 512],
                                         band[:, off:off + 512], ALU.mult).then_inc(s_v, 1)
                    p.v(pmask); v.inc()
                    v_m = v.n

                    def pexp(sc, pb=pb, v_m=v_m):
                        sc.wait_ge(s_v, v_m)
                        sc.activation(Pbuf[:, pb * 512:(pb + 1) * 512], Tbuf[:],
                                      AF.Exp, bias=m64[:], scale=1.0 / 64).then_inc(s_s, 1)
                    p.s(pexp); s.inc()
                    s_e = s.n

                    def pv(te, tbl=tbl, pb=pb, s_e=s_e):
                        te.wait_ge(s_s, s_e)
                        te.matmul(pp[:, 2 * 512: 2 * 512 + 512][0:65, :],
                                  vslab[:, tbl * 130: tbl * 130 + 65],
                                  Pbuf[:, pb * 512:(pb + 1) * 512],
                                  start=(tbl == 0), stop=(tbl == 31))
                        te.matmul(pp[:, 3 * 512: 3 * 512 + 512][0:65, :],
                                  vslab[:, tbl * 130 + 65: tbl * 130 + 130],
                                  Pbuf[:, pb * 512:(pb + 1) * 512],
                                  start=(tbl == 0), stop=(tbl == 31)).then_inc(s_t, 1)
                    p.t(pv); t.inc()

                t_pv = t.n

                def lsetup(vv, t_pv=t_pv):
                    vv.wait_ge(s_t, t_pv)
                    vv.tensor_copy(lrow[0:1, :], pp[64:65, 2 * 512: 2 * 512 + 512])
                    vv.reciprocal(lrow[0:1, :], lrow[0:1, :]).then_inc(s_v, 1)
                p.v(lsetup); v.inc()
                v_l = v.n

                def lbc(te, v_l=v_l):
                    te.wait_ge(s_v, v_l)
                    te.matmul(pp[0:64, 4 * 512: 4 * 512 + 512], C["ones_r_f32"][0:1, 0:64],
                              lrow[0:1, :], start=True, stop=True).then_inc(s_t, 1)
                p.t(lbc); t.inc()
                t_lb = t.n

                def cdiv(vv, h=h, qb=qb, t_lb=t_lb):
                    vv.wait_ge(s_t, t_lb)
                    vv.tensor_copy(lb[:], pp[0:64, 4 * 512: 4 * 512 + 512])
                    vv.scalar_tensor_tensor(
                        ctx_sb[0:64, h * TC + qb * 512: h * TC + qb * 512 + 512],
                        pp[0:64, 2 * 512: 2 * 512 + 512], 1.0, lb[:], ALU.mult, ALU.mult)
                    vv.scalar_tensor_tensor(
                        ctx_sb[64:128, h * TC + qb * 512: h * TC + qb * 512 + 512],
                        pp[0:64, 3 * 512: 3 * 512 + 512], 1.0, lb[:], ALU.mult, ALU.mult
                    ).then_inc(s_v, 1)
                p.v(cdiv); v.inc()
                v_c = v.n
                p.t(lambda te, v_c=v_c: te.wait_ge(s_v, v_c))

        for oc in range(NCH):
            buf = oc % 2
            war = t.n if oc >= 2 else None
            def wol(g, oc=oc, buf=buf, war=war):
                if war is not None:
                    g.wait_ge(s_t, war)
                src = bass.AP(wo.tensor, oc * 128, [[D, 128], [128 * D, NCH], [1, 128]])
                g.dma_start(wot[:, buf * 2048:(buf + 1) * 2048], src).then_inc(s_w, 16)
            p.g(wol); w.inc(16)
            w_t = w.n

            st_prev = st.n
            def xl4(sy, oc=oc, st_prev=st_prev):
                if oc > 0:
                    sy.wait_ge(s_st, st_prev)
                src = bass.AP(xT.tensor, oc * 128 * TC, [[TC, 128], [1, TC]])
                sy.dma_start(xnew[:], src).then_inc(s_ld, 16)
            p.sy(xl4); ld.inc(16)
            ld_xc = ld.n

            def omm(te, oc=oc, buf=buf, w_t=w_t):
                te.wait_ge(s_w, w_t)
                li = None
                for th in range(2):
                    for ic in range(NCH):
                        li = te.matmul(pp[:, (5 + th) * 512:(6 + th) * 512],
                                       wot[:, buf * 2048 + ic * 128: buf * 2048 + ic * 128 + 128],
                                       ctx_sb[:, ic * TC + th * 512: ic * TC + th * 512 + 512],
                                       start=(ic == 0), stop=(ic == NCH - 1))
                li.then_inc(s_t, 1)
            p.t(omm); t.inc()
            t_o = t.n

            def oev(vv, oc=oc, t_o=t_o, ld_xc=ld_xc):
                vv.wait_ge(s_t, t_o)
                vv.wait_ge(s_ld, ld_xc)
                li = None
                for th in range(2):
                    li = vv.tensor_tensor(xnew[:, th * 512:(th + 1) * 512],
                                          xnew[:, th * 512:(th + 1) * 512],
                                          pp[:, (5 + th) * 512:(6 + th) * 512], ALU.add)
                li.then_inc(s_v, 1)
            p.v(oev); v.inc()
            v_o = v.n
            p.t(lambda te, v_o=v_o: te.wait_ge(s_v, v_o))

            def ost(sy, oc=oc, v_o=v_o):
                sy.wait_ge(s_v, v_o)
                dst = bass.AP(xo.tensor, oc * 128 * TC, [[TC, 128], [1, TC]])
                sy.dma_start(dst, xnew[:]).then_inc(s_st, 16)
            p.sy(ost); st.inc(16)

        st_all = st.n
        p.sy(lambda sy, st_all=st_all: sy.wait_ge(s_st, st_all))

        p.emit(blk)


def build_l0attn(stages=99):
    nc = bass.Bass("TRN2", target_bir_lowering=False)
    xT = nc.dram_tensor("xT", [D, TC], F32, kind="ExternalInput").ap()
    xfT = nc.dram_tensor("xfT", [D, 4096], F32, kind="ExternalInput").ap()
    wq = nc.dram_tensor("wq", [D, D], F32, kind="ExternalInput").ap()
    wk = nc.dram_tensor("wk", [D, 512], F32, kind="ExternalInput").ap()
    wv = nc.dram_tensor("wv", [D, 512], F32, kind="ExternalInput").ap()
    wo = nc.dram_tensor("wo", [D, D], F32, kind="ExternalInput").ap()
    bo_in = nc.dram_tensor("bo", [128, 16], F32, kind="ExternalInput").ap()
    qbits_in = nc.dram_tensor("qbits", [12, 32 * TC], BF16, kind="ExternalInput").ap()
    cosq_in = nc.dram_tensor("cosq", [32, TC], BF16, kind="ExternalInput").ap()
    sinq_in = nc.dram_tensor("sinq", [32, TC], BF16, kind="ExternalInput").ap()
    cosk_in = nc.dram_tensor("cosk", [32, 4096], BF16, kind="ExternalInput").ap()
    sink_in = nc.dram_tensor("sink", [32, 4096], BF16, kind="ExternalInput").ap()
    tbias_in = nc.dram_tensor("tbias", [128, 1024], F32, kind="ExternalInput").ap()
    band_in = nc.dram_tensor("band", [128, 4992], BF16, kind="ExternalInput").ap()
    xo = nc.dram_tensor("xo", [D, TC], F32, kind="ExternalOutput").ap()
    with ExitStack() as octx:
        S = _mk_sems_fused(octx, nc)
        blk = octx.enter_context(nc.Block())
        emit_l0attn(nc, blk, S, xT, xfT, wq, wk, wv, wo, bo_in, qbits_in, cosq_in,
                    sinq_in, cosk_in, sink_in, tbias_in, band_in, xo)
    return nc


def emit_l0attn(nc, blk, S, xT, xfT, wq, wk, wv, wo, bo_in, qbits_in, cosq_in,
                sinq_in, cosk_in, sink_in, tbias_in, band_in, xo, stages=99):
    _next_phase()
    NH = 32
    TT = 256

    with ExitStack() as ctx:
        xt = ctx.enter_context(nc.sbuf_tensor(_pn("xt"), [128, NCH * TT], F32))
        ctxa = ctx.enter_context(nc.sbuf_tensor(_pn("ctxa"), [128, 16384], BF16))
        htile = ctx.enter_context(nc.sbuf_tensor(_pn("htile"), [128, NCH * TT], BF16))
        q1 = ctx.enter_context(nc.sbuf_tensor(_pn("q1"), [76, 16 * TC], BF16))
        k1h = ctx.enter_context(nc.sbuf_tensor(_pn("k1h"), [76, 4 * 4096], BF16))
        v1h = ctx.enter_context(nc.sbuf_tensor(_pn("v1h"), [128, 32 * 260], BF16))
        wkv = ctx.enter_context(nc.sbuf_tensor(_pn("wkv"), [128, NCH * 512], BF16))
        wot = ctx.enter_context(nc.sbuf_tensor(_pn("wot"), [128, 2 * 2048], BF16))
        Tbuf = ctx.enter_context(nc.sbuf_tensor(_pn("Tbuf"), [128, 512], F32))
        Pbuf = ctx.enter_context(nc.sbuf_tensor(_pn("Pbuf"), [128, 512], BF16))
        P2 = ctx.enter_context(nc.sbuf_tensor(_pn("P2"), [128, 512], BF16))
        band = ctx.enter_context(nc.sbuf_tensor(_pn("band_sb"), [128, 4992], BF16))
        tbias = ctx.enter_context(nc.sbuf_tensor(_pn("tbias_sb"), [128, 1024], F32))
        bo_sb = ctx.enter_context(nc.sbuf_tensor(_pn("bo_sb"), [128, 16], F32))
        m30 = ctx.enter_context(nc.sbuf_tensor(_pn("m30"), [128, 1], F32))
        lrow = ctx.enter_context(nc.sbuf_tensor(_pn("lrow"), [1, 512], F32))
        lb = ctx.enter_context(nc.sbuf_tensor(_pn("lb"), [64, 512], F32))
        xnew = ctx.enter_context(nc.sbuf_tensor(_pn("xnew"), [128, TC], F32))
        C = alloc_common(ctx, nc, roww=512, ftmpw=512)
        pp = ctx.enter_context(nc.psum_tensor(_pn("pp"), [128, 8 * 512], F32))

        p = P()
        ld, w, v, t, s, st = (S[k] for k in ("ld", "w", "v", "t", "s", "st"))
        cs = S["cs"]
        s_ld, s_w, s_v, s_t, s_s, s_st = (x.h for x in (ld, w, v, t, s, st))
        s_cs = cs.h
        sems = {"v": v, "t": t, "s": s}

        init_consts_emit(p, sems, C)

        def ic2(vv):
            vv.memset(m30[:], -30.0).then_inc(s_v, 1)
        p.v(ic2); v.inc()
        v_c0 = v.n
        p.t(lambda te, v_c0=v_c0: te.wait_ge(s_v, v_c0))
        p.s(lambda sc, v_c0=v_c0: sc.wait_ge(s_v, v_c0))

        def lconsts(sy):
            sy.dma_start(band[:], band_in).then_inc(s_ld, 16)
            sy.dma_start(tbias[:], tbias_in).then_inc(s_ld, 16)
            sy.dma_start(bo_sb[:], bo_in).then_inc(s_ld, 16)
        p.sy(lconsts); ld.inc(16 * 3)
        ld_c = ld.n
        p.t(lambda te, ld_c=ld_c: te.wait_ge(s_ld, ld_c))
        p.v(lambda vv, ld_c=ld_c: vv.wait_ge(s_ld, ld_c))
        p.s(lambda sc, ld_c=ld_c: sc.wait_ge(s_ld, ld_c))

        def rope_batch(colbase, buf, cos0, cos32, sin0, sin32, width):
            A = buf[0:32, colbase:colbase + width]
            B = buf[32:64, colbase:colbase + width]
            xtv = xt[:].bitcast(BF16)
            S1 = xtv[0:32, 0:width]
            S2 = xtv[32:64, 0:width]
            S3 = xtv[32:64, 4096:4096 + width]
            S4 = xtv[0:32, 4096:4096 + width]

            def rp(vv, A=A, B=B, S1=S1, S2=S2, S3=S3, S4=S4,
                   cos0=cos0, cos32=cos32, sin0=sin0, sin32=sin32):
                vv.tensor_tensor(S1, A, cos0, ALU.mult)
                vv.tensor_tensor(S2, A, sin0, ALU.mult)
                vv.tensor_tensor(S3, B, cos32, ALU.mult)
                vv.scalar_tensor_tensor(S4, B, -1.0, sin32, ALU.mult, ALU.mult)
                vv.tensor_tensor(A, S1, S4, ALU.add)
                vv.tensor_tensor(B, S3, S2, ALU.add).then_inc(s_v, 1)
            p.v(rp); v.inc()

        for hh in range(2):
            for mt in range(4):
                war_v = v.n
                war_t = t.n
                def xl(sy, mt=mt, war_v=war_v, war_t=war_t):
                    sy.wait_ge(s_v, war_v)
                    sy.wait_ge(s_t, war_t)
                    src = bass.AP(xT.tensor, mt * TT, [[TC, 128], [128 * TC, NCH], [1, TT]])
                    sy.dma_start(xt[:], src).then_inc(s_ld, 16)
                p.sy(xl); ld.inc(16)
                ld_t = ld.n
                p.s(lambda sc, ld_t=ld_t: sc.wait_ge(s_ld, ld_t))
                p.t(lambda te, ld_t=ld_t: te.wait_ge(s_ld, ld_t))
                p.v(lambda vv, ld_t=ld_t: vv.wait_ge(s_ld, ld_t))
                ln_tile_emit(p, sems, xt, htile, pp, C, tcw=TT, xcolstride=TT,
                             hcolstride=TT, xoff=0, hoff=0)
                v_h = v.n
                for hl in range(16):
                    h = hh * 16 + hl
                    buf = hl % 2
                    war = t.n if t.n >= 1 else None
                    def wql(g, h=h, buf=buf, war=war):
                        if war is not None:
                            g.wait_ge(s_t, war)
                        src = bass.AP(wq.tensor, h * 64, [[D, 128], [128 * D, NCH], [1, 64]])
                        g.dma_start(wot[:, buf * 2048: buf * 2048 + 1024], src).then_inc(s_w, 16)
                    p.g(wql); w.inc(16)
                    w_t = w.n

                    def qmm(te, hl=hl, buf=buf, w_t=w_t, v_h=v_h):
                        te.wait_ge(s_w, w_t)
                        te.wait_ge(s_v, v_h)
                        li = None
                        for kc in range(NCH):
                            li = te.matmul(pp[0:64, 0:TT],
                                           wot[:, buf * 2048 + kc * 64: buf * 2048 + kc * 64 + 64],
                                           htile[:, kc * TT: kc * TT + TT],
                                           start=(kc == 0), stop=(kc == NCH - 1))
                        li.then_inc(s_t, 1)
                    p.t(qmm); t.inc()
                    t_q = t.n

                    def qev(vv, hl=hl, mt=mt, t_q=t_q):
                        vv.wait_ge(s_t, t_q)
                        vv.tensor_copy(q1[0:64, hl * TC + mt * TT: hl * TC + mt * TT + TT],
                                       pp[0:64, 0:TT]).then_inc(s_v, 1)
                    p.v(qev); v.inc()
                    v_q = v.n
                    p.t(lambda te, v_q=v_q: te.wait_ge(s_v, v_q))
            war_qr = t.n
            def csql(sy, war_qr=war_qr):
                sy.wait_ge(s_t, war_qr)
                srcc = bass.AP(cosq_in.tensor, 0, [[0, 4], [TC, 32], [1, TC]])
                srcs = bass.AP(sinq_in.tensor, 0, [[0, 4], [TC, 32], [1, TC]])
                sy.dma_start(htile[:, 0:TC], srcc).then_inc(s_cs, 16)
                sy.dma_start(htile[:, TC:2 * TC], srcs).then_inc(s_cs, 16)
            p.sy(csql); cs.inc(32)
            ld_cs = cs.n
            p.v(lambda vv, ld_cs=ld_cs: vv.wait_ge(s_cs, ld_cs))
            RL = NCH * TT
            for b4 in range(4):
                cos0 = bass.AP(htile, 0 * RL, [[RL, 32], [0, 4], [1, TC]])
                cos32 = bass.AP(htile, 32 * RL, [[RL, 32], [0, 4], [1, TC]])
                sin0 = bass.AP(htile, 0 * RL + TC, [[RL, 32], [0, 4], [1, TC]])
                sin32 = bass.AP(htile, 32 * RL + TC, [[RL, 32], [0, 4], [1, TC]])
                rope_batch(b4 * 4 * TC, q1, cos0, cos32, sin0, sin32, 4 * TC)
            def qbl(sy, hh=hh):
                sy.dma_start(q1[64:76, :], qbits_in[:, hh * 16 * TC:(hh + 1) * 16 * TC]).then_inc(s_ld, 16)
            p.sy(qbl); ld.inc(16)
            ld_qb = ld.n
            p.t(lambda te, ld_qb=ld_qb: te.wait_ge(s_ld, ld_qb))

            war = t.n
            def wkvl(g, hh=hh, war=war):
                g.wait_ge(s_t, war)
                srck = bass.AP(wk.tensor, hh * 256, [[512, 128], [128 * 512, NCH], [1, 256]])
                dstk = bass.AP(wkv, 0, [[NCH * 512, 128], [512, NCH], [1, 256]])
                g.dma_start(dstk, srck).then_inc(s_w, 16)
                srcv = bass.AP(wv.tensor, hh * 256, [[512, 128], [128 * 512, NCH], [1, 256]])
                dstv = bass.AP(wkv, 256, [[NCH * 512, 128], [512, NCH], [1, 256]])
                g.dma_start(dstv, srcv).then_inc(s_w, 16)
            p.g(wkvl); w.inc(32)
            w_kv = w.n
            p.t(lambda te, w_kv=w_kv: te.wait_ge(s_w, w_kv))

            war_vv = t.n
            def vset(vv, war_vv=war_vv):
                vv.wait_ge(s_t, war_vv)
                vv.memset(k1h[64:76, :], 1.0)
                vv.memset(v1h[:], 1.0).then_inc(s_v, 1)
            p.v(vset); v.inc()
            v_set = v.n
            p.t(lambda te, v_set=v_set: te.wait_ge(s_v, v_set))

            for kt in range(16):
                war_v = v.n
                war_t = t.n
                def xfl(sy, kt=kt, war_v=war_v, war_t=war_t):
                    sy.wait_ge(s_v, war_v)
                    sy.wait_ge(s_t, war_t)
                    src = bass.AP(xfT.tensor, kt * TT, [[4096, 128], [128 * 4096, NCH], [1, TT]])
                    sy.dma_start(xt[:], src).then_inc(s_ld, 16)
                p.sy(xfl); ld.inc(16)
                ld_t = ld.n
                p.s(lambda sc, ld_t=ld_t: sc.wait_ge(s_ld, ld_t))
                p.t(lambda te, ld_t=ld_t: te.wait_ge(s_ld, ld_t))
                p.v(lambda vv, ld_t=ld_t: vv.wait_ge(s_ld, ld_t))
                ln_tile_emit(p, sems, xt, htile, pp, C, tcw=TT, xcolstride=TT,
                             hcolstride=TT, xoff=0, hoff=0)
                v_h = v.n
                def kmm(te, v_h=v_h):
                    te.wait_ge(s_v, v_h)
                    li = None
                    for ch in range(2):
                        for kc in range(NCH):
                            li = te.matmul(pp[:, ch * 512: ch * 512 + TT],
                                           wkv[:, kc * 512 + ch * 128: kc * 512 + ch * 128 + 128],
                                           htile[:, kc * TT: kc * TT + TT],
                                           start=(kc == 0), stop=(kc == NCH - 1))
                    for tb in range(2):
                        for kc in range(NCH):
                            li = te.matmul(pp[:, (2 + tb) * 512: (2 + tb) * 512 + 256],
                                           htile[:, kc * TT + tb * 128: kc * TT + tb * 128 + 128],
                                           wkv[:, kc * 512 + 256: kc * 512 + 512],
                                           start=(kc == 0), stop=(kc == NCH - 1))
                    li.then_inc(s_t, 1)
                p.t(kmm); t.inc()
                t_kv = t.n

                def kvev(vv, kt=kt, t_kv=t_kv):
                    vv.wait_ge(s_t, t_kv)
                    li = None
                    for ch in range(2):
                        for gh in range(2):
                            g_loc = ch * 2 + gh
                            li = vv.tensor_copy(
                                k1h[0:64, g_loc * 4096 + kt * TT: g_loc * 4096 + kt * TT + TT],
                                pp[gh * 64:(gh + 1) * 64, ch * 512: ch * 512 + TT])
                    for tb in range(2):
                        tt_idx = kt * 2 + tb
                        for g_loc in range(4):
                            li = vv.tensor_copy(
                                v1h[:, tt_idx * 260 + g_loc * 65: tt_idx * 260 + g_loc * 65 + 64],
                                pp[:, (2 + tb) * 512 + g_loc * 64: (2 + tb) * 512 + g_loc * 64 + 64])
                    li.then_inc(s_v, 1)
                p.v(kvev); v.inc()
                v_ev = v.n
                p.t(lambda te, v_ev=v_ev: te.wait_ge(s_v, v_ev))
            RL = NCH * TT
            for th2 in range(2):
                war_kr = t.n
                war_kv = v.n
                def cskl(sy, th2=th2, war_kr=war_kr, war_kv=war_kv):
                    sy.wait_ge(s_t, war_kr)
                    sy.wait_ge(s_v, war_kv)
                    srcc = bass.AP(cosk_in.tensor, th2 * 2048, [[0, 4], [4096, 32], [1, 2048]])
                    srcs = bass.AP(sink_in.tensor, th2 * 2048, [[0, 4], [4096, 32], [1, 2048]])
                    sy.dma_start(htile[:, 0:2048], srcc).then_inc(s_cs, 16)
                    sy.dma_start(htile[:, 2048:4096], srcs).then_inc(s_cs, 16)
                p.sy(cskl); cs.inc(32)
                ld_cs2 = cs.n
                p.v(lambda vv, ld_cs2=ld_cs2: vv.wait_ge(s_cs, ld_cs2))
                for g_loc in range(4):
                    rope_batch(g_loc * 4096 + th2 * 2048, k1h,
                               bass.AP(htile, 0 * RL, [[RL, 32], [1, 2048]]),
                               bass.AP(htile, 32 * RL, [[RL, 32], [1, 2048]]),
                               bass.AP(htile, 0 * RL + 2048, [[RL, 32], [1, 2048]]),
                               bass.AP(htile, 32 * RL + 2048, [[RL, 32], [1, 2048]]), 2048)
            v_krope = v.n
            p.t(lambda te, v_krope=v_krope: te.wait_ge(s_v, v_krope))

            for qb in range(2):
                for hl in range(16):
                    h = hh * 16 + hl
                    g_loc = hl // 4
                    for tbl in range(32):
                        sb = tbl % 2
                        def qkmm(te, hl=hl, qb=qb, tbl=tbl, sb=sb, g_loc=g_loc):
                            te.matmul(pp[:, sb * 512:(sb + 1) * 512],
                                      k1h[0:76, g_loc * 4096 + tbl * 128: g_loc * 4096 + tbl * 128 + 128],
                                      q1[0:76, hl * TC + qb * 512: hl * TC + qb * 512 + 512],
                                      start=True, stop=True).then_inc(s_t, 1)
                        p.t(qkmm); t.inc()
                        t_qk = t.n

                        def acts(sc, h=h, tbl=tbl, sb=sb, t_qk=t_qk):
                            sc.wait_ge(s_t, t_qk)
                            sc.activation(Tbuf[:], pp[:, sb * 512:(sb + 1) * 512],
                                          AF.Tanh, bias=tbias[:, h * 32 + tbl: h * 32 + tbl + 1],
                                          scale=1.0 / 240)
                            sc.activation(Pbuf[:], Tbuf[:], AF.Exp,
                                          bias=m30[:], scale=30.0).then_inc(s_s, 1)
                        p.s(acts); s.inc()
                        s_e = s.n

                        def pmask(vv, qb=qb, tbl=tbl, s_e=s_e):
                            vv.wait_ge(s_s, s_e)
                            off = 3968 + qb * 512 - tbl * 128
                            vv.tensor_tensor(P2[:], Pbuf[:], band[:, off:off + 512],
                                             ALU.mult).then_inc(s_v, 1)
                        p.v(pmask); v.inc()
                        v_m = v.n

                        def pv(te, tbl=tbl, hl=hl, v_m=v_m, g_loc=g_loc):
                            te.wait_ge(s_v, v_m)
                            te.matmul(pp[0:65, (2 + hl % 2) * 512:(2 + hl % 2) * 512 + 512],
                                      v1h[:, tbl * 260 + g_loc * 65: tbl * 260 + g_loc * 65 + 65],
                                      P2[:], start=(tbl == 0), stop=(tbl == 31)).then_inc(s_t, 1)
                        p.t(pv); t.inc()

                    t_pv = t.n
                    cb = (2 + hl % 2) * 512

                    def lsetup(vv, t_pv=t_pv, cb=cb):
                        vv.wait_ge(s_t, t_pv)
                        vv.tensor_copy(lrow[0:1, :], pp[64:65, cb:cb + 512])
                        vv.reciprocal(lrow[0:1, :], lrow[0:1, :]).then_inc(s_v, 1)
                    p.v(lsetup); v.inc()
                    v_l = v.n

                    def lbc(te, v_l=v_l):
                        te.wait_ge(s_v, v_l)
                        te.matmul(pp[0:64, 4 * 512: 4 * 512 + 512], C["ones_r_f32"][0:1, 0:64],
                                  lrow[0:1, :], start=True, stop=True).then_inc(s_t, 1)
                    p.t(lbc); t.inc()
                    t_lb = t.n

                    def cdiv(vv, h=h, qb=qb, cb=cb, t_lb=t_lb):
                        vv.wait_ge(s_t, t_lb)
                        vv.tensor_copy(lb[:], pp[0:64, 4 * 512: 4 * 512 + 512])
                        vv.scalar_tensor_tensor(
                            ctxa[(h % 2) * 64:(h % 2) * 64 + 64,
                                 (h // 2) * TC + qb * 512:(h // 2) * TC + qb * 512 + 512],
                            pp[0:64, cb:cb + 512], 1.0, lb[:], ALU.mult, ALU.mult
                        ).then_inc(s_v, 1)
                    p.v(cdiv); v.inc()
                    v_c = v.n
                    p.t(lambda te, v_c=v_c: te.wait_ge(s_v, v_c))

        for oc in range(NCH):
            buf = oc % 2
            war = t.n if oc >= 2 else None
            def wol(g, oc=oc, buf=buf, war=war):
                if war is not None:
                    g.wait_ge(s_t, war)
                src = bass.AP(wo.tensor, oc * 128, [[D, 128], [128 * D, NCH], [1, 128]])
                g.dma_start(wot[:, buf * 2048:(buf + 1) * 2048], src).then_inc(s_w, 16)
            p.g(wol); w.inc(16)
            w_t = w.n

            st_prev = st.n
            def xl4(sy, oc=oc, st_prev=st_prev):
                if oc > 0:
                    sy.wait_ge(s_st, st_prev)
                src = bass.AP(xT.tensor, oc * 128 * TC, [[TC, 128], [1, TC]])
                sy.dma_start(xnew[:], src).then_inc(s_ld, 16)
            p.sy(xl4); ld.inc(16)
            ld_xc = ld.n

            def omm(te, oc=oc, buf=buf, w_t=w_t):
                te.wait_ge(s_w, w_t)
                li = None
                for th in range(2):
                    for ic in range(NCH):
                        li = te.matmul(pp[:, (5 + th) * 512:(6 + th) * 512],
                                       wot[:, buf * 2048 + ic * 128: buf * 2048 + ic * 128 + 128],
                                       ctxa[:, ic * TC + th * 512: ic * TC + th * 512 + 512],
                                       start=(ic == 0), stop=(ic == NCH - 1))
                li.then_inc(s_t, 1)
            p.t(omm); t.inc()
            t_o = t.n

            def oev(vv, oc=oc, t_o=t_o, ld_xc=ld_xc):
                vv.wait_ge(s_t, t_o)
                vv.wait_ge(s_ld, ld_xc)
                li = None
                for th in range(2):
                    vv.tensor_scalar(C["ftmp"][:, 0:512], pp[:, (5 + th) * 512:(6 + th) * 512],
                                     bo_sb[:, oc:oc + 1], None, ALU.add)
                    li = vv.scalar_tensor_tensor(xnew[:, th * 512:(th + 1) * 512],
                                                 C["ftmp"][:, 0:512], 1.0,
                                                 xnew[:, th * 512:(th + 1) * 512],
                                                 ALU.mult, ALU.add)
                li.then_inc(s_v, 1)
            p.v(oev); v.inc()
            v_o = v.n
            p.t(lambda te, v_o=v_o: te.wait_ge(s_v, v_o))

            def ost(sy, oc=oc, v_o=v_o):
                sy.wait_ge(s_v, v_o)
                dst = bass.AP(xo.tensor, oc * 128 * TC, [[TC, 128], [1, TC]])
                sy.dma_start(dst, xnew[:]).then_inc(s_st, 16)
            p.sy(ost); st.inc(16)

        st_all = st.n
        p.sy(lambda sy, st_all=st_all: sy.wait_ge(s_st, st_all))

        p.emit(blk)


def _mk_sems_fused(ctx, nc):
    S = {}
    for name in ("ld", "w", "v", "t", "s", "st", "cs", "d", "ag"):
        h = ctx.enter_context(nc.semaphore("sF_" + name))
        S[name] = Sem(h)
    return S


def barrier_emit(blk, S):
    vals = [(sem.h, sem.n) for sem in S.values() if sem.n > 0]
    for eng in ("sync", "gpsimd", "vector", "scalar", "tensor"):
        def b(e, vals=vals):
            for h, n in vals:
                e.wait_ge(h, n)
        getattr(blk, eng)(b)


def build_fused():
    nc = bass.Bass("TRN2", target_bir_lowering=False)
    TT = 256
    xT = nc.dram_tensor("xT", [D, TC], F32, kind="ExternalInput").ap()
    xfT = nc.dram_tensor("xfT", [D, 4096], F32, kind="ExternalInput").ap()
    wq0 = nc.dram_tensor("wq0", [D, D], F32, kind="ExternalInput").ap()
    wk0 = nc.dram_tensor("wk0", [D, 512], F32, kind="ExternalInput").ap()
    wv0 = nc.dram_tensor("wv0", [D, 512], F32, kind="ExternalInput").ap()
    wo0 = nc.dram_tensor("wo0", [D, D], F32, kind="ExternalInput").ap()
    bo0 = nc.dram_tensor("bo0", [128, 16], F32, kind="ExternalInput").ap()
    qbits = nc.dram_tensor("qbits", [12, 32 * TC], BF16, kind="ExternalInput").ap()
    cosq = nc.dram_tensor("cosq", [32, TC], BF16, kind="ExternalInput").ap()
    sinq = nc.dram_tensor("sinq", [32, TC], BF16, kind="ExternalInput").ap()
    cosk = nc.dram_tensor("cosk", [32, 4096], BF16, kind="ExternalInput").ap()
    sink = nc.dram_tensor("sink", [32, 4096], BF16, kind="ExternalInput").ap()
    tbias = nc.dram_tensor("tbias", [128, 1024], F32, kind="ExternalInput").ap()
    band = nc.dram_tensor("band", [128, 4992], BF16, kind="ExternalInput").ap()
    pg8 = nc.dram_tensor("pg8", [1024, 16384], U8, kind="ExternalInput").ap()
    pu8 = nc.dram_tensor("pu8", [1024, 16384], U8, kind="ExternalInput").ap()
    pd8 = nc.dram_tensor("pd8", [8192, D], U8, kind="ExternalInput").ap()
    gscT = nc.dram_tensor("gscT", [64, 16384], F32, kind="ExternalInput").ap()
    uscT = nc.dram_tensor("uscT", [64, 16384], F32, kind="ExternalInput").ap()
    dscT = nc.dram_tensor("dscT", [512, D], F32, kind="ExternalInput").ap()
    wq1 = nc.dram_tensor("wq1", [D, D], F32, kind="ExternalInput").ap()
    wk1 = nc.dram_tensor("wk1", [D, D], F32, kind="ExternalInput").ap()
    wv1 = nc.dram_tensor("wv1", [D, D], F32, kind="ExternalInput").ap()
    wo1 = nc.dram_tensor("wo1", [D, D], F32, kind="ExternalInput").ap()
    wg8 = nc.dram_tensor("wg8", [D, 8192], U8, kind="ExternalInput").ap()
    wu8 = nc.dram_tensor("wu8", [D, 8192], U8, kind="ExternalInput").ap()
    wd8 = nc.dram_tensor("wd8", [8192, D], U8, kind="ExternalInput").ap()
    gsc1 = nc.dram_tensor("gsc1", [128, 64], F32, kind="ExternalInput").ap()
    usc1 = nc.dram_tensor("usc1", [128, 64], F32, kind="ExternalInput").ap()
    dsc1 = nc.dram_tensor("dsc1", [128, 16], F32, kind="ExternalInput").ap()
    xo = nc.dram_tensor("xo", [D, TC], F32, kind="ExternalOutput").ap()

    x_a = nc.dram_tensor("x_a", [D, TC], F32, kind="Internal").ap()
    x_b = nc.dram_tensor("x_b", [D, TC], F32, kind="Internal").ap()
    x_c = nc.dram_tensor("x_c", [D, TC], F32, kind="Internal").ap()
    xful = nc.dram_tensor("xful", [4 * D, TC], F32, kind="Internal").ap()

    with ExitStack() as octx:
        S = _mk_sems_fused(octx, nc)
        blk = octx.enter_context(nc.Block())

        emit_l0attn(nc, blk, S, xT, xfT, wq0, wk0, wv0, wo0, bo0, qbits, cosq,
                    sinq, cosk, sink, tbias, band, x_a)
        barrier_emit(blk, S)
        emit_l0ffn(nc, blk, S, x_a, pg8, pu8, pd8, gscT, uscT, dscT, x_b)
        barrier_emit(blk, S)

        ag = S["ag"]
        stn = S["st"].n

        def do_ag(g, stn=stn):
            g.wait_ge(S["st"].h, stn)
            g.collective_compute(
                "AllGather", ALU.bypass, [[0, 1, 2, 3], [4, 5, 6, 7]],
                ins=[x_b], outs=[xful]).then_inc(ag.h, 16)
        blk.gpsimd(do_ag)
        ag.inc(16)
        agn = ag.n
        blk.sync(lambda sy, agn=agn: sy.wait_ge(ag.h, agn))

        emit_l1attn(nc, blk, S, x_b, xful, wq1, wk1, wv1, wo1, band, x_c,
                    xful_blocks=True)
        barrier_emit(blk, S)
        emit_l1ffn(nc, blk, S, x_c, wg8, wu8, wd8, gsc1, usc1, dsc1, xo)
    return nc


def build_fused_a():
    nc = bass.Bass("TRN2", target_bir_lowering=False)
    TT = 256
    xT = nc.dram_tensor("xT", [D, TC], F32, kind="ExternalInput").ap()
    xfT = nc.dram_tensor("xfT", [D, 4096], F32, kind="ExternalInput").ap()
    wq0 = nc.dram_tensor("wq0", [D, D], F32, kind="ExternalInput").ap()
    wk0 = nc.dram_tensor("wk0", [D, 512], F32, kind="ExternalInput").ap()
    wv0 = nc.dram_tensor("wv0", [D, 512], F32, kind="ExternalInput").ap()
    wo0 = nc.dram_tensor("wo0", [D, D], F32, kind="ExternalInput").ap()
    bo0 = nc.dram_tensor("bo0", [128, 16], F32, kind="ExternalInput").ap()
    qbits = nc.dram_tensor("qbits", [12, 32 * TC], BF16, kind="ExternalInput").ap()
    cosq = nc.dram_tensor("cosq", [32, TC], BF16, kind="ExternalInput").ap()
    sinq = nc.dram_tensor("sinq", [32, TC], BF16, kind="ExternalInput").ap()
    cosk = nc.dram_tensor("cosk", [32, 4096], BF16, kind="ExternalInput").ap()
    sink = nc.dram_tensor("sink", [32, 4096], BF16, kind="ExternalInput").ap()
    tbias = nc.dram_tensor("tbias", [128, 1024], F32, kind="ExternalInput").ap()
    band = nc.dram_tensor("band", [128, 4992], BF16, kind="ExternalInput").ap()
    pg8 = nc.dram_tensor("pg8", [1024, 16384], U8, kind="ExternalInput").ap()
    pu8 = nc.dram_tensor("pu8", [1024, 16384], U8, kind="ExternalInput").ap()
    pd8 = nc.dram_tensor("pd8", [8192, D], U8, kind="ExternalInput").ap()
    gscT = nc.dram_tensor("gscT", [64, 16384], F32, kind="ExternalInput").ap()
    uscT = nc.dram_tensor("uscT", [64, 16384], F32, kind="ExternalInput").ap()
    dscT = nc.dram_tensor("dscT", [512, D], F32, kind="ExternalInput").ap()
    xo = nc.dram_tensor("xo", [D, TC], F32, kind="ExternalOutput").ap()
    x_a = nc.dram_tensor("x_a", [D, TC], F32, kind="Internal").ap()

    with ExitStack() as octx:
        S = _mk_sems_fused(octx, nc)
        blk = octx.enter_context(nc.Block())
        emit_l0attn(nc, blk, S, xT, xfT, wq0, wk0, wv0, wo0, bo0, qbits, cosq,
                    sinq, cosk, sink, tbias, band, x_a)
        barrier_emit(blk, S)
        emit_l0ffn(nc, blk, S, x_a, pg8, pu8, pd8, gscT, uscT, dscT, xo)
    return nc


def build_fused_b():
    nc = bass.Bass("TRN2", target_bir_lowering=False)
    xT = nc.dram_tensor("xT", [D, TC], F32, kind="ExternalInput").ap()
    xfT = nc.dram_tensor("xfT", [D, 4096], F32, kind="ExternalInput").ap()
    wq1 = nc.dram_tensor("wq1", [D, D], F32, kind="ExternalInput").ap()
    wk1 = nc.dram_tensor("wk1", [D, D], F32, kind="ExternalInput").ap()
    wv1 = nc.dram_tensor("wv1", [D, D], F32, kind="ExternalInput").ap()
    wo1 = nc.dram_tensor("wo1", [D, D], F32, kind="ExternalInput").ap()
    band = nc.dram_tensor("band", [128, 4992], BF16, kind="ExternalInput").ap()
    wg8 = nc.dram_tensor("wg8", [D, 8192], U8, kind="ExternalInput").ap()
    wu8 = nc.dram_tensor("wu8", [D, 8192], U8, kind="ExternalInput").ap()
    wd8 = nc.dram_tensor("wd8", [8192, D], U8, kind="ExternalInput").ap()
    gsc1 = nc.dram_tensor("gsc1", [128, 64], F32, kind="ExternalInput").ap()
    usc1 = nc.dram_tensor("usc1", [128, 64], F32, kind="ExternalInput").ap()
    dsc1 = nc.dram_tensor("dsc1", [128, 16], F32, kind="ExternalInput").ap()
    xo = nc.dram_tensor("xo", [D, TC], F32, kind="ExternalOutput").ap()
    x_c = nc.dram_tensor("x_c", [D, TC], F32, kind="Internal").ap()

    with ExitStack() as octx:
        S = _mk_sems_fused(octx, nc)
        blk = octx.enter_context(nc.Block())
        emit_l1attn(nc, blk, S, xT, xfT, wq1, wk1, wv1, wo1, band, x_c)
        barrier_emit(blk, S)
        emit_l1ffn(nc, blk, S, x_c, wg8, wu8, wd8, gsc1, usc1, dsc1, xo)
    return nc


from concourse.bass_utils import run_bass_kernel_spmd

_SIG = perm_evens_odds(2048)
_TAU = perm_evens_odds(16384)
_INV_SIG = np.argsort(_SIG)


def _u8view(a_int32):
    a = np.ascontiguousarray(a_int32.astype(np.int32))
    return a.view(np.uint8)[:, ::4]


def _stage_l0attn_consts(q0):
    NH, Dh = 32, 64
    slopes = 2.0 ** (-(np.arange(1, NH + 1) * 8.0 / NH))
    qpos = q0 + np.arange(1024)
    qbits = np.zeros((12, NH * 1024), dtype=np.float32)
    for b in range(12):
        bit = (qpos >> b) & 1
        for h in range(NH):
            qbits[b, h * 1024:(h + 1) * 1024] = -slopes[h] * (1 << b) * bit
    inv = 1.0 / 10000.0 ** (np.arange(0, Dh, 2) / Dh)
    frq = qpos[:, None] * inv[None, :]
    cosq = np.ascontiguousarray(np.cos(frq).T).astype(BF)
    sinq = np.ascontiguousarray(np.sin(frq).T).astype(BF)
    tpos = np.arange(4096)
    frk = tpos[:, None] * inv[None, :]
    cosk = np.ascontiguousarray(np.cos(frk).T).astype(BF)
    sink = np.ascontiguousarray(np.sin(frk).T).astype(BF)
    tbias = np.zeros((128, 1024), dtype=np.float32)
    pc = np.arange(128)
    for h in range(NH):
        for tbl in range(32):
            tbias[:, h * 32 + tbl] = slopes[h] * (tbl * 128 + pc) / 240.0
    u = np.arange(4992)[None, :]
    band = (u >= pc[:, None] + 3968 - q0).astype(np.float32).astype(BF)
    return {"qbits": qbits.astype(BF), "cosq": cosq, "sinq": sinq,
            "cosk": cosk, "sink": sink, "tbias": tbias, "band": band}


def kernel(L0_attn_hidden_states, L0_attn_Wq, L0_attn_Wk, L0_attn_Wv, L0_attn_Wo,
           L0_attn_bo, L0_ffn_gate_W_packed, L0_ffn_gate_scales, L0_ffn_up_W_packed,
           L0_ffn_up_scales, L0_ffn_down_W_packed, L0_ffn_down_scales,
           L1_attn_Wq, L1_attn_Wk, L1_attn_Wv, L1_attn_Wo,
           L1_ffn_gate_w, L1_ffn_gate_scale, L1_ffn_up_w, L1_ffn_up_scale,
           L1_ffn_down_w, L1_ffn_down_scale):
    x = np.asarray(L0_attn_hidden_states, dtype=np.float32)
    sig, tau, inv = _SIG, _TAU, _INV_SIG
    Dh = 64
    rho = np.concatenate([np.arange(0, Dh, 2), np.arange(1, Dh, 2)])
    colp_q = np.concatenate([h * Dh + rho for h in range(32)])
    colp_k = np.concatenate([g * Dh + rho for g in range(8)])

    wq0 = np.ascontiguousarray(np.asarray(L0_attn_Wq, np.float32)[sig][:, colp_q])
    wk0 = np.ascontiguousarray(np.asarray(L0_attn_Wk, np.float32)[sig][:, colp_k])
    wv0 = np.ascontiguousarray(np.asarray(L0_attn_Wv, np.float32)[sig])
    wo0 = np.ascontiguousarray(np.asarray(L0_attn_Wo, np.float32)[:, sig])
    bo0 = np.ascontiguousarray(np.asarray(L0_attn_bo, np.float32)[sig].reshape(16, 128).T)
    consts = [_stage_l0attn_consts(q0 * 1024) for q0 in range(4)]

    pg8 = np.ascontiguousarray(_u8view(np.asarray(L0_ffn_gate_W_packed)).T[:, tau])
    pu8 = np.ascontiguousarray(_u8view(np.asarray(L0_ffn_up_W_packed)).T[:, tau])
    pd8 = np.ascontiguousarray(_u8view(np.asarray(L0_ffn_down_W_packed)).T[:, sig])
    gscT = np.ascontiguousarray(np.asarray(L0_ffn_gate_scales, np.float32).T[:, tau])
    uscT = np.ascontiguousarray(np.asarray(L0_ffn_up_scales, np.float32).T[:, tau])
    dscT = np.ascontiguousarray(np.asarray(L0_ffn_down_scales, np.float32).T[:, sig])

    wq1 = np.ascontiguousarray(np.asarray(L1_attn_Wq, np.float32)[sig])
    wk1 = np.ascontiguousarray(np.asarray(L1_attn_Wk, np.float32)[sig])
    wv1 = np.ascontiguousarray(np.asarray(L1_attn_Wv, np.float32)[sig])
    wo1 = np.ascontiguousarray(np.asarray(L1_attn_Wo, np.float32)[:, sig])

    wg8 = np.ascontiguousarray(_u8view(np.asarray(L1_ffn_gate_w)).T[sig])
    wu8 = np.ascontiguousarray(_u8view(np.asarray(L1_ffn_up_w)).T[sig])
    wd8 = np.ascontiguousarray(_u8view(np.asarray(L1_ffn_down_w)).T[:, sig])
    gsc1 = np.ascontiguousarray(np.asarray(L1_ffn_gate_scale, np.float32).reshape(64, 128).T)
    usc1 = np.ascontiguousarray(np.asarray(L1_ffn_up_scale, np.float32).reshape(64, 128).T)
    dsc1 = np.ascontiguousarray(np.asarray(L1_ffn_down_scale, np.float32)[sig].reshape(16, 128).T)

    nc_a = build_fused_a()
    in_maps = []
    for c in range(8):
        b, qi = c // 4, c % 4
        xb = x[b]
        im = {"xT": np.ascontiguousarray(xb[qi * 1024:(qi + 1) * 1024].T[sig]),
              "xfT": np.ascontiguousarray(xb.T[sig]),
              "wq0": wq0, "wk0": wk0, "wv0": wv0, "wo0": wo0, "bo0": bo0,
              "qbits": consts[qi]["qbits"], "cosq": consts[qi]["cosq"],
              "sinq": consts[qi]["sinq"], "cosk": consts[qi]["cosk"],
              "sink": consts[qi]["sink"], "tbias": consts[qi]["tbias"],
              "band": consts[qi]["band"],
              "pg8": pg8, "pu8": pu8, "pd8": pd8, "gscT": gscT, "uscT": uscT,
              "dscT": dscT}
        in_maps.append(im)
    res = run_bass_kernel_spmd(nc_a, in_maps, core_ids=list(range(8)))
    shards = [res.results[c]["xo"] for c in range(8)]

    nc_b = build_fused_b()
    in_maps = []
    for c in range(8):
        b, qi = c // 4, c % 4
        xfT = np.ascontiguousarray(np.concatenate(shards[b * 4:(b + 1) * 4], axis=1))
        in_maps.append({"xT": np.ascontiguousarray(shards[c]), "xfT": xfT,
                        "wq1": wq1, "wk1": wk1, "wv1": wv1, "wo1": wo1,
                        "band": consts[qi]["band"],
                        "wg8": wg8, "wu8": wu8, "wd8": wd8, "gsc1": gsc1,
                        "usc1": usc1, "dsc1": dsc1})
    res = run_bass_kernel_spmd(nc_b, in_maps, core_ids=list(range(8)))
    out = np.empty((2, 4096, 2048), dtype=np.float32)
    for c in range(8):
        b, qi = c // 4, c % 4
        out[b, qi * 1024:(qi + 1) * 1024] = res.results[c]["xo"][inv].T
    return out


# version 04
# speedup vs baseline: 1.1921x; 1.1921x over previous
import numpy as np
import ml_dtypes
from contextlib import ExitStack
import concourse.bass as bass
import concourse.mybir as mybir
from concourse.alu_op_type import AluOpType as ALU

F32 = mybir.dt.float32
BF16 = mybir.dt.bfloat16
U16 = mybir.dt.uint16
U8 = mybir.dt.uint8
FP8 = mybir.dt.float8e4
AF = mybir.ActivationFunctionType
BF = ml_dtypes.bfloat16

D = 2048
TC = 1024
NCH = 16


def perm_evens_odds(n):
    p = []
    for b in range(0, n, 256):
        p.extend(range(b, b + 256, 2))
        p.extend(range(b + 1, b + 256, 2))
    return np.array(p)



_PHASE_N = [0]


def _pn(name):
    return f"{name}_ph{_PHASE_N[0]}"


def _next_phase():
    _PHASE_N[0] += 1


class Sem:
    def __init__(self, h):
        self.h = h
        self.n = 0

    def inc(self, k=1):
        self.n += k
        return self.n


class P:
    ENGINES = ("sync", "gpsimd", "vector", "scalar", "tensor")

    def __init__(self):
        self.q = {e: [] for e in self.ENGINES}

    def add(self, eng, fn):
        self.q[eng].append(fn)

    def sy(self, fn): self.add("sync", fn)
    def g(self, fn): self.add("gpsimd", fn)
    def v(self, fn): self.add("vector", fn)
    def s(self, fn): self.add("scalar", fn)
    def t(self, fn): self.add("tensor", fn)

    def emit(self, blk):
        for eng in self.ENGINES:
            fns = self.q[eng]
            if not fns:
                continue
            def body(e, fns=fns):
                for f in fns:
                    f(e)
            getattr(blk, eng)(body)


def ln_emit(p, sems, xsb, hsb, pp, C, make_rs=False, rs_sb=None, tc=TC, nch=NCH):
    v, t, s = sems["v"], sems["t"], sems["s"]
    half = tc // 512

    def sq(sc):
        li = None
        for c in range(nch):
            li = sc.square(hsb[:, c * tc:(c + 1) * tc], xsb[:, c * tc:(c + 1) * tc])
        li.then_inc(s.h, 1)
    p.s(sq); s.inc()
    s_sq = s.n

    b_s1 = pp[0:1, 4 * 512: 4 * 512 + tc]
    b_s2 = pp[0:1, 6 * 512: 6 * 512 + tc]

    def stats(te, s_sq=s_sq):
        te.wait_ge(s.h, s_sq)
        li = None
        for th in range(half):
            for c in range(nch):
                li = te.matmul(b_s1[:, th * 512:(th + 1) * 512], C["ones_c_f32"],
                               xsb[:, c * tc + th * 512: c * tc + (th + 1) * 512],
                               start=(c == 0), stop=(c == nch - 1))
        for th in range(half):
            for c in range(nch):
                li = te.matmul(b_s2[:, th * 512:(th + 1) * 512], C["ones_c_bf"],
                               hsb[:, c * tc + th * 512: c * tc + (th + 1) * 512],
                               start=(c == 0), stop=(c == nch - 1))
        li.then_inc(t.h, 1)
    p.t(stats); t.inc()
    t_stats = t.n

    mrow, rrow, vrow, srow = C["mrow"], C["rrow"], C["vrow"], C["srow"]

    def rows(vv, t_stats=t_stats):
        vv.wait_ge(t.h, t_stats)
        vv.tensor_scalar_mul(mrow[0:1, 0:tc], b_s1, 1.0 / D)
        vv.tensor_tensor(vrow[0:1, 0:tc], mrow[0:1, 0:tc], mrow[0:1, 0:tc], ALU.mult)
        vv.scalar_tensor_tensor(vrow[0:1, 0:tc], b_s2, 1.0 / D, vrow[0:1, 0:tc],
                                ALU.mult, ALU.subtract).then_inc(v.h, 1)
    p.v(rows); v.inc()
    v_rows = v.n

    def dosqrt(sc, v_rows=v_rows):
        sc.wait_ge(v.h, v_rows)
        sc.activation(srow[0:1, 0:tc], vrow[0:1, 0:tc], AF.Sqrt,
                      bias=C["eps"], scale=1.0).then_inc(s.h, 1)
    p.s(dosqrt); s.inc()
    s_sqrt = s.n

    def dorecip(vv, s_sqrt=s_sqrt):
        vv.wait_ge(s.h, s_sqrt)
        vv.reciprocal(rrow[0:1, 0:tc], srow[0:1, 0:tc]).then_inc(v.h, 1)
    p.v(dorecip); v.inc()
    v_recip = v.n

    b_mb = pp[:, 4 * 512: 4 * 512 + tc]
    b_rb = pp[:, 6 * 512: 6 * 512 + tc]

    def bcast(te, v_recip=v_recip):
        te.wait_ge(v.h, v_recip)
        li = None
        for th in range(half):
            te.matmul(b_mb[:, th * 512:(th + 1) * 512], C["ones_r_f32"],
                      mrow[0:1, th * 512:(th + 1) * 512], start=True, stop=True)
            li = te.matmul(b_rb[:, th * 512:(th + 1) * 512], C["ones_r_f32"],
                           rrow[0:1, th * 512:(th + 1) * 512], start=True, stop=True)
        li.then_inc(t.h, 1)
    p.t(bcast); t.inc()
    t_bcast = t.n

    def mk_h(vv, t_bcast=t_bcast):
        vv.wait_ge(t.h, t_bcast)
        li = None
        for c in range(nch):
            vv.tensor_tensor(C["ftmp"][:, 0:tc], xsb[:, c * tc:(c + 1) * tc], b_mb, ALU.subtract)
            li = vv.tensor_tensor(hsb[:, c * tc:(c + 1) * tc], C["ftmp"][:, 0:tc], b_rb, ALU.mult)
        li.then_inc(v.h, 1)
    p.v(mk_h); v.inc()
    v_h = v.n

    if make_rs:
        b_rs = pp[0:1, 4 * 512: 4 * 512 + tc]

        def rs_mm(te, v_h=v_h):
            te.wait_ge(v.h, v_h)
            li = None
            for th in range(half):
                for c in range(nch):
                    li = te.matmul(b_rs[:, th * 512:(th + 1) * 512], C["ones_c_bf"],
                                   hsb[:, c * tc + th * 512: c * tc + (th + 1) * 512],
                                   start=(c == 0), stop=(c == nch - 1))
            li.then_inc(t.h, 1)
        p.t(rs_mm); t.inc()
        t_rs = t.n

        def rs_cp(vv, t_rs=t_rs):
            vv.wait_ge(t.h, t_rs)
            vv.tensor_copy(rs_sb[0:1, 0:tc], b_rs).then_inc(v.h, 1)
        p.v(rs_cp); v.inc()
    return v_h


def alloc_common(ctx, nc, roww=TC, ftmpw=TC):
    C = {}
    for name, shape, dt in [("ones_c_f32", [128, 1], F32), ("ones_c_bf", [128, 1], BF16),
                            ("ones_r_f32", [1, 128], F32), ("c128", [1, 128], BF16),
                            ("eps", [1, 1], F32), ("mrow", [1, roww], F32), ("rrow", [1, roww], F32),
                            ("vrow", [1, roww], F32), ("srow", [1, roww], F32), ("ftmp", [128, ftmpw], F32)]:
        h = ctx.enter_context(nc.sbuf_tensor(_pn(name), shape, dt))
        C[name] = h[:] if name in ("ones_c_f32", "ones_c_bf", "ones_r_f32", "eps") else h
    return C


def init_consts_emit(p, sems, C):
    v = sems["v"]

    def ic(vv):
        vv.memset(C["ones_c_f32"], 1.0)
        vv.memset(C["ones_c_bf"], 1.0)
        vv.memset(C["ones_r_f32"], 1.0)
        vv.memset(C["c128"][:], -128.0)
        vv.memset(C["eps"], 1e-5).then_inc(v.h, 1)
    p.v(ic); v.inc()


def build_l1ffn(stages=99):
    nc = bass.Bass("TRN2", target_bir_lowering=False)
    xT = nc.dram_tensor("xT", [D, TC], F32, kind="ExternalInput").ap()
    wg8 = nc.dram_tensor("wg8", [D, 8192], U8, kind="ExternalInput").ap()
    wu8 = nc.dram_tensor("wu8", [D, 8192], U8, kind="ExternalInput").ap()
    wd8 = nc.dram_tensor("wd8", [8192, D], U8, kind="ExternalInput").ap()
    gsc = nc.dram_tensor("gsc", [128, 64], F32, kind="ExternalInput").ap()
    usc = nc.dram_tensor("usc", [128, 64], F32, kind="ExternalInput").ap()
    dsc = nc.dram_tensor("dsc", [128, 16], F32, kind="ExternalInput").ap()
    xo = nc.dram_tensor("xo", [D, TC], F32, kind="ExternalOutput").ap()
    with ExitStack() as octx:
        S = _mk_sems_fused(octx, nc)
        blk = octx.enter_context(nc.Block())
        emit_l1ffn(nc, blk, S, xT, wg8, wu8, wd8, gsc, usc, dsc, xo)
    return nc


def emit_l1ffn(nc, blk, S, xT, wg8, wu8, wd8, gsc, usc, dsc, xo, stages=99):
    _next_phase()

    I = 8192
    NIC = I // 128
    GRP = 16
    NGRP = NIC // GRP
    NIS = 4

    with ExitStack() as ctx:
        xsb = ctx.enter_context(nc.sbuf_tensor(_pn("xsb"), [128, NCH * TC], F32))
        hsb = ctx.enter_context(nc.sbuf_tensor(_pn("hsb"), [128, NCH * TC], BF16))
        inter = ctx.enter_context(nc.sbuf_tensor(_pn("inter"), [128, GRP * TC], BF16))
        g_sb = ctx.enter_context(nc.sbuf_tensor(_pn("g_sb"), [128, 4 * TC], BF16))
        sig = ctx.enter_context(nc.sbuf_tensor(_pn("sig"), [128, 4 * TC], BF16))
        wgt = ctx.enter_context(nc.sbuf_tensor(_pn("wgt"), [128, 2 * 512], BF16))
        wut = ctx.enter_context(nc.sbuf_tensor(_pn("wut"), [128, 2 * 512], BF16))
        wdt = ctx.enter_context(nc.sbuf_tensor(_pn("wdt"), [128, 2 * 512], BF16))
        gsc_sb = ctx.enter_context(nc.sbuf_tensor(_pn("gsc_sb"), [128, 64], F32))
        usc_sb = ctx.enter_context(nc.sbuf_tensor(_pn("usc_sb"), [128, 64], F32))
        dsc_sb = ctx.enter_context(nc.sbuf_tensor(_pn("dsc_sb"), [128, 16], F32))
        rs_sb = ctx.enter_context(nc.sbuf_tensor(_pn("rs_sb"), [1, TC], BF16))
        rs2_sb = ctx.enter_context(nc.sbuf_tensor(_pn("rs2_sb"), [1, TC], BF16))
        btmp = ctx.enter_context(nc.sbuf_tensor(_pn("btmp"), [128, 512], BF16))
        C = alloc_common(ctx, nc)
        pp = ctx.enter_context(nc.psum_tensor(_pn("pp"), [128, 8 * 512], F32))

        p = P()
        ld, w, v, t, s, st = S["ld"], S["w"], S["v"], S["t"], S["s"], S["st"]
        s_ld, s_w, s_v, s_t, s_s, s_st = ld.h, w.h, v.h, t.h, s.h, st.h
        sems = {"v": v, "t": t, "s": s}

        init_consts_emit(p, sems, C)
        v_c0 = v.n
        p.t(lambda te, v_c0=v_c0: te.wait_ge(s_v, v_c0))
        p.s(lambda sc, v_c0=v_c0: sc.wait_ge(s_v, v_c0))

        def loads(sy):
            xT_ap = bass.AP(xT.tensor, 0, [[TC, 128], [128 * TC, NCH], [1, TC]])
            sy.dma_start(xsb[:], xT_ap).then_inc(s_ld, 16)
            sy.dma_start(gsc_sb[:], gsc).then_inc(s_ld, 16)
            sy.dma_start(usc_sb[:], usc).then_inc(s_ld, 16)
            sy.dma_start(dsc_sb[:], dsc).then_inc(s_ld, 16)
        p.sy(loads); ld.inc(64)
        ld_x = ld.n

        p.s(lambda sc: sc.wait_ge(s_ld, ld_x))
        p.v(lambda vv: vv.wait_ge(s_ld, ld_x))
        p.t(lambda te: te.wait_ge(s_ld, ld_x))

        if stages >= 2:
            ln_emit(p, sems, xsb, hsb, pp, C, make_rs=True, rs_sb=rs_sb)
            v_ln = v.n
            p.t(lambda te, v_ln=v_ln: te.wait_ge(s_v, v_ln))

        s_sig_cur = [0]
        for grp in range(NGRP if stages >= 3 else 0):
            for isl in range(NIS):
                isg = grp * NIS + isl
                for mat in range(2):
                    wt = wgt if mat == 0 else wut
                    w8 = wg8 if mat == 0 else wu8
                    for kc in range(NCH):
                        buf = kc % 2
                        war = t.n - 1 if t.n >= 1 else None
                        def wl(g, kc=kc, isg=isg, wt=wt, w8=w8, buf=buf, war=war):
                            if war is not None:
                                g.wait_ge(s_t, war)
                            g.dma_start(wt[:, buf * 512:(buf + 1) * 512],
                                        w8[kc * 128:(kc + 1) * 128, isg * 512:(isg + 1) * 512]
                                        ).then_inc(s_w, 16)
                        p.g(wl); w.inc(16)
                        w_tile = w.n

                        def mm(te, kc=kc, buf=buf, w_tile=w_tile, wt=wt):
                            te.wait_ge(s_w, w_tile)
                            li = None
                            for ic in range(4):
                                for th in range(2):
                                    li = te.matmul(pp[:, (ic * 2 + th) * 512:(ic * 2 + th + 1) * 512],
                                                   wt[:, buf * 512 + ic * 128: buf * 512 + ic * 128 + 128],
                                                   hsb[:, kc * TC + th * 512: kc * TC + th * 512 + 512],
                                                   start=(kc == 0), stop=False)
                            li.then_inc(s_t, 1)
                        p.t(mm); t.inc()

                    def corr(te):
                        li = None
                        for ic in range(4):
                            for th in range(2):
                                li = te.matmul(pp[:, (ic * 2 + th) * 512:(ic * 2 + th + 1) * 512],
                                               C["c128"][:], rs_sb[0:1, th * 512:(th + 1) * 512],
                                               start=False, stop=True)
                        li.then_inc(s_t, 1)
                    p.t(corr); t.inc()
                    t_mat = t.n

                    if mat == 0:
                        def evac_g(vv, isg=isg, t_mat=t_mat):
                            vv.wait_ge(s_t, t_mat)
                            li = None
                            for ic in range(4):
                                gic = isg * 4 + ic
                                for th in range(2):
                                    li = vv.tensor_scalar(
                                        g_sb[:, ic * TC + th * 512: ic * TC + th * 512 + 512],
                                        pp[:, (ic * 2 + th) * 512:(ic * 2 + th + 1) * 512],
                                        gsc_sb[:, gic:gic + 1], None, ALU.mult)
                            li.then_inc(s_v, 1)
                        p.v(evac_g); v.inc()
                        v_g = v.n

                        def mk_sig(sc, v_g=v_g):
                            sc.wait_ge(s_v, v_g)
                            li = None
                            for ic in range(4):
                                li = sc.activation(sig[:, ic * TC:(ic + 1) * TC],
                                                   g_sb[:, ic * TC:(ic + 1) * TC],
                                                   AF.Sigmoid, scale=1.0)
                            li.then_inc(s_s, 1)
                        p.s(mk_sig); s.inc()
                        s_sig_cur[0] = s.n
                        p.t(lambda te, v_g=v_g: te.wait_ge(s_v, v_g))
                    else:
                        def evac_u(vv, isg=isg, isl=isl, t_mat=t_mat, s_sig=s_sig_cur[0]):
                            vv.wait_ge(s_t, t_mat)
                            vv.wait_ge(s_s, s_sig)
                            li = None
                            for ic in range(4):
                                gic = isg * 4 + ic
                                iin = isl * 4 + ic
                                for th in range(2):
                                    vv.tensor_scalar(C["ftmp"][:, 0:512],
                                                     pp[:, (ic * 2 + th) * 512:(ic * 2 + th + 1) * 512],
                                                     usc_sb[:, gic:gic + 1], None, ALU.mult)
                                    vv.tensor_tensor(btmp[:, 0:512],
                                                     sig[:, ic * TC + th * 512: ic * TC + th * 512 + 512],
                                                     g_sb[:, ic * TC + th * 512: ic * TC + th * 512 + 512],
                                                     ALU.mult)
                                    li = vv.tensor_tensor(
                                        inter[:, iin * TC + th * 512: iin * TC + th * 512 + 512],
                                        btmp[:, 0:512], C["ftmp"][:, 0:512], ALU.mult)
                            li.then_inc(s_v, 1)
                        p.v(evac_u); v.inc()
                        v_u = v.n
                        p.t(lambda te, v_u=v_u: te.wait_ge(s_v, v_u))

            def rs2_mm(te):
                li = None
                for th in range(2):
                    for iin in range(GRP):
                        li = te.matmul(pp[0:1, th * 512:(th + 1) * 512], C["ones_c_bf"],
                                       inter[:, iin * TC + th * 512: iin * TC + th * 512 + 512],
                                       start=(iin == 0), stop=(iin == GRP - 1))
                li.then_inc(s_t, 1)
            p.t(rs2_mm); t.inc()
            t_rs2 = t.n

            def rs2_cp(vv, t_rs2=t_rs2):
                vv.wait_ge(s_t, t_rs2)
                vv.tensor_copy(rs2_sb[0:1, :], pp[0:1, 0:TC]).then_inc(s_v, 1)
            p.v(rs2_cp); v.inc()
            v_rs2 = v.n
            p.t(lambda te, v_rs2=v_rs2: te.wait_ge(s_v, v_rs2))

            for hg in range(4):
                for ic in range(GRP):
                    nlin = (grp * 4 + hg) * GRP + ic
                    buf = nlin % 2
                    war = t.n - 1 if t.n >= 1 else None

                    def wdl(g, grp=grp, hg=hg, ic=ic, buf=buf, war=war):
                        if war is not None:
                            g.wait_ge(s_t, war)
                        g.dma_start(wdt[:, buf * 512:(buf + 1) * 512],
                                    wd8[(grp * GRP + ic) * 128:(grp * GRP + ic + 1) * 128,
                                        hg * 512:(hg + 1) * 512]).then_inc(s_w, 16)
                    p.g(wdl); w.inc(16)
                    w_tile = w.n

                    def dmm(te, ic=ic, buf=buf, w_tile=w_tile):
                        te.wait_ge(s_w, w_tile)
                        li = None
                        for hc in range(4):
                            for th in range(2):
                                li = te.matmul(pp[:, (hc * 2 + th) * 512:(hc * 2 + th + 1) * 512],
                                               wdt[:, buf * 512 + hc * 128: buf * 512 + hc * 128 + 128],
                                               inter[:, ic * TC + th * 512: ic * TC + th * 512 + 512],
                                               start=(ic == 0), stop=False)
                        li.then_inc(s_t, 1)
                    p.t(dmm); t.inc()

                def dcorr(te):
                    li = None
                    for hc in range(4):
                        for th in range(2):
                            li = te.matmul(pp[:, (hc * 2 + th) * 512:(hc * 2 + th + 1) * 512],
                                           C["c128"][:], rs2_sb[0:1, th * 512:(th + 1) * 512],
                                           start=False, stop=True)
                    li.then_inc(s_t, 1)
                p.t(dcorr); t.inc()
                t_down = t.n

                def xacc(vv, hg=hg, t_down=t_down):
                    vv.wait_ge(s_t, t_down)
                    li = None
                    for hc in range(4):
                        h = hg * 4 + hc
                        for th in range(2):
                            li = vv.scalar_tensor_tensor(
                                xsb[:, h * TC + th * 512: h * TC + th * 512 + 512],
                                pp[:, (hc * 2 + th) * 512:(hc * 2 + th + 1) * 512],
                                dsc_sb[:, h:h + 1],
                                xsb[:, h * TC + th * 512: h * TC + th * 512 + 512],
                                ALU.mult, ALU.add)
                    li.then_inc(s_v, 1)
                p.v(xacc); v.inc()
                v_x = v.n
                p.t(lambda te, v_x=v_x: te.wait_ge(s_v, v_x))

        v_done = v.n

        stv = st.n + 16

        def store(sy, v_done=v_done, stv=stv):
            sy.wait_ge(s_v, v_done)
            xo_ap = bass.AP(xo.tensor, 0, [[TC, 128], [128 * TC, NCH], [1, TC]])
            sy.dma_start(xo_ap, xsb[:]).then_inc(s_st, 16)
            sy.wait_ge(s_st, stv)
        p.sy(store); st.inc(16)

        p.emit(blk)


def build_l0ffn(stages=99):
    nc = bass.Bass("TRN2", target_bir_lowering=False)
    xT = nc.dram_tensor("xT", [D, TC], F32, kind="ExternalInput").ap()
    pg8 = nc.dram_tensor("pg8", [1024, 16384], U8, kind="ExternalInput").ap()
    pu8 = nc.dram_tensor("pu8", [1024, 16384], U8, kind="ExternalInput").ap()
    pd8 = nc.dram_tensor("pd8", [8192, D], U8, kind="ExternalInput").ap()
    gscT = nc.dram_tensor("gscT", [64, 16384], F32, kind="ExternalInput").ap()
    uscT = nc.dram_tensor("uscT", [64, 16384], F32, kind="ExternalInput").ap()
    dscT = nc.dram_tensor("dscT", [512, D], F32, kind="ExternalInput").ap()
    xo = nc.dram_tensor("xo", [D, TC], F32, kind="ExternalOutput").ap()
    with ExitStack() as octx:
        S = _mk_sems_fused(octx, nc)
        blk = octx.enter_context(nc.Block())
        emit_l0ffn(nc, blk, S, xT, pg8, pu8, pd8, gscT, uscT, dscT, xo)
    return nc


def emit_l0ffn(nc, blk, S, xT, pg8, pu8, pd8, gscT, uscT, dscT, xo, stages=99):
    _next_phase()

    I = 16384
    NIC = I // 128
    GRP = 16
    NGRP = NIC // GRP
    NIS = 4
    GELU_S = 1.5957691216057308

    with ExitStack() as ctx:
        xsb = ctx.enter_context(nc.sbuf_tensor(_pn("xsb"), [128, NCH * TC], F32))
        hsb = ctx.enter_context(nc.sbuf_tensor(_pn("hsb"), [128, NCH * TC], BF16))
        inter = ctx.enter_context(nc.sbuf_tensor(_pn("inter"), [128, GRP * TC], BF16))
        g_sb = ctx.enter_context(nc.sbuf_tensor(_pn("g_sb"), [128, 4 * TC], BF16))
        sig = ctx.enter_context(nc.sbuf_tensor(_pn("sig"), [128, 4 * TC], BF16))
        wgb = ctx.enter_context(nc.sbuf_tensor(_pn("wgb"), [128, 8 * 512], BF16))
        wub = ctx.enter_context(nc.sbuf_tensor(_pn("wub"), [128, 8 * 512], BF16))
        wdb = ctx.enter_context(nc.sbuf_tensor(_pn("wdb"), [128, 8 * 512], BF16))
        pk = ctx.enter_context(nc.sbuf_tensor(_pn("pk"), [128, 3 * 512], U8))
        f8a = ctx.enter_context(nc.sbuf_tensor(_pn("f8a"), [128, 256], U16))
        f8b = ctx.enter_context(nc.sbuf_tensor(_pn("f8b"), [128, 256], U16))
        f8lo = ctx.enter_context(nc.sbuf_tensor(_pn("f8lo"), [128, 512], U8))
        f8hi = ctx.enter_context(nc.sbuf_tensor(_pn("f8hi"), [128, 512], U8))
        srg = ctx.enter_context(nc.sbuf_tensor(_pn("srg"), [128, 512], F32))
        sru = ctx.enter_context(nc.sbuf_tensor(_pn("sru"), [128, 512], F32))
        btmp = ctx.enter_context(nc.sbuf_tensor(_pn("btmp"), [128, TC], BF16))
        C = alloc_common(ctx, nc)
        pp = ctx.enter_context(nc.psum_tensor(_pn("pp"), [128, 8 * 512], F32))

        p = P()
        ld, w, v, t, s, st = S["ld"], S["w"], S["v"], S["t"], S["s"], S["st"]
        s_ld, s_w, s_v, s_t, s_s, s_st = ld.h, w.h, v.h, t.h, s.h, st.h
        sems = {"v": v, "t": t, "s": s}

        init_consts_emit(p, sems, C)
        v_c0 = v.n
        p.t(lambda te, v_c0=v_c0: te.wait_ge(s_v, v_c0))
        p.s(lambda sc, v_c0=v_c0: sc.wait_ge(s_v, v_c0))

        def loads(sy):
            xT_ap = bass.AP(xT.tensor, 0, [[TC, 128], [128 * TC, NCH], [1, TC]])
            sy.dma_start(xsb[:], xT_ap).then_inc(s_ld, 16)
        p.sy(loads); ld.inc(16)
        ld_x = ld.n

        p.s(lambda sc: sc.wait_ge(s_ld, ld_x))
        p.v(lambda vv: vv.wait_ge(s_ld, ld_x))
        p.t(lambda te: te.wait_ge(s_ld, ld_x))

        if stages >= 2:
            ln_emit(p, sems, xsb, hsb, pp, C)
            v_ln = v.n
            p.t(lambda te, v_ln=v_ln: te.wait_ge(s_v, v_ln))

        def emit_dequant(srcp8, scT, bc, col0, wb, wb_col0, pk_off, sr_sb, war_t, war_v):
            war_pk = v.n if v.n > 0 else None

            def pk_l(g, bc=bc, col0=col0, pk_off=pk_off, war_pk=war_pk):
                if war_pk is not None:
                    g.wait_ge(s_v, war_pk)
                g.dma_start(pk[:, pk_off:pk_off + 512],
                            srcp8[bc * 128:(bc + 1) * 128, col0:col0 + 512]).then_inc(s_w, 16)
            p.g(pk_l); w.inc(16)
            w_pk = w.n

            def sr_l(sy, bc=bc, col0=col0, scT=scT, sr_sb=sr_sb, war_pk=war_pk):
                if war_pk is not None:
                    sy.wait_ge(s_v, war_pk)
                src = bass.AP(scT.tensor, bc * 8 * scT.shape[1] + col0,
                              [[scT.shape[1], 8], [0, 16], [1, 512]])
                sy.dma_start(sr_sb[:], src).then_inc(s_w, 16)
            p.sy(sr_l); w.inc(16)
            w_sr = w.n

            def simd(vv, pk_off=pk_off, w_pk=w_pk, w_sr=w_sr, wb=wb, wb_col0=wb_col0,
                     sr_sb=sr_sb, war_t=war_t):
                vv.wait_ge(s_w, w_pk)
                vv.wait_ge(s_w, w_sr)
                if war_t is not None:
                    vv.wait_ge(s_t, war_t)
                p16 = pk[:, pk_off:pk_off + 512].bitcast(U16)
                vv.tensor_scalar(f8a[:], p16, 0x0707, 2, ALU.bitwise_and, ALU.logical_shift_left)
                vv.tensor_scalar(f8b[:], p16, 0x0808, 4, ALU.bitwise_and, ALU.logical_shift_left)
                vv.tensor_tensor(f8lo[:].bitcast(U16), f8a[:], f8b[:], ALU.bitwise_or)
                vv.tensor_scalar(f8a[:], p16, 0x7070, 2, ALU.bitwise_and, ALU.logical_shift_right)
                vv.tensor_scalar(f8b[:], p16, 0x8080, None, ALU.bitwise_and)
                vv.tensor_tensor(f8hi[:].bitcast(U16), f8a[:], f8b[:], ALU.bitwise_or)
                vv.scalar_tensor_tensor(wb[:, wb_col0:wb_col0 + 512], f8lo[:].bitcast(FP8),
                                        64.0, sr_sb[:], ALU.mult, ALU.mult)
                vv.scalar_tensor_tensor(wb[:, wb_col0 + 512:wb_col0 + 1024], f8hi[:].bitcast(FP8),
                                        64.0, sr_sb[:], ALU.mult, ALU.mult).then_inc(s_v, 1)
            p.v(simd); v.inc()

        s_sig_cur = [0]
        for grp in range(NGRP if stages >= 3 else 0):
            for isl in range(NIS):
                isg = grp * NIS + isl
                for mat in range(2):
                    srcp8 = pg8 if mat == 0 else pu8
                    scT = gscT if mat == 0 else uscT
                    wb = wgb if mat == 0 else wub
                    sr_sb = srg if mat == 0 else sru
                    for halfr in range(2):
                        for bcl in range(4):
                            bc = halfr * 4 + bcl
                            war_t = t.n if t.n > 0 else None
                            emit_dequant(srcp8, scT, bc, isg * 512, wb, bcl * 1024,
                                         (0 if mat == 0 else 512), sr_sb, war_t, None)
                        v_deq = v.n

                        def mmat(te, wb=wb, v_deq=v_deq, halfr=halfr):
                            te.wait_ge(s_v, v_deq)
                            li = None
                            for kcl in range(8):
                                kc = halfr * 8 + kcl
                                for ic in range(4):
                                    for th in range(2):
                                        li = te.matmul(pp[:, (ic * 2 + th) * 512:(ic * 2 + th + 1) * 512],
                                                       wb[:, kcl * 512 + ic * 128: kcl * 512 + ic * 128 + 128],
                                                       hsb[:, kc * TC + th * 512: kc * TC + th * 512 + 512],
                                                       start=(kc == 0), stop=(kc == NCH - 1))
                            li.then_inc(s_t, 1)
                        p.t(mmat); t.inc()
                    t_mat = t.n

                    if mat == 0:
                        def evac_g(vv, t_mat=t_mat):
                            vv.wait_ge(s_t, t_mat)
                            li = None
                            for ic in range(4):
                                for th in range(2):
                                    li = vv.tensor_copy(
                                        g_sb[:, ic * TC + th * 512: ic * TC + th * 512 + 512],
                                        pp[:, (ic * 2 + th) * 512:(ic * 2 + th + 1) * 512])
                            li.then_inc(s_v, 1)
                        p.v(evac_g); v.inc()
                        v_g = v.n

                        def gelu1(sc, t_mat=t_mat, v_g=v_g):
                            sc.wait_ge(s_t, t_mat)
                            li = None
                            for ic in range(4):
                                for th in range(2):
                                    li = sc.square(sq_sb[:, th * 512:th * 512 + 512],
                                                   pp[:, (ic * 2 + th) * 512:(ic * 2 + th + 1) * 512]) \
                                        if False else None
                            sc.wait_ge(s_v, v_g)
                            for ic in range(4):
                                li = sc.square(sig[:, ic * TC:(ic + 1) * TC], g_sb[:, ic * TC:(ic + 1) * TC])
                            li.then_inc(s_s, 1)
                        p.s(gelu1); s.inc()
                        s_sq2 = s.n

                        def gelu2(vv, s_sq2=s_sq2):
                            vv.wait_ge(s_s, s_sq2)
                            li = None
                            for ic in range(4):
                                vv.tensor_scalar(sig[:, ic * TC:(ic + 1) * TC],
                                                 sig[:, ic * TC:(ic + 1) * TC],
                                                 0.044715, 1.0, ALU.mult, ALU.add)
                                li = vv.tensor_tensor(sig[:, ic * TC:(ic + 1) * TC],
                                                      sig[:, ic * TC:(ic + 1) * TC],
                                                      g_sb[:, ic * TC:(ic + 1) * TC], ALU.mult)
                            li.then_inc(s_v, 1)
                        p.v(gelu2); v.inc()
                        v_z = v.n

                        def gelu3(sc, v_z=v_z):
                            sc.wait_ge(s_v, v_z)
                            li = None
                            for ic in range(4):
                                li = sc.activation(sig[:, ic * TC:(ic + 1) * TC],
                                                   sig[:, ic * TC:(ic + 1) * TC],
                                                   AF.Sigmoid, scale=GELU_S)
                            li.then_inc(s_s, 1)
                        p.s(gelu3); s.inc()
                        s_sig_cur[0] = s.n
                        p.t(lambda te, v_g=v_g: te.wait_ge(s_v, v_g))
                    else:
                        def evac_u(vv, isl=isl, t_mat=t_mat, s_sig=s_sig_cur[0]):
                            vv.wait_ge(s_t, t_mat)
                            vv.wait_ge(s_s, s_sig)
                            li = None
                            for ic in range(4):
                                iin = isl * 4 + ic
                                for th in range(2):
                                    sl = slice(ic * TC + th * 512, ic * TC + th * 512 + 512)
                                    vv.tensor_tensor(btmp[:, 0:512], sig[:, sl], g_sb[:, sl], ALU.mult)
                                    li = vv.tensor_tensor(
                                        inter[:, iin * TC + th * 512: iin * TC + th * 512 + 512],
                                        btmp[:, 0:512],
                                        pp[:, (ic * 2 + th) * 512:(ic * 2 + th + 1) * 512], ALU.mult)
                            li.then_inc(s_v, 1)
                        p.v(evac_u); v.inc()
                        v_u = v.n
                        p.t(lambda te, v_u=v_u: te.wait_ge(s_v, v_u))

            for hg in range(4):
                for halfr in range(2):
                    for bci in range(4):
                        bcl = halfr * 4 + bci
                        bc_glob = grp * 8 + bcl
                        war_t = t.n if t.n > 0 else None
                        emit_dequant(pd8, dscT, bc_glob, hg * 512, wdb, bci * 1024,
                                     1024, srg, war_t, None)
                    v_deqd = v.n

                    def dmms(te, v_deqd=v_deqd, halfr=halfr):
                        te.wait_ge(s_v, v_deqd)
                        li = None
                        for icll in range(8):
                            icl = halfr * 8 + icll
                            for hc in range(4):
                                for th in range(2):
                                    li = te.matmul(pp[:, (hc * 2 + th) * 512:(hc * 2 + th + 1) * 512],
                                                   wdb[:, icll * 512 + hc * 128: icll * 512 + hc * 128 + 128],
                                                   inter[:, icl * TC + th * 512: icl * TC + th * 512 + 512],
                                                   start=(icl == 0), stop=(icl == GRP - 1))
                        li.then_inc(s_t, 1)
                    p.t(dmms); t.inc()
                t_down = t.n

                def xacc(vv, hg=hg, t_down=t_down):
                    vv.wait_ge(s_t, t_down)
                    li = None
                    for hc in range(4):
                        h = hg * 4 + hc
                        for th in range(2):
                            li = vv.scalar_tensor_tensor(
                                xsb[:, h * TC + th * 512: h * TC + th * 512 + 512],
                                pp[:, (hc * 2 + th) * 512:(hc * 2 + th + 1) * 512],
                                1.0,
                                xsb[:, h * TC + th * 512: h * TC + th * 512 + 512],
                                ALU.mult, ALU.add)
                    li.then_inc(s_v, 1)
                p.v(xacc); v.inc()
                v_x = v.n
                p.t(lambda te, v_x=v_x: te.wait_ge(s_v, v_x))

        v_done = v.n

        stv = st.n + 16

        def store(sy, v_done=v_done, stv=stv):
            sy.wait_ge(s_v, v_done)
            xo_ap = bass.AP(xo.tensor, 0, [[TC, 128], [128 * TC, NCH], [1, TC]])
            sy.dma_start(xo_ap, xsb[:]).then_inc(s_st, 16)
            sy.wait_ge(s_st, stv)
        p.sy(store); st.inc(16)

        p.emit(blk)


def ln_tile_emit(p, sems, xt, ht, pp, C, tcw=512, nch=NCH, xcolstride=None, hcolstride=None,
                 xoff=0, hoff=0):
    v, t, s = sems["v"], sems["t"], sems["s"]
    xs = xcolstride
    hs = hcolstride

    def sq(sc):
        li = None
        for c in range(nch):
            li = sc.square(ht[:, hoff + c * hs: hoff + c * hs + tcw],
                           xt[:, xoff + c * xs: xoff + c * xs + tcw])
        li.then_inc(s.h, 1)
    p.s(sq); s.inc()
    s_sq = s.n

    b_s1 = pp[0:1, 4 * 512: 4 * 512 + tcw]
    b_s2 = pp[0:1, 6 * 512: 6 * 512 + tcw]

    def stats(te, s_sq=s_sq):
        te.wait_ge(s.h, s_sq)
        li = None
        for c in range(nch):
            li = te.matmul(b_s1, C["ones_c_f32"], xt[:, xoff + c * xs: xoff + c * xs + tcw],
                           start=(c == 0), stop=(c == nch - 1))
        for c in range(nch):
            li = te.matmul(b_s2, C["ones_c_bf"], ht[:, hoff + c * hs: hoff + c * hs + tcw],
                           start=(c == 0), stop=(c == nch - 1))
        li.then_inc(t.h, 1)
    p.t(stats); t.inc()
    t_stats = t.n

    mrow, rrow, vrow, srow = C["mrow"], C["rrow"], C["vrow"], C["srow"]

    def rows(vv, t_stats=t_stats):
        vv.wait_ge(t.h, t_stats)
        vv.tensor_scalar_mul(mrow[0:1, 0:tcw], b_s1, 1.0 / D)
        vv.tensor_tensor(vrow[0:1, 0:tcw], mrow[0:1, 0:tcw], mrow[0:1, 0:tcw], ALU.mult)
        vv.scalar_tensor_tensor(vrow[0:1, 0:tcw], b_s2, 1.0 / D, vrow[0:1, 0:tcw],
                                ALU.mult, ALU.subtract).then_inc(v.h, 1)
    p.v(rows); v.inc()
    v_rows = v.n

    def dosqrt(sc, v_rows=v_rows):
        sc.wait_ge(v.h, v_rows)
        sc.activation(srow[0:1, 0:tcw], vrow[0:1, 0:tcw], AF.Sqrt,
                      bias=C["eps"], scale=1.0).then_inc(s.h, 1)
    p.s(dosqrt); s.inc()
    s_sqrt = s.n

    def dorecip(vv, s_sqrt=s_sqrt):
        vv.wait_ge(s.h, s_sqrt)
        vv.reciprocal(rrow[0:1, 0:tcw], srow[0:1, 0:tcw]).then_inc(v.h, 1)
    p.v(dorecip); v.inc()
    v_recip = v.n

    b_mb = pp[:, 4 * 512: 4 * 512 + tcw]
    b_rb = pp[:, 6 * 512: 6 * 512 + tcw]

    def bcast(te, v_recip=v_recip):
        te.wait_ge(v.h, v_recip)
        te.matmul(b_mb, C["ones_r_f32"], mrow[0:1, 0:tcw], start=True, stop=True)
        te.matmul(b_rb, C["ones_r_f32"], rrow[0:1, 0:tcw], start=True, stop=True).then_inc(t.h, 1)
    p.t(bcast); t.inc()
    t_bcast = t.n

    def mk_h(vv, t_bcast=t_bcast):
        vv.wait_ge(t.h, t_bcast)
        li = None
        for c in range(nch):
            vv.tensor_tensor(C["ftmp"][:, 0:tcw], xt[:, xoff + c * xs: xoff + c * xs + tcw],
                             b_mb, ALU.subtract)
            li = vv.tensor_tensor(ht[:, hoff + c * hs: hoff + c * hs + tcw],
                                  C["ftmp"][:, 0:tcw], b_rb, ALU.mult)
        li.then_inc(v.h, 1)
    p.v(mk_h); v.inc()
    return v.n


def build_l1attn(stages=99):
    nc = bass.Bass("TRN2", target_bir_lowering=False)
    xT = nc.dram_tensor("xT", [D, TC], F32, kind="ExternalInput").ap()
    xfT = nc.dram_tensor("xfT", [D, 4096], F32, kind="ExternalInput").ap()
    wq = nc.dram_tensor("wq", [D, D], F32, kind="ExternalInput").ap()
    wk = nc.dram_tensor("wk", [D, D], F32, kind="ExternalInput").ap()
    wv = nc.dram_tensor("wv", [D, D], F32, kind="ExternalInput").ap()
    wo = nc.dram_tensor("wo", [D, D], F32, kind="ExternalInput").ap()
    band_in = nc.dram_tensor("band", [128, 4992], BF16, kind="ExternalInput").ap()
    xo = nc.dram_tensor("xo", [D, TC], F32, kind="ExternalOutput").ap()
    with ExitStack() as octx:
        S = _mk_sems_fused(octx, nc)
        blk = octx.enter_context(nc.Block())
        emit_l1attn(nc, blk, S, xT, xfT, wq, wk, wv, wo, band_in, xo)
    return nc


def emit_l1attn(nc, blk, S, xT, xfT, wq, wk, wv, wo, band_in, xo,
                xful_blocks=False, stages=99):
    _next_phase()
    """xful_blocks: xfT is an AllGather output [4*2048, 1024] (rank-major blocks)
    instead of [2048, 4096]."""
    k1d = nc.dram_tensor("k1d_p2", [D, 4096], BF16, kind="Internal").ap()
    v1d = nc.dram_tensor("v1d_p2", [16 * 4096 * 130], BF16, kind="Internal").ap()

    NH = 16

    with ExitStack() as ctx:
        xt = ctx.enter_context(nc.sbuf_tensor(_pn("xt"), [128, NCH * 512], F32))
        harena = ctx.enter_context(nc.sbuf_tensor(_pn("harena"), [128, NCH * 2048], BF16))
        q1 = ctx.enter_context(nc.sbuf_tensor(_pn("q1"), [128, NH * TC], BF16))
        wsl = ctx.enter_context(nc.sbuf_tensor(_pn("wsl"), [128, NCH * 512], BF16))
        wot = ctx.enter_context(nc.sbuf_tensor(_pn("wot"), [128, 2 * NCH * 128], BF16))
        kslab = ctx.enter_context(nc.sbuf_tensor(_pn("kslab"), [128, 4096], BF16))
        vslab = ctx.enter_context(nc.sbuf_tensor(_pn("vslab"), [128, 32 * 130], BF16))
        band = ctx.enter_context(nc.sbuf_tensor(_pn("band_sb"), [128, 4992], BF16))
        Tbuf = ctx.enter_context(nc.sbuf_tensor(_pn("Tbuf"), [128, 2 * 512], F32))
        Pbuf = ctx.enter_context(nc.sbuf_tensor(_pn("Pbuf"), [128, 2 * 512], BF16))

        lrow = ctx.enter_context(nc.sbuf_tensor(_pn("lrow"), [1, 512], F32))
        lb = ctx.enter_context(nc.sbuf_tensor(_pn("lb"), [64, 512], F32))
        m64 = ctx.enter_context(nc.sbuf_tensor(_pn("m64"), [128, 1], F32))
        evb = ctx.enter_context(nc.sbuf_tensor(_pn("evb"), [128, 512], BF16))
        xnew = ctx.enter_context(nc.sbuf_tensor(_pn("xnew"), [128, TC], F32))
        C = alloc_common(ctx, nc, roww=512, ftmpw=512)
        pp = ctx.enter_context(nc.psum_tensor(_pn("pp"), [128, 8 * 512], F32))

        ctx_sb = harena
        p = P()
        ld, w, v, t, s, st, sd = (S[k] for k in ("ld", "w", "v", "t", "s", "st", "d"))
        s_ld, s_w, s_v, s_t, s_s, s_st, s_d = (x.h for x in (ld, w, v, t, s, st, sd))
        sems = {"v": v, "t": t, "s": s}

        init_consts_emit(p, sems, C)

        def ic2(vv):
            vv.memset(m64[:], -64.0).then_inc(s_v, 1)
        p.v(ic2); v.inc()
        v_c0 = v.n
        p.t(lambda te, v_c0=v_c0: te.wait_ge(s_v, v_c0))
        p.s(lambda sc, v_c0=v_c0: sc.wait_ge(s_v, v_c0))

        def ld_band(sy):
            sy.dma_start(band[:], band_in).then_inc(s_ld, 16)
        p.sy(ld_band); ld.inc(16)

        for mt in range(2):
            war_v = v.n
            war_t = t.n
            def xl2(sy, mt=mt, war_v=war_v, war_t=war_t):
                if mt > 0:
                    sy.wait_ge(s_v, war_v)
                    sy.wait_ge(s_t, war_t)
                src = bass.AP(xT.tensor, mt * 512, [[TC, 128], [128 * TC, NCH], [1, 512]])
                sy.dma_start(xt[:], src).then_inc(s_ld, 16)
            p.sy(xl2); ld.inc(16)
            ld_t = ld.n
            p.s(lambda sc, ld_t=ld_t: sc.wait_ge(s_ld, ld_t))
            p.t(lambda te, ld_t=ld_t: te.wait_ge(s_ld, ld_t))
            p.v(lambda vv, ld_t=ld_t: vv.wait_ge(s_ld, ld_t))
            ln_tile_emit(p, sems, xt, harena, pp, C, tcw=512, xcolstride=512,
                         hcolstride=2048, xoff=0, hoff=mt * 512)
        v_hmy = v.n
        for h in range(NH):
            buf = h % 2
            war = t.n if h >= 2 else None
            def wql(g, h=h, buf=buf, war=war):
                if war is not None:
                    g.wait_ge(s_t, war)
                src = bass.AP(wq.tensor, h * 128, [[D, 128], [128 * D, NCH], [1, 128]])
                g.dma_start(wot[:, buf * 2048:(buf + 1) * 2048], src).then_inc(s_w, 16)
            p.g(wql); w.inc(16)
            w_t = w.n

            def qmm(te, h=h, buf=buf, w_t=w_t, v_hmy=v_hmy):
                te.wait_ge(s_w, w_t)
                te.wait_ge(s_v, v_hmy)
                li = None
                for th in range(2):
                    for kc in range(NCH):
                        li = te.matmul(pp[:, th * 512:(th + 1) * 512],
                                       wot[:, buf * 2048 + kc * 128: buf * 2048 + kc * 128 + 128],
                                       harena[:, kc * 2048 + th * 512: kc * 2048 + th * 512 + 512],
                                       start=(kc == 0), stop=(kc == NCH - 1))
                li.then_inc(s_t, 1)
            p.t(qmm); t.inc()
            t_q = t.n

            def qev(vv, h=h, t_q=t_q):
                vv.wait_ge(s_t, t_q)
                li = None
                for th in range(2):
                    li = vv.tensor_copy(q1[:, h * TC + th * 512: h * TC + th * 512 + 512],
                                        pp[:, th * 512:(th + 1) * 512])
                li.then_inc(s_v, 1)
            p.v(qev); v.inc()
            v_q = v.n
            p.t(lambda te, v_q=v_q: te.wait_ge(s_v, v_q))

        def ones_ms(vv):
            vv.memset(harena[:, 16384:16384 + 4160], 1.0).then_inc(s_v, 1)
        p.v(ones_ms); v.inc()
        v_ones = v.n

        def ones_init(sy, v_ones=v_ones):
            sy.wait_ge(s_v, v_ones)
            for h in range(NH):
                dst = bass.AP(v1d.tensor, h * 4096 * 130,
                              [[130, 128], [130 * 128, 32], [1, 130]])
                src = bass.AP(harena, 16384, [[32768, 128], [0, 32], [1, 130]])
                sy.dma_start(dst, src).then_inc(s_d, 16)
        p.sy(ones_init); sd.inc(16 * NH)
        sd_ones = sd.n
        p.s(lambda sc, sd_ones=sd_ones: sc.wait_ge(s_d, sd_ones))
        p.v(lambda vv, sd_ones=sd_ones: vv.wait_ge(s_d, sd_ones))

        for half in range(2):
            for kvt in range(4):
                war_v = v.n
                war_t = t.n
                def xfl(sy, half=half, kvt=kvt, war_v=war_v, war_t=war_t):
                    sy.wait_ge(s_v, war_v)
                    sy.wait_ge(s_t, war_t)
                    kt = half * 4 + kvt
                    if xful_blocks:
                        blkid, j0 = kt // 2, (kt % 2) * 512
                        src = bass.AP(xfT.tensor, blkid * D * TC + j0,
                                      [[TC, 128], [128 * TC, NCH], [1, 512]])
                    else:
                        src = bass.AP(xfT.tensor, kt * 512,
                                      [[4096, 128], [128 * 4096, NCH], [1, 512]])
                    sy.dma_start(xt[:], src).then_inc(s_ld, 16)
                p.sy(xfl); ld.inc(16)
                ld_t = ld.n
                p.s(lambda sc, ld_t=ld_t: sc.wait_ge(s_ld, ld_t))
                p.t(lambda te, ld_t=ld_t: te.wait_ge(s_ld, ld_t))
                p.v(lambda vv, ld_t=ld_t: vv.wait_ge(s_ld, ld_t))
                ln_tile_emit(p, sems, xt, harena, pp, C, tcw=512, xcolstride=512,
                             hcolstride=2048, xoff=0, hoff=kvt * 512)
            v_h = v.n
            for mat in range(2):
                wsrc = wk if mat == 0 else wv
                for ds in range(4):
                    war = t.n
                    def wsl_l(g, ds=ds, wsrc=wsrc, war=war):
                        g.wait_ge(s_t, war)
                        src = bass.AP(wsrc.tensor, ds * 512, [[D, 128], [128 * D, NCH], [1, 512]])
                        g.dma_start(wsl[:], src).then_inc(s_w, 16)
                    p.g(wsl_l); w.inc(16)
                    w_s = w.n
                    p.t(lambda te, w_s=w_s, v_h=v_h: (te.wait_ge(s_w, w_s), te.wait_ge(s_v, v_h)))
                    for kvt in range(4):
                        for blk4 in range(4):
                            def kvmm(te, mat=mat, ds=ds, kvt=kvt, blk4=blk4):
                                li = None
                                for kc in range(NCH):
                                    if mat == 0:
                                        li = te.matmul(pp[:, 0:512],
                                                       wsl[:, kc * 512 + blk4 * 128: kc * 512 + blk4 * 128 + 128],
                                                       harena[:, kc * 2048 + kvt * 512: kc * 2048 + kvt * 512 + 512],
                                                       start=(kc == 0), stop=(kc == NCH - 1))
                                    else:
                                        li = te.matmul(pp[:, 0:512],
                                                       harena[:, kc * 2048 + kvt * 512 + blk4 * 128: kc * 2048 + kvt * 512 + blk4 * 128 + 128],
                                                       wsl[:, kc * 512:(kc + 1) * 512],
                                                       start=(kc == 0), stop=(kc == NCH - 1))
                                li.then_inc(s_t, 1)
                            p.t(kvmm); t.inc()
                            t_kv = t.n

                            def kvev(vv, t_kv=t_kv):
                                vv.wait_ge(s_t, t_kv)
                                vv.tensor_copy(evb[:], pp[:, 0:512]).then_inc(s_v, 1)
                            p.v(kvev); v.inc()
                            v_ev = v.n
                            p.t(lambda te, v_ev=v_ev: te.wait_ge(s_v, v_ev))

                            def kvst(sy, mat=mat, ds=ds, kvt=kvt, blk4=blk4, half=half,
                                     v_ev=v_ev, sd_ones=sd_ones):
                                sy.wait_ge(s_v, v_ev)
                                if mat == 0:
                                    d0 = ds * 512 + blk4 * 128
                                    dst = bass.AP(k1d.tensor, d0 * 4096 + half * 2048 + kvt * 512,
                                                  [[4096, 128], [1, 512]])
                                    sy.dma_start(dst, evb[:]).then_inc(s_d, 16)
                                else:
                                    sy.wait_ge(s_d, sd_ones)
                                    t0 = half * 2048 + kvt * 512 + blk4 * 128
                                    base = (ds * 4) * 4096 * 130 + t0 * 130
                                    for hf in range(2):
                                        dst = bass.AP(v1d.tensor, base + hf * 65,
                                                      [[130, 128], [4096 * 130, 4], [1, 64]])
                                        srcap = bass.AP(evb, hf * 64,
                                                        [[512, 128], [128, 4], [1, 64]])
                                        sy.dma_start(dst, srcap).then_inc(s_d, 16)
                            p.sy(kvst); sd.inc(16 if mat == 0 else 32)
                            sd_cur = sd.n
                            p.v(lambda vv, sd_cur=sd_cur: vv.wait_ge(s_d, sd_cur))

        sd_all = sd.n
        p.sy(lambda sy, sd_all=sd_all: sy.wait_ge(s_d, sd_all))
        ld_b = 16
        for qb in range(2):
            for h in range(NH):
                war = t.n
                def ksl(sy, h=h, war=war):
                    sy.wait_ge(s_t, war)
                    src = bass.AP(k1d.tensor, h * 128 * 4096, [[4096, 128], [1, 4096]])
                    sy.dma_start(kslab[:], src).then_inc(s_ld, 16)
                p.sy(ksl); ld.inc(16)

                def vsl(sy, h=h, war=war):
                    sy.wait_ge(s_t, war)
                    src = bass.AP(v1d.tensor, h * 4096 * 130,
                                  [[130, 128], [130 * 128, 32], [1, 130]])
                    sy.dma_start(vslab[:], src).then_inc(s_ld, 16)
                p.sy(vsl); ld.inc(16)
                ld_slab = ld.n
                p.t(lambda te, ld_slab=ld_slab: te.wait_ge(s_ld, ld_slab))
                p.v(lambda vv, ld_slab=ld_slab: vv.wait_ge(s_ld, ld_slab))

                acts_done = {}
                mask_done = {}
                pv_done = {}
                pend_pv = []
                for tbl in range(32):
                    sb = tbl % 2
                    pb = tbl % 2
                    war_sb = mask_done.get(tbl - 2)
                    def qkmm(te, h=h, qb=qb, tbl=tbl, sb=sb, war_sb=war_sb):
                        if war_sb is not None:
                            te.wait_ge(s_v, war_sb)
                        te.matmul(pp[:, sb * 512:(sb + 1) * 512],
                                  kslab[:, tbl * 128:(tbl + 1) * 128],
                                  q1[:, h * TC + qb * 512: h * TC + qb * 512 + 512],
                                  start=True, stop=True).then_inc(s_t, 1)
                    p.t(qkmm); t.inc()
                    t_qk = t.n
                    if pend_pv:
                        ptbl, pf = pend_pv.pop(0)
                        p.t(pf); t.inc()
                        pv_done[ptbl] = t.n

                    war_tb = acts_done.get(tbl - 2)
                    def pmask(vv, qb=qb, tbl=tbl, sb=sb, pb=pb, t_qk=t_qk, war_tb=war_tb):
                        vv.wait_ge(s_t, t_qk)
                        if war_tb is not None:
                            vv.wait_ge(s_s, war_tb)
                        off = 3968 + qb * 512 - tbl * 128
                        vv.tensor_tensor(Tbuf[:, pb * 512:(pb + 1) * 512],
                                         pp[:, sb * 512:(sb + 1) * 512],
                                         band[:, off:off + 512], ALU.mult).then_inc(s_v, 1)
                    p.v(pmask); v.inc()
                    v_m = v.n
                    mask_done[tbl] = v_m

                    war_pbuf = pv_done.get(tbl - 2)
                    def pexp(sc, pb=pb, v_m=v_m, war_pbuf=war_pbuf):
                        sc.wait_ge(s_v, v_m)
                        if war_pbuf is not None:
                            sc.wait_ge(s_t, war_pbuf)
                        sc.activation(Pbuf[:, pb * 512:(pb + 1) * 512],
                                      Tbuf[:, pb * 512:(pb + 1) * 512],
                                      AF.Exp, bias=m64[:], scale=1.0 / 64).then_inc(s_s, 1)
                    p.s(pexp); s.inc()
                    s_e = s.n
                    acts_done[tbl] = s_e

                    def pv(te, tbl=tbl, pb=pb, s_e=s_e):
                        te.wait_ge(s_s, s_e)
                        te.matmul(pp[:, 2 * 512: 2 * 512 + 512][0:65, :],
                                  vslab[:, tbl * 130: tbl * 130 + 65],
                                  Pbuf[:, pb * 512:(pb + 1) * 512],
                                  start=(tbl == 0), stop=(tbl == 31))
                        te.matmul(pp[:, 3 * 512: 3 * 512 + 512][0:65, :],
                                  vslab[:, tbl * 130 + 65: tbl * 130 + 130],
                                  Pbuf[:, pb * 512:(pb + 1) * 512],
                                  start=(tbl == 0), stop=(tbl == 31)).then_inc(s_t, 1)
                    pend_pv.append((tbl, pv))
                for ptbl, pf in pend_pv:
                    p.t(pf); t.inc()
                    pv_done[ptbl] = t.n
                pend_pv = []

                t_pv = t.n

                def lsetup(vv, t_pv=t_pv):
                    vv.wait_ge(s_t, t_pv)
                    vv.tensor_copy(lrow[0:1, :], pp[64:65, 2 * 512: 2 * 512 + 512])
                    vv.reciprocal(lrow[0:1, :], lrow[0:1, :]).then_inc(s_v, 1)
                p.v(lsetup); v.inc()
                v_l = v.n

                def lbc(te, v_l=v_l):
                    te.wait_ge(s_v, v_l)
                    te.matmul(pp[0:64, 4 * 512: 4 * 512 + 512], C["ones_r_f32"][0:1, 0:64],
                              lrow[0:1, :], start=True, stop=True).then_inc(s_t, 1)
                p.t(lbc); t.inc()
                t_lb = t.n

                def cdiv(vv, h=h, qb=qb, t_lb=t_lb):
                    vv.wait_ge(s_t, t_lb)
                    vv.tensor_copy(lb[:], pp[0:64, 4 * 512: 4 * 512 + 512])
                    vv.scalar_tensor_tensor(
                        ctx_sb[0:64, h * TC + qb * 512: h * TC + qb * 512 + 512],
                        pp[0:64, 2 * 512: 2 * 512 + 512], 1.0, lb[:], ALU.mult, ALU.mult)
                    vv.scalar_tensor_tensor(
                        ctx_sb[64:128, h * TC + qb * 512: h * TC + qb * 512 + 512],
                        pp[0:64, 3 * 512: 3 * 512 + 512], 1.0, lb[:], ALU.mult, ALU.mult
                    ).then_inc(s_v, 1)
                p.v(cdiv); v.inc()
                v_c = v.n
                p.t(lambda te, v_c=v_c: te.wait_ge(s_v, v_c))

        for oc in range(NCH):
            buf = oc % 2
            war = t.n if oc >= 2 else None
            def wol(g, oc=oc, buf=buf, war=war):
                if war is not None:
                    g.wait_ge(s_t, war)
                src = bass.AP(wo.tensor, oc * 128, [[D, 128], [128 * D, NCH], [1, 128]])
                g.dma_start(wot[:, buf * 2048:(buf + 1) * 2048], src).then_inc(s_w, 16)
            p.g(wol); w.inc(16)
            w_t = w.n

            st_prev = st.n
            def xl4(sy, oc=oc, st_prev=st_prev):
                if oc > 0:
                    sy.wait_ge(s_st, st_prev)
                src = bass.AP(xT.tensor, oc * 128 * TC, [[TC, 128], [1, TC]])
                sy.dma_start(xnew[:], src).then_inc(s_ld, 16)
            p.sy(xl4); ld.inc(16)
            ld_xc = ld.n

            def omm(te, oc=oc, buf=buf, w_t=w_t):
                te.wait_ge(s_w, w_t)
                li = None
                for th in range(2):
                    for ic in range(NCH):
                        li = te.matmul(pp[:, (5 + th) * 512:(6 + th) * 512],
                                       wot[:, buf * 2048 + ic * 128: buf * 2048 + ic * 128 + 128],
                                       ctx_sb[:, ic * TC + th * 512: ic * TC + th * 512 + 512],
                                       start=(ic == 0), stop=(ic == NCH - 1))
                li.then_inc(s_t, 1)
            p.t(omm); t.inc()
            t_o = t.n

            def oev(vv, oc=oc, t_o=t_o, ld_xc=ld_xc):
                vv.wait_ge(s_t, t_o)
                vv.wait_ge(s_ld, ld_xc)
                li = None
                for th in range(2):
                    li = vv.tensor_tensor(xnew[:, th * 512:(th + 1) * 512],
                                          xnew[:, th * 512:(th + 1) * 512],
                                          pp[:, (5 + th) * 512:(6 + th) * 512], ALU.add)
                li.then_inc(s_v, 1)
            p.v(oev); v.inc()
            v_o = v.n
            p.t(lambda te, v_o=v_o: te.wait_ge(s_v, v_o))

            def ost(sy, oc=oc, v_o=v_o):
                sy.wait_ge(s_v, v_o)
                dst = bass.AP(xo.tensor, oc * 128 * TC, [[TC, 128], [1, TC]])
                sy.dma_start(dst, xnew[:]).then_inc(s_st, 16)
            p.sy(ost); st.inc(16)

        st_all = st.n
        p.sy(lambda sy, st_all=st_all: sy.wait_ge(s_st, st_all))

        p.emit(blk)


def build_l0attn(stages=99):
    nc = bass.Bass("TRN2", target_bir_lowering=False)
    xT = nc.dram_tensor("xT", [D, TC], F32, kind="ExternalInput").ap()
    xfT = nc.dram_tensor("xfT", [D, 4096], F32, kind="ExternalInput").ap()
    wq = nc.dram_tensor("wq", [D, D], F32, kind="ExternalInput").ap()
    wk = nc.dram_tensor("wk", [D, 512], F32, kind="ExternalInput").ap()
    wv = nc.dram_tensor("wv", [D, 512], F32, kind="ExternalInput").ap()
    wo = nc.dram_tensor("wo", [D, D], F32, kind="ExternalInput").ap()
    bo_in = nc.dram_tensor("bo", [128, 16], F32, kind="ExternalInput").ap()
    qbits_in = nc.dram_tensor("qbits", [12, 32 * TC], BF16, kind="ExternalInput").ap()
    cosq_in = nc.dram_tensor("cosq", [32, TC], BF16, kind="ExternalInput").ap()
    sinq_in = nc.dram_tensor("sinq", [32, TC], BF16, kind="ExternalInput").ap()
    cosk_in = nc.dram_tensor("cosk", [32, 4096], BF16, kind="ExternalInput").ap()
    sink_in = nc.dram_tensor("sink", [32, 4096], BF16, kind="ExternalInput").ap()
    tbias_in = nc.dram_tensor("tbias", [128, 1024], F32, kind="ExternalInput").ap()
    band_in = nc.dram_tensor("band", [128, 4992], BF16, kind="ExternalInput").ap()
    xo = nc.dram_tensor("xo", [D, TC], F32, kind="ExternalOutput").ap()
    with ExitStack() as octx:
        S = _mk_sems_fused(octx, nc)
        blk = octx.enter_context(nc.Block())
        emit_l0attn(nc, blk, S, xT, xfT, wq, wk, wv, wo, bo_in, qbits_in, cosq_in,
                    sinq_in, cosk_in, sink_in, tbias_in, band_in, xo)
    return nc


def emit_l0attn(nc, blk, S, xT, xfT, wq, wk, wv, wo, bo_in, qbits_in, cosq_in,
                sinq_in, cosk_in, sink_in, tbias_in, band_in, xo, stages=99):
    _next_phase()
    NH = 32
    TT = 256

    with ExitStack() as ctx:
        xt = ctx.enter_context(nc.sbuf_tensor(_pn("xt"), [128, NCH * TT], F32))
        ctxa = ctx.enter_context(nc.sbuf_tensor(_pn("ctxa"), [128, 16384], BF16))
        htile = ctx.enter_context(nc.sbuf_tensor(_pn("htile"), [128, NCH * TT], BF16))
        q1 = ctx.enter_context(nc.sbuf_tensor(_pn("q1"), [76, 16 * TC], BF16))
        k1h = ctx.enter_context(nc.sbuf_tensor(_pn("k1h"), [76, 4 * 4096], BF16))
        v1h = ctx.enter_context(nc.sbuf_tensor(_pn("v1h"), [128, 32 * 260], BF16))
        wkv = ctx.enter_context(nc.sbuf_tensor(_pn("wkv"), [128, NCH * 512], BF16))
        wot = ctx.enter_context(nc.sbuf_tensor(_pn("wot"), [128, 2 * 2048], BF16))
        Tbuf = ctx.enter_context(nc.sbuf_tensor(_pn("Tbuf"), [128, 512], F32))
        Pbuf = ctx.enter_context(nc.sbuf_tensor(_pn("Pbuf"), [128, 2 * 512], BF16))
        P2 = ctx.enter_context(nc.sbuf_tensor(_pn("P2"), [128, 2 * 512], BF16))
        band = ctx.enter_context(nc.sbuf_tensor(_pn("band_sb"), [128, 4992], BF16))
        tbias = ctx.enter_context(nc.sbuf_tensor(_pn("tbias_sb"), [128, 1024], F32))
        bo_sb = ctx.enter_context(nc.sbuf_tensor(_pn("bo_sb"), [128, 16], F32))
        m30 = ctx.enter_context(nc.sbuf_tensor(_pn("m30"), [128, 1], F32))
        lrow = ctx.enter_context(nc.sbuf_tensor(_pn("lrow"), [1, 512], F32))
        lb = ctx.enter_context(nc.sbuf_tensor(_pn("lb"), [64, 512], F32))
        xnew = ctx.enter_context(nc.sbuf_tensor(_pn("xnew"), [128, TC], F32))
        C = alloc_common(ctx, nc, roww=512, ftmpw=512)
        pp = ctx.enter_context(nc.psum_tensor(_pn("pp"), [128, 8 * 512], F32))

        p = P()
        ld, w, v, t, s, st = (S[k] for k in ("ld", "w", "v", "t", "s", "st"))
        cs = S["cs"]
        s_ld, s_w, s_v, s_t, s_s, s_st = (x.h for x in (ld, w, v, t, s, st))
        s_cs = cs.h
        sems = {"v": v, "t": t, "s": s}

        init_consts_emit(p, sems, C)

        def ic2(vv):
            vv.memset(m30[:], -30.0).then_inc(s_v, 1)
        p.v(ic2); v.inc()
        v_c0 = v.n
        p.t(lambda te, v_c0=v_c0: te.wait_ge(s_v, v_c0))
        p.s(lambda sc, v_c0=v_c0: sc.wait_ge(s_v, v_c0))

        def lconsts(sy):
            sy.dma_start(band[:], band_in).then_inc(s_ld, 16)
            sy.dma_start(tbias[:], tbias_in).then_inc(s_ld, 16)
            sy.dma_start(bo_sb[:], bo_in).then_inc(s_ld, 16)
        p.sy(lconsts); ld.inc(16 * 3)
        ld_c = ld.n
        p.t(lambda te, ld_c=ld_c: te.wait_ge(s_ld, ld_c))
        p.v(lambda vv, ld_c=ld_c: vv.wait_ge(s_ld, ld_c))
        p.s(lambda sc, ld_c=ld_c: sc.wait_ge(s_ld, ld_c))

        def rope_batch(colbase, buf, cos0, cos32, sin0, sin32, width):
            A = buf[0:32, colbase:colbase + width]
            B = buf[32:64, colbase:colbase + width]
            xtv = xt[:].bitcast(BF16)
            S1 = xtv[0:32, 0:width]
            S2 = xtv[32:64, 0:width]
            S3 = xtv[32:64, 4096:4096 + width]
            S4 = xtv[0:32, 4096:4096 + width]

            def rp(vv, A=A, B=B, S1=S1, S2=S2, S3=S3, S4=S4,
                   cos0=cos0, cos32=cos32, sin0=sin0, sin32=sin32):
                vv.tensor_tensor(S1, A, cos0, ALU.mult)
                vv.tensor_tensor(S2, A, sin0, ALU.mult)
                vv.tensor_tensor(S3, B, cos32, ALU.mult)
                vv.scalar_tensor_tensor(S4, B, -1.0, sin32, ALU.mult, ALU.mult)
                vv.tensor_tensor(A, S1, S4, ALU.add)
                vv.tensor_tensor(B, S3, S2, ALU.add).then_inc(s_v, 1)
            p.v(rp); v.inc()

        for hh in range(2):
            for mt in range(4):
                war_v = v.n
                war_t = t.n
                def xl(sy, mt=mt, war_v=war_v, war_t=war_t):
                    sy.wait_ge(s_v, war_v)
                    sy.wait_ge(s_t, war_t)
                    src = bass.AP(xT.tensor, mt * TT, [[TC, 128], [128 * TC, NCH], [1, TT]])
                    sy.dma_start(xt[:], src).then_inc(s_ld, 16)
                p.sy(xl); ld.inc(16)
                ld_t = ld.n
                p.s(lambda sc, ld_t=ld_t: sc.wait_ge(s_ld, ld_t))
                p.t(lambda te, ld_t=ld_t: te.wait_ge(s_ld, ld_t))
                p.v(lambda vv, ld_t=ld_t: vv.wait_ge(s_ld, ld_t))
                ln_tile_emit(p, sems, xt, htile, pp, C, tcw=TT, xcolstride=TT,
                             hcolstride=TT, xoff=0, hoff=0)
                v_h = v.n
                for hl in range(16):
                    h = hh * 16 + hl
                    buf = hl % 2
                    war = t.n if t.n >= 1 else None
                    def wql(g, h=h, buf=buf, war=war):
                        if war is not None:
                            g.wait_ge(s_t, war)
                        src = bass.AP(wq.tensor, h * 64, [[D, 128], [128 * D, NCH], [1, 64]])
                        g.dma_start(wot[:, buf * 2048: buf * 2048 + 1024], src).then_inc(s_w, 16)
                    p.g(wql); w.inc(16)
                    w_t = w.n

                    def qmm(te, hl=hl, buf=buf, w_t=w_t, v_h=v_h):
                        te.wait_ge(s_w, w_t)
                        te.wait_ge(s_v, v_h)
                        li = None
                        for kc in range(NCH):
                            li = te.matmul(pp[0:64, 0:TT],
                                           wot[:, buf * 2048 + kc * 64: buf * 2048 + kc * 64 + 64],
                                           htile[:, kc * TT: kc * TT + TT],
                                           start=(kc == 0), stop=(kc == NCH - 1))
                        li.then_inc(s_t, 1)
                    p.t(qmm); t.inc()
                    t_q = t.n

                    def qev(vv, hl=hl, mt=mt, t_q=t_q):
                        vv.wait_ge(s_t, t_q)
                        vv.tensor_copy(q1[0:64, hl * TC + mt * TT: hl * TC + mt * TT + TT],
                                       pp[0:64, 0:TT]).then_inc(s_v, 1)
                    p.v(qev); v.inc()
                    v_q = v.n
                    p.t(lambda te, v_q=v_q: te.wait_ge(s_v, v_q))
            war_qr = t.n
            def csql(sy, war_qr=war_qr):
                sy.wait_ge(s_t, war_qr)
                srcc = bass.AP(cosq_in.tensor, 0, [[0, 4], [TC, 32], [1, TC]])
                srcs = bass.AP(sinq_in.tensor, 0, [[0, 4], [TC, 32], [1, TC]])
                sy.dma_start(htile[:, 0:TC], srcc).then_inc(s_cs, 16)
                sy.dma_start(htile[:, TC:2 * TC], srcs).then_inc(s_cs, 16)
            p.sy(csql); cs.inc(32)
            ld_cs = cs.n
            p.v(lambda vv, ld_cs=ld_cs: vv.wait_ge(s_cs, ld_cs))
            RL = NCH * TT
            for b4 in range(4):
                cos0 = bass.AP(htile, 0 * RL, [[RL, 32], [0, 4], [1, TC]])
                cos32 = bass.AP(htile, 32 * RL, [[RL, 32], [0, 4], [1, TC]])
                sin0 = bass.AP(htile, 0 * RL + TC, [[RL, 32], [0, 4], [1, TC]])
                sin32 = bass.AP(htile, 32 * RL + TC, [[RL, 32], [0, 4], [1, TC]])
                rope_batch(b4 * 4 * TC, q1, cos0, cos32, sin0, sin32, 4 * TC)
            def qbl(sy, hh=hh):
                sy.dma_start(q1[64:76, :], qbits_in[:, hh * 16 * TC:(hh + 1) * 16 * TC]).then_inc(s_ld, 16)
            p.sy(qbl); ld.inc(16)
            ld_qb = ld.n
            p.t(lambda te, ld_qb=ld_qb: te.wait_ge(s_ld, ld_qb))

            war = t.n
            def wkvl(g, hh=hh, war=war):
                g.wait_ge(s_t, war)
                srck = bass.AP(wk.tensor, hh * 256, [[512, 128], [128 * 512, NCH], [1, 256]])
                dstk = bass.AP(wkv, 0, [[NCH * 512, 128], [512, NCH], [1, 256]])
                g.dma_start(dstk, srck).then_inc(s_w, 16)
                srcv = bass.AP(wv.tensor, hh * 256, [[512, 128], [128 * 512, NCH], [1, 256]])
                dstv = bass.AP(wkv, 256, [[NCH * 512, 128], [512, NCH], [1, 256]])
                g.dma_start(dstv, srcv).then_inc(s_w, 16)
            p.g(wkvl); w.inc(32)
            w_kv = w.n
            p.t(lambda te, w_kv=w_kv: te.wait_ge(s_w, w_kv))

            war_vv = t.n
            def vset(vv, war_vv=war_vv):
                vv.wait_ge(s_t, war_vv)
                vv.memset(k1h[64:76, :], 1.0)
                vv.memset(v1h[:], 1.0).then_inc(s_v, 1)
            p.v(vset); v.inc()
            v_set = v.n
            p.t(lambda te, v_set=v_set: te.wait_ge(s_v, v_set))

            for kt in range(16):
                war_v = v.n
                war_t = t.n
                def xfl(sy, kt=kt, war_v=war_v, war_t=war_t):
                    sy.wait_ge(s_v, war_v)
                    sy.wait_ge(s_t, war_t)
                    src = bass.AP(xfT.tensor, kt * TT, [[4096, 128], [128 * 4096, NCH], [1, TT]])
                    sy.dma_start(xt[:], src).then_inc(s_ld, 16)
                p.sy(xfl); ld.inc(16)
                ld_t = ld.n
                p.s(lambda sc, ld_t=ld_t: sc.wait_ge(s_ld, ld_t))
                p.t(lambda te, ld_t=ld_t: te.wait_ge(s_ld, ld_t))
                p.v(lambda vv, ld_t=ld_t: vv.wait_ge(s_ld, ld_t))
                ln_tile_emit(p, sems, xt, htile, pp, C, tcw=TT, xcolstride=TT,
                             hcolstride=TT, xoff=0, hoff=0)
                v_h = v.n
                def kmm(te, v_h=v_h):
                    te.wait_ge(s_v, v_h)
                    li = None
                    for ch in range(2):
                        for kc in range(NCH):
                            li = te.matmul(pp[:, ch * 512: ch * 512 + TT],
                                           wkv[:, kc * 512 + ch * 128: kc * 512 + ch * 128 + 128],
                                           htile[:, kc * TT: kc * TT + TT],
                                           start=(kc == 0), stop=(kc == NCH - 1))
                    for tb in range(2):
                        for kc in range(NCH):
                            li = te.matmul(pp[:, (2 + tb) * 512: (2 + tb) * 512 + 256],
                                           htile[:, kc * TT + tb * 128: kc * TT + tb * 128 + 128],
                                           wkv[:, kc * 512 + 256: kc * 512 + 512],
                                           start=(kc == 0), stop=(kc == NCH - 1))
                    li.then_inc(s_t, 1)
                p.t(kmm); t.inc()
                t_kv = t.n

                def kvev(vv, kt=kt, t_kv=t_kv):
                    vv.wait_ge(s_t, t_kv)
                    li = None
                    for ch in range(2):
                        for gh in range(2):
                            g_loc = ch * 2 + gh
                            li = vv.tensor_copy(
                                k1h[0:64, g_loc * 4096 + kt * TT: g_loc * 4096 + kt * TT + TT],
                                pp[gh * 64:(gh + 1) * 64, ch * 512: ch * 512 + TT])
                    for tb in range(2):
                        tt_idx = kt * 2 + tb
                        for g_loc in range(4):
                            li = vv.tensor_copy(
                                v1h[:, tt_idx * 260 + g_loc * 65: tt_idx * 260 + g_loc * 65 + 64],
                                pp[:, (2 + tb) * 512 + g_loc * 64: (2 + tb) * 512 + g_loc * 64 + 64])
                    li.then_inc(s_v, 1)
                p.v(kvev); v.inc()
                v_ev = v.n
                p.t(lambda te, v_ev=v_ev: te.wait_ge(s_v, v_ev))
            RL = NCH * TT
            for th2 in range(2):
                war_kr = t.n
                war_kv = v.n
                def cskl(sy, th2=th2, war_kr=war_kr, war_kv=war_kv):
                    sy.wait_ge(s_t, war_kr)
                    sy.wait_ge(s_v, war_kv)
                    srcc = bass.AP(cosk_in.tensor, th2 * 2048, [[0, 4], [4096, 32], [1, 2048]])
                    srcs = bass.AP(sink_in.tensor, th2 * 2048, [[0, 4], [4096, 32], [1, 2048]])
                    sy.dma_start(htile[:, 0:2048], srcc).then_inc(s_cs, 16)
                    sy.dma_start(htile[:, 2048:4096], srcs).then_inc(s_cs, 16)
                p.sy(cskl); cs.inc(32)
                ld_cs2 = cs.n
                p.v(lambda vv, ld_cs2=ld_cs2: vv.wait_ge(s_cs, ld_cs2))
                for g_loc in range(4):
                    rope_batch(g_loc * 4096 + th2 * 2048, k1h,
                               bass.AP(htile, 0 * RL, [[RL, 32], [1, 2048]]),
                               bass.AP(htile, 32 * RL, [[RL, 32], [1, 2048]]),
                               bass.AP(htile, 0 * RL + 2048, [[RL, 32], [1, 2048]]),
                               bass.AP(htile, 32 * RL + 2048, [[RL, 32], [1, 2048]]), 2048)
            v_krope = v.n
            p.t(lambda te, v_krope=v_krope: te.wait_ge(s_v, v_krope))

            for qb in range(2):
                for hl in range(16):
                    h = hh * 16 + hl
                    g_loc = hl // 4
                    acts_done = {}
                    mask_done = {}
                    pv_done = {}
                    pend_pv = []
                    for tbl in range(32):
                        sb = tbl % 2
                        war_s = acts_done.get(tbl - 2)
                        def qkmm(te, hl=hl, qb=qb, tbl=tbl, sb=sb, g_loc=g_loc, war_s=war_s):
                            if war_s is not None:
                                te.wait_ge(s_s, war_s)
                            te.matmul(pp[:, sb * 512:(sb + 1) * 512],
                                      k1h[0:76, g_loc * 4096 + tbl * 128: g_loc * 4096 + tbl * 128 + 128],
                                      q1[0:76, hl * TC + qb * 512: hl * TC + qb * 512 + 512],
                                      start=True, stop=True).then_inc(s_t, 1)
                        p.t(qkmm); t.inc()
                        t_qk = t.n
                        if pend_pv:
                            ptbl, pf = pend_pv.pop(0)
                            p.t(pf); t.inc()
                            pv_done[ptbl] = t.n

                        pb = tbl % 2
                        war_pb = mask_done.get(tbl - 2)
                        def acts(sc, h=h, tbl=tbl, sb=sb, pb=pb, t_qk=t_qk, war_pb=war_pb):
                            sc.wait_ge(s_t, t_qk)
                            if war_pb is not None:
                                sc.wait_ge(s_v, war_pb)
                            sc.activation(Tbuf[:], pp[:, sb * 512:(sb + 1) * 512],
                                          AF.Tanh, bias=tbias[:, h * 32 + tbl: h * 32 + tbl + 1],
                                          scale=1.0 / 240)
                            sc.activation(Pbuf[:, pb * 512:(pb + 1) * 512], Tbuf[:], AF.Exp,
                                          bias=m30[:], scale=30.0).then_inc(s_s, 1)
                        p.s(acts); s.inc()
                        s_e = s.n
                        acts_done[tbl] = s_e

                        war_p2 = pv_done.get(tbl - 2)
                        def pmask(vv, qb=qb, tbl=tbl, pb=pb, s_e=s_e, war_p2=war_p2):
                            vv.wait_ge(s_s, s_e)
                            if war_p2 is not None:
                                vv.wait_ge(s_t, war_p2)
                            off = 3968 + qb * 512 - tbl * 128
                            vv.tensor_tensor(P2[:, pb * 512:(pb + 1) * 512],
                                             Pbuf[:, pb * 512:(pb + 1) * 512],
                                             band[:, off:off + 512],
                                             ALU.mult).then_inc(s_v, 1)
                        p.v(pmask); v.inc()
                        v_m = v.n
                        mask_done[tbl] = v_m

                        def pv(te, tbl=tbl, hl=hl, pb=pb, v_m=v_m, g_loc=g_loc):
                            te.wait_ge(s_v, v_m)
                            te.matmul(pp[0:65, (2 + hl % 2) * 512:(2 + hl % 2) * 512 + 512],
                                      v1h[:, tbl * 260 + g_loc * 65: tbl * 260 + g_loc * 65 + 65],
                                      P2[:, pb * 512:(pb + 1) * 512],
                                      start=(tbl == 0), stop=(tbl == 31)).then_inc(s_t, 1)
                        pend_pv.append((tbl, pv))
                    for ptbl, pf in pend_pv:
                        p.t(pf); t.inc()
                        pv_done[ptbl] = t.n
                    pend_pv = []

                    t_pv = t.n
                    cb = (2 + hl % 2) * 512

                    def lsetup(vv, t_pv=t_pv, cb=cb):
                        vv.wait_ge(s_t, t_pv)
                        vv.tensor_copy(lrow[0:1, :], pp[64:65, cb:cb + 512])
                        vv.reciprocal(lrow[0:1, :], lrow[0:1, :]).then_inc(s_v, 1)
                    p.v(lsetup); v.inc()
                    v_l = v.n

                    def lbc(te, v_l=v_l):
                        te.wait_ge(s_v, v_l)
                        te.matmul(pp[0:64, 4 * 512: 4 * 512 + 512], C["ones_r_f32"][0:1, 0:64],
                                  lrow[0:1, :], start=True, stop=True).then_inc(s_t, 1)
                    p.t(lbc); t.inc()
                    t_lb = t.n

                    def cdiv(vv, h=h, qb=qb, cb=cb, t_lb=t_lb):
                        vv.wait_ge(s_t, t_lb)
                        vv.tensor_copy(lb[:], pp[0:64, 4 * 512: 4 * 512 + 512])
                        vv.scalar_tensor_tensor(
                            ctxa[(h % 2) * 64:(h % 2) * 64 + 64,
                                 (h // 2) * TC + qb * 512:(h // 2) * TC + qb * 512 + 512],
                            pp[0:64, cb:cb + 512], 1.0, lb[:], ALU.mult, ALU.mult
                        ).then_inc(s_v, 1)
                    p.v(cdiv); v.inc()
                    v_c = v.n
                    p.t(lambda te, v_c=v_c: te.wait_ge(s_v, v_c))

        for oc in range(NCH):
            buf = oc % 2
            war = t.n if oc >= 2 else None
            def wol(g, oc=oc, buf=buf, war=war):
                if war is not None:
                    g.wait_ge(s_t, war)
                src = bass.AP(wo.tensor, oc * 128, [[D, 128], [128 * D, NCH], [1, 128]])
                g.dma_start(wot[:, buf * 2048:(buf + 1) * 2048], src).then_inc(s_w, 16)
            p.g(wol); w.inc(16)
            w_t = w.n

            st_prev = st.n
            def xl4(sy, oc=oc, st_prev=st_prev):
                if oc > 0:
                    sy.wait_ge(s_st, st_prev)
                src = bass.AP(xT.tensor, oc * 128 * TC, [[TC, 128], [1, TC]])
                sy.dma_start(xnew[:], src).then_inc(s_ld, 16)
            p.sy(xl4); ld.inc(16)
            ld_xc = ld.n

            def omm(te, oc=oc, buf=buf, w_t=w_t):
                te.wait_ge(s_w, w_t)
                li = None
                for th in range(2):
                    for ic in range(NCH):
                        li = te.matmul(pp[:, (5 + th) * 512:(6 + th) * 512],
                                       wot[:, buf * 2048 + ic * 128: buf * 2048 + ic * 128 + 128],
                                       ctxa[:, ic * TC + th * 512: ic * TC + th * 512 + 512],
                                       start=(ic == 0), stop=(ic == NCH - 1))
                li.then_inc(s_t, 1)
            p.t(omm); t.inc()
            t_o = t.n

            def oev(vv, oc=oc, t_o=t_o, ld_xc=ld_xc):
                vv.wait_ge(s_t, t_o)
                vv.wait_ge(s_ld, ld_xc)
                li = None
                for th in range(2):
                    vv.tensor_scalar(C["ftmp"][:, 0:512], pp[:, (5 + th) * 512:(6 + th) * 512],
                                     bo_sb[:, oc:oc + 1], None, ALU.add)
                    li = vv.scalar_tensor_tensor(xnew[:, th * 512:(th + 1) * 512],
                                                 C["ftmp"][:, 0:512], 1.0,
                                                 xnew[:, th * 512:(th + 1) * 512],
                                                 ALU.mult, ALU.add)
                li.then_inc(s_v, 1)
            p.v(oev); v.inc()
            v_o = v.n
            p.t(lambda te, v_o=v_o: te.wait_ge(s_v, v_o))

            def ost(sy, oc=oc, v_o=v_o):
                sy.wait_ge(s_v, v_o)
                dst = bass.AP(xo.tensor, oc * 128 * TC, [[TC, 128], [1, TC]])
                sy.dma_start(dst, xnew[:]).then_inc(s_st, 16)
            p.sy(ost); st.inc(16)

        st_all = st.n
        p.sy(lambda sy, st_all=st_all: sy.wait_ge(s_st, st_all))

        p.emit(blk)


def _mk_sems_fused(ctx, nc):
    S = {}
    for name in ("ld", "w", "v", "t", "s", "st", "cs", "d", "ag"):
        h = ctx.enter_context(nc.semaphore("sF_" + name))
        S[name] = Sem(h)
    return S


def barrier_emit(blk, S):
    vals = [(sem.h, sem.n) for sem in S.values() if sem.n > 0]
    for eng in ("sync", "gpsimd", "vector", "scalar", "tensor"):
        def b(e, vals=vals):
            for h, n in vals:
                e.wait_ge(h, n)
        getattr(blk, eng)(b)


def build_fused():
    nc = bass.Bass("TRN2", target_bir_lowering=False)
    TT = 256
    xT = nc.dram_tensor("xT", [D, TC], F32, kind="ExternalInput").ap()
    xfT = nc.dram_tensor("xfT", [D, 4096], F32, kind="ExternalInput").ap()
    wq0 = nc.dram_tensor("wq0", [D, D], F32, kind="ExternalInput").ap()
    wk0 = nc.dram_tensor("wk0", [D, 512], F32, kind="ExternalInput").ap()
    wv0 = nc.dram_tensor("wv0", [D, 512], F32, kind="ExternalInput").ap()
    wo0 = nc.dram_tensor("wo0", [D, D], F32, kind="ExternalInput").ap()
    bo0 = nc.dram_tensor("bo0", [128, 16], F32, kind="ExternalInput").ap()
    qbits = nc.dram_tensor("qbits", [12, 32 * TC], BF16, kind="ExternalInput").ap()
    cosq = nc.dram_tensor("cosq", [32, TC], BF16, kind="ExternalInput").ap()
    sinq = nc.dram_tensor("sinq", [32, TC], BF16, kind="ExternalInput").ap()
    cosk = nc.dram_tensor("cosk", [32, 4096], BF16, kind="ExternalInput").ap()
    sink = nc.dram_tensor("sink", [32, 4096], BF16, kind="ExternalInput").ap()
    tbias = nc.dram_tensor("tbias", [128, 1024], F32, kind="ExternalInput").ap()
    band = nc.dram_tensor("band", [128, 4992], BF16, kind="ExternalInput").ap()
    pg8 = nc.dram_tensor("pg8", [1024, 16384], U8, kind="ExternalInput").ap()
    pu8 = nc.dram_tensor("pu8", [1024, 16384], U8, kind="ExternalInput").ap()
    pd8 = nc.dram_tensor("pd8", [8192, D], U8, kind="ExternalInput").ap()
    gscT = nc.dram_tensor("gscT", [64, 16384], F32, kind="ExternalInput").ap()
    uscT = nc.dram_tensor("uscT", [64, 16384], F32, kind="ExternalInput").ap()
    dscT = nc.dram_tensor("dscT", [512, D], F32, kind="ExternalInput").ap()
    wq1 = nc.dram_tensor("wq1", [D, D], F32, kind="ExternalInput").ap()
    wk1 = nc.dram_tensor("wk1", [D, D], F32, kind="ExternalInput").ap()
    wv1 = nc.dram_tensor("wv1", [D, D], F32, kind="ExternalInput").ap()
    wo1 = nc.dram_tensor("wo1", [D, D], F32, kind="ExternalInput").ap()
    wg8 = nc.dram_tensor("wg8", [D, 8192], U8, kind="ExternalInput").ap()
    wu8 = nc.dram_tensor("wu8", [D, 8192], U8, kind="ExternalInput").ap()
    wd8 = nc.dram_tensor("wd8", [8192, D], U8, kind="ExternalInput").ap()
    gsc1 = nc.dram_tensor("gsc1", [128, 64], F32, kind="ExternalInput").ap()
    usc1 = nc.dram_tensor("usc1", [128, 64], F32, kind="ExternalInput").ap()
    dsc1 = nc.dram_tensor("dsc1", [128, 16], F32, kind="ExternalInput").ap()
    xo = nc.dram_tensor("xo", [D, TC], F32, kind="ExternalOutput").ap()

    x_a = nc.dram_tensor("x_a", [D, TC], F32, kind="Internal").ap()
    x_b = nc.dram_tensor("x_b", [D, TC], F32, kind="Internal").ap()
    x_c = nc.dram_tensor("x_c", [D, TC], F32, kind="Internal").ap()
    xful = nc.dram_tensor("xful", [4 * D, TC], F32, kind="Internal").ap()

    with ExitStack() as octx:
        S = _mk_sems_fused(octx, nc)
        blk = octx.enter_context(nc.Block())

        emit_l0attn(nc, blk, S, xT, xfT, wq0, wk0, wv0, wo0, bo0, qbits, cosq,
                    sinq, cosk, sink, tbias, band, x_a)
        barrier_emit(blk, S)
        emit_l0ffn(nc, blk, S, x_a, pg8, pu8, pd8, gscT, uscT, dscT, x_b)
        barrier_emit(blk, S)

        ag = S["ag"]
        stn = S["st"].n

        def do_ag(g, stn=stn):
            g.wait_ge(S["st"].h, stn)
            g.collective_compute(
                "AllGather", ALU.bypass, [[0, 1, 2, 3], [4, 5, 6, 7]],
                ins=[x_b], outs=[xful]).then_inc(ag.h, 16)
        blk.gpsimd(do_ag)
        ag.inc(16)
        agn = ag.n
        blk.sync(lambda sy, agn=agn: sy.wait_ge(ag.h, agn))

        emit_l1attn(nc, blk, S, x_b, xful, wq1, wk1, wv1, wo1, band, x_c,
                    xful_blocks=True)
        barrier_emit(blk, S)
        emit_l1ffn(nc, blk, S, x_c, wg8, wu8, wd8, gsc1, usc1, dsc1, xo)
    return nc


def build_fused_a():
    nc = bass.Bass("TRN2", target_bir_lowering=False)
    TT = 256
    xT = nc.dram_tensor("xT", [D, TC], F32, kind="ExternalInput").ap()
    xfT = nc.dram_tensor("xfT", [D, 4096], F32, kind="ExternalInput").ap()
    wq0 = nc.dram_tensor("wq0", [D, D], F32, kind="ExternalInput").ap()
    wk0 = nc.dram_tensor("wk0", [D, 512], F32, kind="ExternalInput").ap()
    wv0 = nc.dram_tensor("wv0", [D, 512], F32, kind="ExternalInput").ap()
    wo0 = nc.dram_tensor("wo0", [D, D], F32, kind="ExternalInput").ap()
    bo0 = nc.dram_tensor("bo0", [128, 16], F32, kind="ExternalInput").ap()
    qbits = nc.dram_tensor("qbits", [12, 32 * TC], BF16, kind="ExternalInput").ap()
    cosq = nc.dram_tensor("cosq", [32, TC], BF16, kind="ExternalInput").ap()
    sinq = nc.dram_tensor("sinq", [32, TC], BF16, kind="ExternalInput").ap()
    cosk = nc.dram_tensor("cosk", [32, 4096], BF16, kind="ExternalInput").ap()
    sink = nc.dram_tensor("sink", [32, 4096], BF16, kind="ExternalInput").ap()
    tbias = nc.dram_tensor("tbias", [128, 1024], F32, kind="ExternalInput").ap()
    band = nc.dram_tensor("band", [128, 4992], BF16, kind="ExternalInput").ap()
    pg8 = nc.dram_tensor("pg8", [1024, 16384], U8, kind="ExternalInput").ap()
    pu8 = nc.dram_tensor("pu8", [1024, 16384], U8, kind="ExternalInput").ap()
    pd8 = nc.dram_tensor("pd8", [8192, D], U8, kind="ExternalInput").ap()
    gscT = nc.dram_tensor("gscT", [64, 16384], F32, kind="ExternalInput").ap()
    uscT = nc.dram_tensor("uscT", [64, 16384], F32, kind="ExternalInput").ap()
    dscT = nc.dram_tensor("dscT", [512, D], F32, kind="ExternalInput").ap()
    xo = nc.dram_tensor("xo", [D, TC], F32, kind="ExternalOutput").ap()
    x_a = nc.dram_tensor("x_a", [D, TC], F32, kind="Internal").ap()

    with ExitStack() as octx:
        S = _mk_sems_fused(octx, nc)
        blk = octx.enter_context(nc.Block())
        emit_l0attn(nc, blk, S, xT, xfT, wq0, wk0, wv0, wo0, bo0, qbits, cosq,
                    sinq, cosk, sink, tbias, band, x_a)
        barrier_emit(blk, S)
        emit_l0ffn(nc, blk, S, x_a, pg8, pu8, pd8, gscT, uscT, dscT, xo)
    return nc


def build_fused_b():
    nc = bass.Bass("TRN2", target_bir_lowering=False)
    xT = nc.dram_tensor("xT", [D, TC], F32, kind="ExternalInput").ap()
    xfT = nc.dram_tensor("xfT", [D, 4096], F32, kind="ExternalInput").ap()
    wq1 = nc.dram_tensor("wq1", [D, D], F32, kind="ExternalInput").ap()
    wk1 = nc.dram_tensor("wk1", [D, D], F32, kind="ExternalInput").ap()
    wv1 = nc.dram_tensor("wv1", [D, D], F32, kind="ExternalInput").ap()
    wo1 = nc.dram_tensor("wo1", [D, D], F32, kind="ExternalInput").ap()
    band = nc.dram_tensor("band", [128, 4992], BF16, kind="ExternalInput").ap()
    wg8 = nc.dram_tensor("wg8", [D, 8192], U8, kind="ExternalInput").ap()
    wu8 = nc.dram_tensor("wu8", [D, 8192], U8, kind="ExternalInput").ap()
    wd8 = nc.dram_tensor("wd8", [8192, D], U8, kind="ExternalInput").ap()
    gsc1 = nc.dram_tensor("gsc1", [128, 64], F32, kind="ExternalInput").ap()
    usc1 = nc.dram_tensor("usc1", [128, 64], F32, kind="ExternalInput").ap()
    dsc1 = nc.dram_tensor("dsc1", [128, 16], F32, kind="ExternalInput").ap()
    xo = nc.dram_tensor("xo", [D, TC], F32, kind="ExternalOutput").ap()
    x_c = nc.dram_tensor("x_c", [D, TC], F32, kind="Internal").ap()

    with ExitStack() as octx:
        S = _mk_sems_fused(octx, nc)
        blk = octx.enter_context(nc.Block())
        emit_l1attn(nc, blk, S, xT, xfT, wq1, wk1, wv1, wo1, band, x_c)
        barrier_emit(blk, S)
        emit_l1ffn(nc, blk, S, x_c, wg8, wu8, wd8, gsc1, usc1, dsc1, xo)
    return nc


from concourse.bass_utils import run_bass_kernel_spmd

_SIG = perm_evens_odds(2048)
_TAU = perm_evens_odds(16384)
_INV_SIG = np.argsort(_SIG)


def _u8view(a_int32):
    a = np.ascontiguousarray(a_int32.astype(np.int32))
    return a.view(np.uint8)[:, ::4]


def _stage_l0attn_consts(q0):
    NH, Dh = 32, 64
    slopes = 2.0 ** (-(np.arange(1, NH + 1) * 8.0 / NH))
    qpos = q0 + np.arange(1024)
    qbits = np.zeros((12, NH * 1024), dtype=np.float32)
    for b in range(12):
        bit = (qpos >> b) & 1
        for h in range(NH):
            qbits[b, h * 1024:(h + 1) * 1024] = -slopes[h] * (1 << b) * bit
    inv = 1.0 / 10000.0 ** (np.arange(0, Dh, 2) / Dh)
    frq = qpos[:, None] * inv[None, :]
    cosq = np.ascontiguousarray(np.cos(frq).T).astype(BF)
    sinq = np.ascontiguousarray(np.sin(frq).T).astype(BF)
    tpos = np.arange(4096)
    frk = tpos[:, None] * inv[None, :]
    cosk = np.ascontiguousarray(np.cos(frk).T).astype(BF)
    sink = np.ascontiguousarray(np.sin(frk).T).astype(BF)
    tbias = np.zeros((128, 1024), dtype=np.float32)
    pc = np.arange(128)
    for h in range(NH):
        for tbl in range(32):
            tbias[:, h * 32 + tbl] = slopes[h] * (tbl * 128 + pc) / 240.0
    u = np.arange(4992)[None, :]
    band = (u >= pc[:, None] + 3968 - q0).astype(np.float32).astype(BF)
    return {"qbits": qbits.astype(BF), "cosq": cosq, "sinq": sinq,
            "cosk": cosk, "sink": sink, "tbias": tbias, "band": band}


def kernel(L0_attn_hidden_states, L0_attn_Wq, L0_attn_Wk, L0_attn_Wv, L0_attn_Wo,
           L0_attn_bo, L0_ffn_gate_W_packed, L0_ffn_gate_scales, L0_ffn_up_W_packed,
           L0_ffn_up_scales, L0_ffn_down_W_packed, L0_ffn_down_scales,
           L1_attn_Wq, L1_attn_Wk, L1_attn_Wv, L1_attn_Wo,
           L1_ffn_gate_w, L1_ffn_gate_scale, L1_ffn_up_w, L1_ffn_up_scale,
           L1_ffn_down_w, L1_ffn_down_scale):
    x = np.asarray(L0_attn_hidden_states, dtype=np.float32)
    sig, tau, inv = _SIG, _TAU, _INV_SIG
    Dh = 64
    rho = np.concatenate([np.arange(0, Dh, 2), np.arange(1, Dh, 2)])
    colp_q = np.concatenate([h * Dh + rho for h in range(32)])
    colp_k = np.concatenate([g * Dh + rho for g in range(8)])

    wq0 = np.ascontiguousarray(np.asarray(L0_attn_Wq, np.float32)[sig][:, colp_q])
    wk0 = np.ascontiguousarray(np.asarray(L0_attn_Wk, np.float32)[sig][:, colp_k])
    wv0 = np.ascontiguousarray(np.asarray(L0_attn_Wv, np.float32)[sig])
    wo0 = np.ascontiguousarray(np.asarray(L0_attn_Wo, np.float32)[:, sig])
    bo0 = np.ascontiguousarray(np.asarray(L0_attn_bo, np.float32)[sig].reshape(16, 128).T)
    consts = [_stage_l0attn_consts(q0 * 1024) for q0 in range(4)]

    pg8 = np.ascontiguousarray(_u8view(np.asarray(L0_ffn_gate_W_packed)).T[:, tau])
    pu8 = np.ascontiguousarray(_u8view(np.asarray(L0_ffn_up_W_packed)).T[:, tau])
    pd8 = np.ascontiguousarray(_u8view(np.asarray(L0_ffn_down_W_packed)).T[:, sig])
    gscT = np.ascontiguousarray(np.asarray(L0_ffn_gate_scales, np.float32).T[:, tau])
    uscT = np.ascontiguousarray(np.asarray(L0_ffn_up_scales, np.float32).T[:, tau])
    dscT = np.ascontiguousarray(np.asarray(L0_ffn_down_scales, np.float32).T[:, sig])

    wq1 = np.ascontiguousarray(np.asarray(L1_attn_Wq, np.float32)[sig])
    wk1 = np.ascontiguousarray(np.asarray(L1_attn_Wk, np.float32)[sig])
    wv1 = np.ascontiguousarray(np.asarray(L1_attn_Wv, np.float32)[sig])
    wo1 = np.ascontiguousarray(np.asarray(L1_attn_Wo, np.float32)[:, sig])

    wg8 = np.ascontiguousarray(_u8view(np.asarray(L1_ffn_gate_w)).T[sig])
    wu8 = np.ascontiguousarray(_u8view(np.asarray(L1_ffn_up_w)).T[sig])
    wd8 = np.ascontiguousarray(_u8view(np.asarray(L1_ffn_down_w)).T[:, sig])
    gsc1 = np.ascontiguousarray(np.asarray(L1_ffn_gate_scale, np.float32).reshape(64, 128).T)
    usc1 = np.ascontiguousarray(np.asarray(L1_ffn_up_scale, np.float32).reshape(64, 128).T)
    dsc1 = np.ascontiguousarray(np.asarray(L1_ffn_down_scale, np.float32)[sig].reshape(16, 128).T)

    nc_a = build_fused_a()
    in_maps = []
    for c in range(8):
        b, qi = c // 4, c % 4
        xb = x[b]
        im = {"xT": np.ascontiguousarray(xb[qi * 1024:(qi + 1) * 1024].T[sig]),
              "xfT": np.ascontiguousarray(xb.T[sig]),
              "wq0": wq0, "wk0": wk0, "wv0": wv0, "wo0": wo0, "bo0": bo0,
              "qbits": consts[qi]["qbits"], "cosq": consts[qi]["cosq"],
              "sinq": consts[qi]["sinq"], "cosk": consts[qi]["cosk"],
              "sink": consts[qi]["sink"], "tbias": consts[qi]["tbias"],
              "band": consts[qi]["band"],
              "pg8": pg8, "pu8": pu8, "pd8": pd8, "gscT": gscT, "uscT": uscT,
              "dscT": dscT}
        in_maps.append(im)
    res = run_bass_kernel_spmd(nc_a, in_maps, core_ids=list(range(8)))
    shards = [res.results[c]["xo"] for c in range(8)]

    nc_b = build_fused_b()
    in_maps = []
    for c in range(8):
        b, qi = c // 4, c % 4
        xfT = np.ascontiguousarray(np.concatenate(shards[b * 4:(b + 1) * 4], axis=1))
        in_maps.append({"xT": np.ascontiguousarray(shards[c]), "xfT": xfT,
                        "wq1": wq1, "wk1": wk1, "wv1": wv1, "wo1": wo1,
                        "band": consts[qi]["band"],
                        "wg8": wg8, "wu8": wu8, "wd8": wd8, "gsc1": gsc1,
                        "usc1": usc1, "dsc1": dsc1})
    res = run_bass_kernel_spmd(nc_b, in_maps, core_ids=list(range(8)))
    out = np.empty((2, 4096, 2048), dtype=np.float32)
    for c in range(8):
        b, qi = c // 4, c % 4
        out[b, qi * 1024:(qi + 1) * 1024] = res.results[c]["xo"][inv].T
    return out


# version 05 (final)
# speedup vs baseline: 1.2479x; 1.0467x over previous
import numpy as np
import ml_dtypes
from contextlib import ExitStack
import concourse.bass as bass
import concourse.mybir as mybir
from concourse.alu_op_type import AluOpType as ALU

F32 = mybir.dt.float32
BF16 = mybir.dt.bfloat16
U16 = mybir.dt.uint16
U8 = mybir.dt.uint8
FP8 = mybir.dt.float8e4
AF = mybir.ActivationFunctionType
BF = ml_dtypes.bfloat16

D = 2048
TC = 1024
NCH = 16


def perm_evens_odds(n):
    p = []
    for b in range(0, n, 256):
        p.extend(range(b, b + 256, 2))
        p.extend(range(b + 1, b + 256, 2))
    return np.array(p)



_PHASE_N = [0]


def _pn(name):
    return f"{name}_ph{_PHASE_N[0]}"


def _next_phase():
    _PHASE_N[0] += 1


class Sem:
    def __init__(self, h):
        self.h = h
        self.n = 0

    def inc(self, k=1):
        self.n += k
        return self.n


class P:
    ENGINES = ("sync", "gpsimd", "vector", "scalar", "tensor")

    def __init__(self):
        self.q = {e: [] for e in self.ENGINES}

    def add(self, eng, fn):
        self.q[eng].append(fn)

    def sy(self, fn): self.add("sync", fn)
    def g(self, fn): self.add("gpsimd", fn)
    def v(self, fn): self.add("vector", fn)
    def s(self, fn): self.add("scalar", fn)
    def t(self, fn): self.add("tensor", fn)

    def emit(self, blk):
        for eng in self.ENGINES:
            fns = self.q[eng]
            if not fns:
                continue
            def body(e, fns=fns):
                for f in fns:
                    f(e)
            getattr(blk, eng)(body)


def ln_emit(p, sems, xsb, hsb, pp, C, make_rs=False, rs_sb=None, tc=TC, nch=NCH):
    v, t, s = sems["v"], sems["t"], sems["s"]
    half = tc // 512

    def sq(sc):
        li = None
        for c in range(nch):
            li = sc.square(hsb[:, c * tc:(c + 1) * tc], xsb[:, c * tc:(c + 1) * tc])
        li.then_inc(s.h, 1)
    p.s(sq); s.inc()
    s_sq = s.n

    b_s1 = pp[0:1, 4 * 512: 4 * 512 + tc]
    b_s2 = pp[0:1, 6 * 512: 6 * 512 + tc]

    def stats(te, s_sq=s_sq):
        te.wait_ge(s.h, s_sq)
        li = None
        for th in range(half):
            for c in range(nch):
                li = te.matmul(b_s1[:, th * 512:(th + 1) * 512], C["ones_c_f32"],
                               xsb[:, c * tc + th * 512: c * tc + (th + 1) * 512],
                               start=(c == 0), stop=(c == nch - 1))
        for th in range(half):
            for c in range(nch):
                li = te.matmul(b_s2[:, th * 512:(th + 1) * 512], C["ones_c_bf"],
                               hsb[:, c * tc + th * 512: c * tc + (th + 1) * 512],
                               start=(c == 0), stop=(c == nch - 1))
        li.then_inc(t.h, 1)
    p.t(stats); t.inc()
    t_stats = t.n

    mrow, rrow, vrow, srow = C["mrow"], C["rrow"], C["vrow"], C["srow"]

    def rows(vv, t_stats=t_stats):
        vv.wait_ge(t.h, t_stats)
        vv.tensor_scalar_mul(mrow[0:1, 0:tc], b_s1, 1.0 / D)
        vv.tensor_tensor(vrow[0:1, 0:tc], mrow[0:1, 0:tc], mrow[0:1, 0:tc], ALU.mult)
        vv.scalar_tensor_tensor(vrow[0:1, 0:tc], b_s2, 1.0 / D, vrow[0:1, 0:tc],
                                ALU.mult, ALU.subtract).then_inc(v.h, 1)
    p.v(rows); v.inc()
    v_rows = v.n

    def dosqrt(sc, v_rows=v_rows):
        sc.wait_ge(v.h, v_rows)
        sc.activation(srow[0:1, 0:tc], vrow[0:1, 0:tc], AF.Sqrt,
                      bias=C["eps"], scale=1.0).then_inc(s.h, 1)
    p.s(dosqrt); s.inc()
    s_sqrt = s.n

    def dorecip(vv, s_sqrt=s_sqrt):
        vv.wait_ge(s.h, s_sqrt)
        vv.reciprocal(rrow[0:1, 0:tc], srow[0:1, 0:tc]).then_inc(v.h, 1)
    p.v(dorecip); v.inc()
    v_recip = v.n

    b_mb = pp[:, 4 * 512: 4 * 512 + tc]
    b_rb = pp[:, 6 * 512: 6 * 512 + tc]

    def bcast(te, v_recip=v_recip):
        te.wait_ge(v.h, v_recip)
        li = None
        for th in range(half):
            te.matmul(b_mb[:, th * 512:(th + 1) * 512], C["ones_r_f32"],
                      mrow[0:1, th * 512:(th + 1) * 512], start=True, stop=True)
            li = te.matmul(b_rb[:, th * 512:(th + 1) * 512], C["ones_r_f32"],
                           rrow[0:1, th * 512:(th + 1) * 512], start=True, stop=True)
        li.then_inc(t.h, 1)
    p.t(bcast); t.inc()
    t_bcast = t.n

    def mk_h(vv, t_bcast=t_bcast):
        vv.wait_ge(t.h, t_bcast)
        li = None
        for c in range(nch):
            vv.tensor_tensor(C["ftmp"][:, 0:tc], xsb[:, c * tc:(c + 1) * tc], b_mb, ALU.subtract)
            li = vv.tensor_tensor(hsb[:, c * tc:(c + 1) * tc], C["ftmp"][:, 0:tc], b_rb, ALU.mult)
        li.then_inc(v.h, 1)
    p.v(mk_h); v.inc()
    v_h = v.n

    if make_rs:
        b_rs = pp[0:1, 4 * 512: 4 * 512 + tc]

        def rs_mm(te, v_h=v_h):
            te.wait_ge(v.h, v_h)
            li = None
            for th in range(half):
                for c in range(nch):
                    li = te.matmul(b_rs[:, th * 512:(th + 1) * 512], C["ones_c_bf"],
                                   hsb[:, c * tc + th * 512: c * tc + (th + 1) * 512],
                                   start=(c == 0), stop=(c == nch - 1))
            li.then_inc(t.h, 1)
        p.t(rs_mm); t.inc()
        t_rs = t.n

        def rs_cp(vv, t_rs=t_rs):
            vv.wait_ge(t.h, t_rs)
            vv.tensor_copy(rs_sb[0:1, 0:tc], b_rs).then_inc(v.h, 1)
        p.v(rs_cp); v.inc()
    return v_h


def alloc_common(ctx, nc, roww=TC, ftmpw=TC):
    C = {}
    for name, shape, dt in [("ones_c_f32", [128, 1], F32), ("ones_c_bf", [128, 1], BF16),
                            ("ones_r_f32", [1, 128], F32), ("c128", [1, 128], BF16),
                            ("eps", [1, 1], F32), ("mrow", [1, roww], F32), ("rrow", [1, roww], F32),
                            ("vrow", [1, roww], F32), ("srow", [1, roww], F32), ("ftmp", [128, ftmpw], F32)]:
        h = ctx.enter_context(nc.sbuf_tensor(_pn(name), shape, dt))
        C[name] = h[:] if name in ("ones_c_f32", "ones_c_bf", "ones_r_f32", "eps") else h
    return C


def init_consts_emit(p, sems, C):
    v = sems["v"]

    def ic(vv):
        vv.memset(C["ones_c_f32"], 1.0)
        vv.memset(C["ones_c_bf"], 1.0)
        vv.memset(C["ones_r_f32"], 1.0)
        vv.memset(C["c128"][:], -128.0)
        vv.memset(C["eps"], 1e-5).then_inc(v.h, 1)
    p.v(ic); v.inc()


def build_l1ffn(stages=99):
    nc = bass.Bass("TRN2", target_bir_lowering=False)
    xT = nc.dram_tensor("xT", [D, TC], F32, kind="ExternalInput").ap()
    wg8 = nc.dram_tensor("wg8", [D, 8192], U8, kind="ExternalInput").ap()
    wu8 = nc.dram_tensor("wu8", [D, 8192], U8, kind="ExternalInput").ap()
    wd8 = nc.dram_tensor("wd8", [8192, D], U8, kind="ExternalInput").ap()
    gsc = nc.dram_tensor("gsc", [128, 64], F32, kind="ExternalInput").ap()
    usc = nc.dram_tensor("usc", [128, 64], F32, kind="ExternalInput").ap()
    dsc = nc.dram_tensor("dsc", [128, 16], F32, kind="ExternalInput").ap()
    xo = nc.dram_tensor("xo", [D, TC], F32, kind="ExternalOutput").ap()
    with ExitStack() as octx:
        S = _mk_sems_fused(octx, nc)
        blk = octx.enter_context(nc.Block())
        emit_l1ffn(nc, blk, S, xT, wg8, wu8, wd8, gsc, usc, dsc, xo)
    return nc


def emit_l1ffn(nc, blk, S, xT, wg8, wu8, wd8, gsc, usc, dsc, xo, stages=99):
    _next_phase()

    I = 8192
    NIC = I // 128
    GRP = 16
    NGRP = NIC // GRP
    NIS = 4

    with ExitStack() as ctx:
        xsb = ctx.enter_context(nc.sbuf_tensor(_pn("xsb"), [128, NCH * TC], F32))
        hsb = ctx.enter_context(nc.sbuf_tensor(_pn("hsb"), [128, NCH * TC], BF16))
        inter = ctx.enter_context(nc.sbuf_tensor(_pn("inter"), [128, GRP * TC], BF16))
        g_sb = ctx.enter_context(nc.sbuf_tensor(_pn("g_sb"), [128, 4 * TC], BF16))
        sig = ctx.enter_context(nc.sbuf_tensor(_pn("sig"), [128, 4 * TC], BF16))
        wgt = ctx.enter_context(nc.sbuf_tensor(_pn("wgt"), [128, 2 * 512], BF16))
        wut = ctx.enter_context(nc.sbuf_tensor(_pn("wut"), [128, 2 * 512], BF16))
        wdt = ctx.enter_context(nc.sbuf_tensor(_pn("wdt"), [128, 2 * 512], BF16))
        gsc_sb = ctx.enter_context(nc.sbuf_tensor(_pn("gsc_sb"), [128, 64], F32))
        usc_sb = ctx.enter_context(nc.sbuf_tensor(_pn("usc_sb"), [128, 64], F32))
        dsc_sb = ctx.enter_context(nc.sbuf_tensor(_pn("dsc_sb"), [128, 16], F32))
        rs_sb = ctx.enter_context(nc.sbuf_tensor(_pn("rs_sb"), [1, TC], BF16))
        rs2_sb = ctx.enter_context(nc.sbuf_tensor(_pn("rs2_sb"), [1, TC], BF16))
        btmp = ctx.enter_context(nc.sbuf_tensor(_pn("btmp"), [128, 512], BF16))
        C = alloc_common(ctx, nc)
        pp = ctx.enter_context(nc.psum_tensor(_pn("pp"), [128, 8 * 512], F32))

        p = P()
        ld, w, v, t, s, st = S["ld"], S["w"], S["v"], S["t"], S["s"], S["st"]
        s_ld, s_w, s_v, s_t, s_s, s_st = ld.h, w.h, v.h, t.h, s.h, st.h
        sems = {"v": v, "t": t, "s": s}

        init_consts_emit(p, sems, C)
        v_c0 = v.n
        p.t(lambda te, v_c0=v_c0: te.wait_ge(s_v, v_c0))
        p.s(lambda sc, v_c0=v_c0: sc.wait_ge(s_v, v_c0))

        def loads(sy):
            xT_ap = bass.AP(xT.tensor, 0, [[TC, 128], [128 * TC, NCH], [1, TC]])
            sy.dma_start(xsb[:], xT_ap).then_inc(s_ld, 16)
            sy.dma_start(gsc_sb[:], gsc).then_inc(s_ld, 16)
            sy.dma_start(usc_sb[:], usc).then_inc(s_ld, 16)
            sy.dma_start(dsc_sb[:], dsc).then_inc(s_ld, 16)
        p.sy(loads); ld.inc(64)
        ld_x = ld.n

        p.s(lambda sc: sc.wait_ge(s_ld, ld_x))
        p.v(lambda vv: vv.wait_ge(s_ld, ld_x))
        p.t(lambda te: te.wait_ge(s_ld, ld_x))

        if stages >= 2:
            ln_emit(p, sems, xsb, hsb, pp, C, make_rs=True, rs_sb=rs_sb)
            v_ln = v.n
            p.t(lambda te, v_ln=v_ln: te.wait_ge(s_v, v_ln))

        s_sig_cur = [0]
        for grp in range(NGRP if stages >= 3 else 0):
            for isl in range(NIS):
                isg = grp * NIS + isl
                for mat in range(2):
                    wt = wgt if mat == 0 else wut
                    w8 = wg8 if mat == 0 else wu8
                    for kc in range(NCH):
                        buf = kc % 2
                        war = t.n - 1 if t.n >= 1 else None
                        def wl(g, kc=kc, isg=isg, wt=wt, w8=w8, buf=buf, war=war):
                            if war is not None:
                                g.wait_ge(s_t, war)
                            g.dma_start(wt[:, buf * 512:(buf + 1) * 512],
                                        w8[kc * 128:(kc + 1) * 128, isg * 512:(isg + 1) * 512]
                                        ).then_inc(s_w, 16)
                        p.g(wl); w.inc(16)
                        w_tile = w.n

                        def mm(te, kc=kc, buf=buf, w_tile=w_tile, wt=wt):
                            te.wait_ge(s_w, w_tile)
                            li = None
                            for ic in range(4):
                                for th in range(2):
                                    li = te.matmul(pp[:, (ic * 2 + th) * 512:(ic * 2 + th + 1) * 512],
                                                   wt[:, buf * 512 + ic * 128: buf * 512 + ic * 128 + 128],
                                                   hsb[:, kc * TC + th * 512: kc * TC + th * 512 + 512],
                                                   start=(kc == 0), stop=False)
                            li.then_inc(s_t, 1)
                        p.t(mm); t.inc()

                    def corr(te):
                        li = None
                        for ic in range(4):
                            for th in range(2):
                                li = te.matmul(pp[:, (ic * 2 + th) * 512:(ic * 2 + th + 1) * 512],
                                               C["c128"][:], rs_sb[0:1, th * 512:(th + 1) * 512],
                                               start=False, stop=True)
                        li.then_inc(s_t, 1)
                    p.t(corr); t.inc()
                    t_mat = t.n

                    if mat == 0:
                        def evac_g(vv, isg=isg, t_mat=t_mat):
                            vv.wait_ge(s_t, t_mat)
                            li = None
                            for ic in range(4):
                                gic = isg * 4 + ic
                                for th in range(2):
                                    li = vv.tensor_scalar(
                                        g_sb[:, ic * TC + th * 512: ic * TC + th * 512 + 512],
                                        pp[:, (ic * 2 + th) * 512:(ic * 2 + th + 1) * 512],
                                        gsc_sb[:, gic:gic + 1], None, ALU.mult)
                            li.then_inc(s_v, 1)
                        p.v(evac_g); v.inc()
                        v_g = v.n

                        def mk_sig(sc, v_g=v_g):
                            sc.wait_ge(s_v, v_g)
                            li = None
                            for ic in range(4):
                                li = sc.activation(sig[:, ic * TC:(ic + 1) * TC],
                                                   g_sb[:, ic * TC:(ic + 1) * TC],
                                                   AF.Sigmoid, scale=1.0)
                            li.then_inc(s_s, 1)
                        p.s(mk_sig); s.inc()
                        s_sig_cur[0] = s.n
                        p.t(lambda te, v_g=v_g: te.wait_ge(s_v, v_g))
                    else:
                        def evac_u(vv, isg=isg, isl=isl, t_mat=t_mat, s_sig=s_sig_cur[0]):
                            vv.wait_ge(s_t, t_mat)
                            vv.wait_ge(s_s, s_sig)
                            li = None
                            for ic in range(4):
                                gic = isg * 4 + ic
                                iin = isl * 4 + ic
                                for th in range(2):
                                    vv.tensor_scalar(C["ftmp"][:, 0:512],
                                                     pp[:, (ic * 2 + th) * 512:(ic * 2 + th + 1) * 512],
                                                     usc_sb[:, gic:gic + 1], None, ALU.mult)
                                    vv.tensor_tensor(btmp[:, 0:512],
                                                     sig[:, ic * TC + th * 512: ic * TC + th * 512 + 512],
                                                     g_sb[:, ic * TC + th * 512: ic * TC + th * 512 + 512],
                                                     ALU.mult)
                                    li = vv.tensor_tensor(
                                        inter[:, iin * TC + th * 512: iin * TC + th * 512 + 512],
                                        btmp[:, 0:512], C["ftmp"][:, 0:512], ALU.mult)
                            li.then_inc(s_v, 1)
                        p.v(evac_u); v.inc()
                        v_u = v.n
                        p.t(lambda te, v_u=v_u: te.wait_ge(s_v, v_u))

            def rs2_mm(te):
                li = None
                for th in range(2):
                    for iin in range(GRP):
                        li = te.matmul(pp[0:1, th * 512:(th + 1) * 512], C["ones_c_bf"],
                                       inter[:, iin * TC + th * 512: iin * TC + th * 512 + 512],
                                       start=(iin == 0), stop=(iin == GRP - 1))
                li.then_inc(s_t, 1)
            p.t(rs2_mm); t.inc()
            t_rs2 = t.n

            def rs2_cp(vv, t_rs2=t_rs2):
                vv.wait_ge(s_t, t_rs2)
                vv.tensor_copy(rs2_sb[0:1, :], pp[0:1, 0:TC]).then_inc(s_v, 1)
            p.v(rs2_cp); v.inc()
            v_rs2 = v.n
            p.t(lambda te, v_rs2=v_rs2: te.wait_ge(s_v, v_rs2))

            for hg in range(4):
                for ic in range(GRP):
                    nlin = (grp * 4 + hg) * GRP + ic
                    buf = nlin % 2
                    war = t.n - 1 if t.n >= 1 else None

                    def wdl(g, grp=grp, hg=hg, ic=ic, buf=buf, war=war):
                        if war is not None:
                            g.wait_ge(s_t, war)
                        g.dma_start(wdt[:, buf * 512:(buf + 1) * 512],
                                    wd8[(grp * GRP + ic) * 128:(grp * GRP + ic + 1) * 128,
                                        hg * 512:(hg + 1) * 512]).then_inc(s_w, 16)
                    p.g(wdl); w.inc(16)
                    w_tile = w.n

                    def dmm(te, ic=ic, buf=buf, w_tile=w_tile):
                        te.wait_ge(s_w, w_tile)
                        li = None
                        for hc in range(4):
                            for th in range(2):
                                li = te.matmul(pp[:, (hc * 2 + th) * 512:(hc * 2 + th + 1) * 512],
                                               wdt[:, buf * 512 + hc * 128: buf * 512 + hc * 128 + 128],
                                               inter[:, ic * TC + th * 512: ic * TC + th * 512 + 512],
                                               start=(ic == 0), stop=False)
                        li.then_inc(s_t, 1)
                    p.t(dmm); t.inc()

                def dcorr(te):
                    li = None
                    for hc in range(4):
                        for th in range(2):
                            li = te.matmul(pp[:, (hc * 2 + th) * 512:(hc * 2 + th + 1) * 512],
                                           C["c128"][:], rs2_sb[0:1, th * 512:(th + 1) * 512],
                                           start=False, stop=True)
                    li.then_inc(s_t, 1)
                p.t(dcorr); t.inc()
                t_down = t.n

                def xacc(vv, hg=hg, t_down=t_down):
                    vv.wait_ge(s_t, t_down)
                    li = None
                    for hc in range(4):
                        h = hg * 4 + hc
                        for th in range(2):
                            li = vv.scalar_tensor_tensor(
                                xsb[:, h * TC + th * 512: h * TC + th * 512 + 512],
                                pp[:, (hc * 2 + th) * 512:(hc * 2 + th + 1) * 512],
                                dsc_sb[:, h:h + 1],
                                xsb[:, h * TC + th * 512: h * TC + th * 512 + 512],
                                ALU.mult, ALU.add)
                    li.then_inc(s_v, 1)
                p.v(xacc); v.inc()
                v_x = v.n
                p.t(lambda te, v_x=v_x: te.wait_ge(s_v, v_x))

        v_done = v.n

        stv = st.n + 16

        def store(sy, v_done=v_done, stv=stv):
            sy.wait_ge(s_v, v_done)
            xo_ap = bass.AP(xo.tensor, 0, [[TC, 128], [128 * TC, NCH], [1, TC]])
            sy.dma_start(xo_ap, xsb[:]).then_inc(s_st, 16)
            sy.wait_ge(s_st, stv)
        p.sy(store); st.inc(16)

        p.emit(blk)


def build_l0ffn(stages=99):
    nc = bass.Bass("TRN2", target_bir_lowering=False)
    xT = nc.dram_tensor("xT", [D, TC], F32, kind="ExternalInput").ap()
    pg8 = nc.dram_tensor("pg8", [1024, 16384], U8, kind="ExternalInput").ap()
    pu8 = nc.dram_tensor("pu8", [1024, 16384], U8, kind="ExternalInput").ap()
    pd8 = nc.dram_tensor("pd8", [8192, D], U8, kind="ExternalInput").ap()
    gscT = nc.dram_tensor("gscT", [64, 16384], F32, kind="ExternalInput").ap()
    uscT = nc.dram_tensor("uscT", [64, 16384], F32, kind="ExternalInput").ap()
    dscT = nc.dram_tensor("dscT", [512, D], F32, kind="ExternalInput").ap()
    xo = nc.dram_tensor("xo", [D, TC], F32, kind="ExternalOutput").ap()
    with ExitStack() as octx:
        S = _mk_sems_fused(octx, nc)
        blk = octx.enter_context(nc.Block())
        emit_l0ffn(nc, blk, S, xT, pg8, pu8, pd8, gscT, uscT, dscT, xo)
    return nc


def emit_l0ffn(nc, blk, S, xT, pg8, pu8, pd8, gscT, uscT, dscT, xo, stages=99):
    _next_phase()

    I = 16384
    NIC = I // 128
    GRP = 16
    NGRP = NIC // GRP
    NIS = 4
    GELU_S = 1.5957691216057308

    with ExitStack() as ctx:
        xsb = ctx.enter_context(nc.sbuf_tensor(_pn("xsb"), [128, NCH * TC], F32))
        hsb = ctx.enter_context(nc.sbuf_tensor(_pn("hsb"), [128, NCH * TC], BF16))
        inter = ctx.enter_context(nc.sbuf_tensor(_pn("inter"), [128, GRP * TC], BF16))
        g_sb = ctx.enter_context(nc.sbuf_tensor(_pn("g_sb"), [128, 4 * TC], BF16))
        sig = ctx.enter_context(nc.sbuf_tensor(_pn("sig"), [128, 4 * TC], BF16))
        wgb = ctx.enter_context(nc.sbuf_tensor(_pn("wgb"), [128, 8 * 512], BF16))
        wub = ctx.enter_context(nc.sbuf_tensor(_pn("wub"), [128, 8 * 512], BF16))
        wdb = ctx.enter_context(nc.sbuf_tensor(_pn("wdb"), [128, 8 * 512], BF16))
        pk = ctx.enter_context(nc.sbuf_tensor(_pn("pk"), [128, 3 * 512], U8))
        f8a = ctx.enter_context(nc.sbuf_tensor(_pn("f8a"), [128, 256], U16))
        f8b = ctx.enter_context(nc.sbuf_tensor(_pn("f8b"), [128, 256], U16))
        f8lo = ctx.enter_context(nc.sbuf_tensor(_pn("f8lo"), [128, 512], U8))
        f8hi = ctx.enter_context(nc.sbuf_tensor(_pn("f8hi"), [128, 512], U8))
        srg = ctx.enter_context(nc.sbuf_tensor(_pn("srg"), [128, 512], F32))
        sru = ctx.enter_context(nc.sbuf_tensor(_pn("sru"), [128, 512], F32))
        btmp = ctx.enter_context(nc.sbuf_tensor(_pn("btmp"), [128, TC], BF16))
        C = alloc_common(ctx, nc)
        pp = ctx.enter_context(nc.psum_tensor(_pn("pp"), [128, 8 * 512], F32))

        p = P()
        ld, w, v, t, s, st = S["ld"], S["w"], S["v"], S["t"], S["s"], S["st"]
        s_ld, s_w, s_v, s_t, s_s, s_st = ld.h, w.h, v.h, t.h, s.h, st.h
        sems = {"v": v, "t": t, "s": s}

        init_consts_emit(p, sems, C)
        v_c0 = v.n
        p.t(lambda te, v_c0=v_c0: te.wait_ge(s_v, v_c0))
        p.s(lambda sc, v_c0=v_c0: sc.wait_ge(s_v, v_c0))

        def loads(sy):
            xT_ap = bass.AP(xT.tensor, 0, [[TC, 128], [128 * TC, NCH], [1, TC]])
            sy.dma_start(xsb[:], xT_ap).then_inc(s_ld, 16)
        p.sy(loads); ld.inc(16)
        ld_x = ld.n

        p.s(lambda sc: sc.wait_ge(s_ld, ld_x))
        p.v(lambda vv: vv.wait_ge(s_ld, ld_x))
        p.t(lambda te: te.wait_ge(s_ld, ld_x))

        if stages >= 2:
            ln_emit(p, sems, xsb, hsb, pp, C)
            v_ln = v.n
            p.t(lambda te, v_ln=v_ln: te.wait_ge(s_v, v_ln))

        def emit_dequant(srcp8, scT, bc, col0, wb, wb_col0, pk_off, sr_sb, war_t, war_v):
            war_pk = v.n if v.n > 0 else None

            def pk_l(g, bc=bc, col0=col0, pk_off=pk_off, war_pk=war_pk):
                if war_pk is not None:
                    g.wait_ge(s_v, war_pk)
                g.dma_start(pk[:, pk_off:pk_off + 512],
                            srcp8[bc * 128:(bc + 1) * 128, col0:col0 + 512]).then_inc(s_w, 16)
            p.g(pk_l); w.inc(16)
            w_pk = w.n

            def sr_l(sy, bc=bc, col0=col0, scT=scT, sr_sb=sr_sb, war_pk=war_pk):
                if war_pk is not None:
                    sy.wait_ge(s_v, war_pk)
                src = bass.AP(scT.tensor, bc * 8 * scT.shape[1] + col0,
                              [[scT.shape[1], 8], [0, 16], [1, 512]])
                sy.dma_start(sr_sb[:], src).then_inc(s_w, 16)
            p.sy(sr_l); w.inc(16)
            w_sr = w.n

            def simd(vv, pk_off=pk_off, w_pk=w_pk, w_sr=w_sr, wb=wb, wb_col0=wb_col0,
                     sr_sb=sr_sb, war_t=war_t):
                vv.wait_ge(s_w, w_pk)
                vv.wait_ge(s_w, w_sr)
                if war_t is not None:
                    vv.wait_ge(s_t, war_t)
                p16 = pk[:, pk_off:pk_off + 512].bitcast(U16)
                vv.tensor_scalar(f8a[:], p16, 0x0707, 2, ALU.bitwise_and, ALU.logical_shift_left)
                vv.tensor_scalar(f8b[:], p16, 0x0808, 4, ALU.bitwise_and, ALU.logical_shift_left)
                vv.tensor_tensor(f8lo[:].bitcast(U16), f8a[:], f8b[:], ALU.bitwise_or)
                vv.tensor_scalar(f8a[:], p16, 0x7070, 2, ALU.bitwise_and, ALU.logical_shift_right)
                vv.tensor_scalar(f8b[:], p16, 0x8080, None, ALU.bitwise_and)
                vv.tensor_tensor(f8hi[:].bitcast(U16), f8a[:], f8b[:], ALU.bitwise_or)
                vv.scalar_tensor_tensor(wb[:, wb_col0:wb_col0 + 512], f8lo[:].bitcast(FP8),
                                        64.0, sr_sb[:], ALU.mult, ALU.mult)
                vv.scalar_tensor_tensor(wb[:, wb_col0 + 512:wb_col0 + 1024], f8hi[:].bitcast(FP8),
                                        64.0, sr_sb[:], ALU.mult, ALU.mult).then_inc(s_v, 1)
            p.v(simd); v.inc()

        s_sig_cur = [0]
        wb_last_use = {}
        for grp in range(NGRP if stages >= 3 else 0):
            for isl in range(NIS):
                isg = grp * NIS + isl
                for mat in range(2):
                    srcp8 = pg8 if mat == 0 else pu8
                    scT = gscT if mat == 0 else uscT
                    wb = wgb if mat == 0 else wub
                    sr_sb = srg if mat == 0 else sru
                    for halfr in range(2):
                        for bcl in range(4):
                            bc = halfr * 4 + bcl
                            war_t = wb_last_use.get((mat, halfr))
                            emit_dequant(srcp8, scT, bc, isg * 512, wb, bcl * 1024,
                                         (0 if mat == 0 else 512), sr_sb, war_t, None)
                        v_deq = v.n

                        def mmat(te, wb=wb, v_deq=v_deq, halfr=halfr):
                            te.wait_ge(s_v, v_deq)
                            li = None
                            for kcl in range(8):
                                kc = halfr * 8 + kcl
                                for ic in range(4):
                                    for th in range(2):
                                        li = te.matmul(pp[:, (ic * 2 + th) * 512:(ic * 2 + th + 1) * 512],
                                                       wb[:, kcl * 512 + ic * 128: kcl * 512 + ic * 128 + 128],
                                                       hsb[:, kc * TC + th * 512: kc * TC + th * 512 + 512],
                                                       start=(kc == 0), stop=(kc == NCH - 1))
                            li.then_inc(s_t, 1)
                        p.t(mmat); t.inc()
                        wb_last_use[(mat, halfr)] = t.n
                    t_mat = t.n

                    if mat == 0:
                        def evac_g(vv, t_mat=t_mat):
                            vv.wait_ge(s_t, t_mat)
                            li = None
                            for ic in range(4):
                                for th in range(2):
                                    li = vv.tensor_copy(
                                        g_sb[:, ic * TC + th * 512: ic * TC + th * 512 + 512],
                                        pp[:, (ic * 2 + th) * 512:(ic * 2 + th + 1) * 512])
                            li.then_inc(s_v, 1)
                        p.v(evac_g); v.inc()
                        v_g = v.n

                        def gelu1(sc, t_mat=t_mat, v_g=v_g):
                            sc.wait_ge(s_t, t_mat)
                            li = None
                            for ic in range(4):
                                for th in range(2):
                                    li = sc.square(sq_sb[:, th * 512:th * 512 + 512],
                                                   pp[:, (ic * 2 + th) * 512:(ic * 2 + th + 1) * 512]) \
                                        if False else None
                            sc.wait_ge(s_v, v_g)
                            for ic in range(4):
                                li = sc.square(sig[:, ic * TC:(ic + 1) * TC], g_sb[:, ic * TC:(ic + 1) * TC])
                            li.then_inc(s_s, 1)
                        p.s(gelu1); s.inc()
                        s_sq2 = s.n

                        def gelu2(vv, s_sq2=s_sq2):
                            vv.wait_ge(s_s, s_sq2)
                            li = None
                            for ic in range(4):
                                vv.tensor_scalar(sig[:, ic * TC:(ic + 1) * TC],
                                                 sig[:, ic * TC:(ic + 1) * TC],
                                                 0.044715, 1.0, ALU.mult, ALU.add)
                                li = vv.tensor_tensor(sig[:, ic * TC:(ic + 1) * TC],
                                                      sig[:, ic * TC:(ic + 1) * TC],
                                                      g_sb[:, ic * TC:(ic + 1) * TC], ALU.mult)
                            li.then_inc(s_v, 1)
                        p.v(gelu2); v.inc()
                        v_z = v.n

                        def gelu3(sc, v_z=v_z):
                            sc.wait_ge(s_v, v_z)
                            li = None
                            for ic in range(4):
                                li = sc.activation(sig[:, ic * TC:(ic + 1) * TC],
                                                   sig[:, ic * TC:(ic + 1) * TC],
                                                   AF.Sigmoid, scale=GELU_S)
                            li.then_inc(s_s, 1)
                        p.s(gelu3); s.inc()
                        s_sig_cur[0] = s.n
                        p.t(lambda te, v_g=v_g: te.wait_ge(s_v, v_g))
                    else:
                        def evac_u(vv, isl=isl, t_mat=t_mat, s_sig=s_sig_cur[0]):
                            vv.wait_ge(s_t, t_mat)
                            vv.wait_ge(s_s, s_sig)
                            li = None
                            for ic in range(4):
                                iin = isl * 4 + ic
                                for th in range(2):
                                    sl = slice(ic * TC + th * 512, ic * TC + th * 512 + 512)
                                    vv.tensor_tensor(btmp[:, 0:512], sig[:, sl], g_sb[:, sl], ALU.mult)
                                    li = vv.tensor_tensor(
                                        inter[:, iin * TC + th * 512: iin * TC + th * 512 + 512],
                                        btmp[:, 0:512],
                                        pp[:, (ic * 2 + th) * 512:(ic * 2 + th + 1) * 512], ALU.mult)
                            li.then_inc(s_v, 1)
                        p.v(evac_u); v.inc()
                        v_u = v.n
                        p.t(lambda te, v_u=v_u: te.wait_ge(s_v, v_u))

            for hg in range(4):
                for halfr in range(2):
                    for bci in range(4):
                        bcl = halfr * 4 + bci
                        bc_glob = grp * 8 + bcl
                        war_t = wb_last_use.get(("d", halfr))
                        emit_dequant(pd8, dscT, bc_glob, hg * 512, wdb, bci * 1024,
                                     1024, srg, war_t, None)
                    v_deqd = v.n

                    def dmms(te, v_deqd=v_deqd, halfr=halfr):
                        te.wait_ge(s_v, v_deqd)
                        li = None
                        for icll in range(8):
                            icl = halfr * 8 + icll
                            for hc in range(4):
                                for th in range(2):
                                    li = te.matmul(pp[:, (hc * 2 + th) * 512:(hc * 2 + th + 1) * 512],
                                                   wdb[:, icll * 512 + hc * 128: icll * 512 + hc * 128 + 128],
                                                   inter[:, icl * TC + th * 512: icl * TC + th * 512 + 512],
                                                   start=(icl == 0), stop=(icl == GRP - 1))
                        li.then_inc(s_t, 1)
                    p.t(dmms); t.inc()
                    wb_last_use[("d", halfr)] = t.n
                t_down = t.n

                def xacc(vv, hg=hg, t_down=t_down):
                    vv.wait_ge(s_t, t_down)
                    li = None
                    for hc in range(4):
                        h = hg * 4 + hc
                        for th in range(2):
                            li = vv.scalar_tensor_tensor(
                                xsb[:, h * TC + th * 512: h * TC + th * 512 + 512],
                                pp[:, (hc * 2 + th) * 512:(hc * 2 + th + 1) * 512],
                                1.0,
                                xsb[:, h * TC + th * 512: h * TC + th * 512 + 512],
                                ALU.mult, ALU.add)
                    li.then_inc(s_v, 1)
                p.v(xacc); v.inc()
                v_x = v.n
                p.t(lambda te, v_x=v_x: te.wait_ge(s_v, v_x))

        v_done = v.n

        stv = st.n + 16

        def store(sy, v_done=v_done, stv=stv):
            sy.wait_ge(s_v, v_done)
            xo_ap = bass.AP(xo.tensor, 0, [[TC, 128], [128 * TC, NCH], [1, TC]])
            sy.dma_start(xo_ap, xsb[:]).then_inc(s_st, 16)
            sy.wait_ge(s_st, stv)
        p.sy(store); st.inc(16)

        p.emit(blk)


def ln_tile_emit(p, sems, xt, ht, pp, C, tcw=512, nch=NCH, xcolstride=None, hcolstride=None,
                 xoff=0, hoff=0):
    v, t, s = sems["v"], sems["t"], sems["s"]
    xs = xcolstride
    hs = hcolstride

    def sq(sc):
        li = None
        for c in range(nch):
            li = sc.square(ht[:, hoff + c * hs: hoff + c * hs + tcw],
                           xt[:, xoff + c * xs: xoff + c * xs + tcw])
        li.then_inc(s.h, 1)
    p.s(sq); s.inc()
    s_sq = s.n

    b_s1 = pp[0:1, 4 * 512: 4 * 512 + tcw]
    b_s2 = pp[0:1, 6 * 512: 6 * 512 + tcw]

    def stats(te, s_sq=s_sq):
        te.wait_ge(s.h, s_sq)
        li = None
        for c in range(nch):
            li = te.matmul(b_s1, C["ones_c_f32"], xt[:, xoff + c * xs: xoff + c * xs + tcw],
                           start=(c == 0), stop=(c == nch - 1))
        for c in range(nch):
            li = te.matmul(b_s2, C["ones_c_bf"], ht[:, hoff + c * hs: hoff + c * hs + tcw],
                           start=(c == 0), stop=(c == nch - 1))
        li.then_inc(t.h, 1)
    p.t(stats); t.inc()
    t_stats = t.n

    mrow, rrow, vrow, srow = C["mrow"], C["rrow"], C["vrow"], C["srow"]

    def rows(vv, t_stats=t_stats):
        vv.wait_ge(t.h, t_stats)
        vv.tensor_scalar_mul(mrow[0:1, 0:tcw], b_s1, 1.0 / D)
        vv.tensor_tensor(vrow[0:1, 0:tcw], mrow[0:1, 0:tcw], mrow[0:1, 0:tcw], ALU.mult)
        vv.scalar_tensor_tensor(vrow[0:1, 0:tcw], b_s2, 1.0 / D, vrow[0:1, 0:tcw],
                                ALU.mult, ALU.subtract).then_inc(v.h, 1)
    p.v(rows); v.inc()
    v_rows = v.n

    def dosqrt(sc, v_rows=v_rows):
        sc.wait_ge(v.h, v_rows)
        sc.activation(srow[0:1, 0:tcw], vrow[0:1, 0:tcw], AF.Sqrt,
                      bias=C["eps"], scale=1.0).then_inc(s.h, 1)
    p.s(dosqrt); s.inc()
    s_sqrt = s.n

    def dorecip(vv, s_sqrt=s_sqrt):
        vv.wait_ge(s.h, s_sqrt)
        vv.reciprocal(rrow[0:1, 0:tcw], srow[0:1, 0:tcw]).then_inc(v.h, 1)
    p.v(dorecip); v.inc()
    v_recip = v.n

    b_mb = pp[:, 4 * 512: 4 * 512 + tcw]
    b_rb = pp[:, 6 * 512: 6 * 512 + tcw]

    def bcast(te, v_recip=v_recip):
        te.wait_ge(v.h, v_recip)
        te.matmul(b_mb, C["ones_r_f32"], mrow[0:1, 0:tcw], start=True, stop=True)
        te.matmul(b_rb, C["ones_r_f32"], rrow[0:1, 0:tcw], start=True, stop=True).then_inc(t.h, 1)
    p.t(bcast); t.inc()
    t_bcast = t.n

    def mk_h(vv, t_bcast=t_bcast):
        vv.wait_ge(t.h, t_bcast)
        li = None
        for c in range(nch):
            vv.tensor_tensor(C["ftmp"][:, 0:tcw], xt[:, xoff + c * xs: xoff + c * xs + tcw],
                             b_mb, ALU.subtract)
            li = vv.tensor_tensor(ht[:, hoff + c * hs: hoff + c * hs + tcw],
                                  C["ftmp"][:, 0:tcw], b_rb, ALU.mult)
        li.then_inc(v.h, 1)
    p.v(mk_h); v.inc()
    return v.n


def build_l1attn(stages=99):
    nc = bass.Bass("TRN2", target_bir_lowering=False)
    xT = nc.dram_tensor("xT", [D, TC], F32, kind="ExternalInput").ap()
    xfT = nc.dram_tensor("xfT", [D, 4096], F32, kind="ExternalInput").ap()
    wq = nc.dram_tensor("wq", [D, D], F32, kind="ExternalInput").ap()
    wk = nc.dram_tensor("wk", [D, D], F32, kind="ExternalInput").ap()
    wv = nc.dram_tensor("wv", [D, D], F32, kind="ExternalInput").ap()
    wo = nc.dram_tensor("wo", [D, D], F32, kind="ExternalInput").ap()
    band_in = nc.dram_tensor("band", [128, 4992], BF16, kind="ExternalInput").ap()
    xo = nc.dram_tensor("xo", [D, TC], F32, kind="ExternalOutput").ap()
    with ExitStack() as octx:
        S = _mk_sems_fused(octx, nc)
        blk = octx.enter_context(nc.Block())
        emit_l1attn(nc, blk, S, xT, xfT, wq, wk, wv, wo, band_in, xo)
    return nc


def emit_l1attn(nc, blk, S, xT, xfT, wq, wk, wv, wo, band_in, xo,
                xful_blocks=False, stages=99):
    _next_phase()
    """xful_blocks: xfT is an AllGather output [4*2048, 1024] (rank-major blocks)
    instead of [2048, 4096]."""
    k1d = nc.dram_tensor("k1d_p2", [D, 4096], BF16, kind="Internal").ap()
    v1d = nc.dram_tensor("v1d_p2", [16 * 4096 * 130], BF16, kind="Internal").ap()

    NH = 16

    with ExitStack() as ctx:
        xt = ctx.enter_context(nc.sbuf_tensor(_pn("xt"), [128, NCH * 512], F32))
        harena = ctx.enter_context(nc.sbuf_tensor(_pn("harena"), [128, NCH * 2048], BF16))
        q1 = ctx.enter_context(nc.sbuf_tensor(_pn("q1"), [128, NH * TC], BF16))
        wsl = ctx.enter_context(nc.sbuf_tensor(_pn("wsl"), [128, NCH * 512], BF16))
        wot = ctx.enter_context(nc.sbuf_tensor(_pn("wot"), [128, 2 * NCH * 128], BF16))
        kslab = ctx.enter_context(nc.sbuf_tensor(_pn("kslab"), [128, 4096], BF16))
        vslab = ctx.enter_context(nc.sbuf_tensor(_pn("vslab"), [128, 32 * 130], BF16))
        band = ctx.enter_context(nc.sbuf_tensor(_pn("band_sb"), [128, 4992], BF16))
        Tbuf = ctx.enter_context(nc.sbuf_tensor(_pn("Tbuf"), [128, 2 * 512], F32))
        Pbuf = ctx.enter_context(nc.sbuf_tensor(_pn("Pbuf"), [128, 2 * 512], BF16))

        lrow = ctx.enter_context(nc.sbuf_tensor(_pn("lrow"), [1, 512], F32))
        lb = ctx.enter_context(nc.sbuf_tensor(_pn("lb"), [64, 512], F32))
        m64 = ctx.enter_context(nc.sbuf_tensor(_pn("m64"), [128, 1], F32))
        evb = ctx.enter_context(nc.sbuf_tensor(_pn("evb"), [128, 512], BF16))
        xnew = ctx.enter_context(nc.sbuf_tensor(_pn("xnew"), [128, TC], F32))
        C = alloc_common(ctx, nc, roww=512, ftmpw=512)
        pp = ctx.enter_context(nc.psum_tensor(_pn("pp"), [128, 8 * 512], F32))

        ctx_sb = harena
        p = P()
        ld, w, v, t, s, st, sd = (S[k] for k in ("ld", "w", "v", "t", "s", "st", "d"))
        s_ld, s_w, s_v, s_t, s_s, s_st, s_d = (x.h for x in (ld, w, v, t, s, st, sd))
        sems = {"v": v, "t": t, "s": s}

        init_consts_emit(p, sems, C)

        def ic2(vv):
            vv.memset(m64[:], -64.0).then_inc(s_v, 1)
        p.v(ic2); v.inc()
        v_c0 = v.n
        p.t(lambda te, v_c0=v_c0: te.wait_ge(s_v, v_c0))
        p.s(lambda sc, v_c0=v_c0: sc.wait_ge(s_v, v_c0))

        def ld_band(sy):
            sy.dma_start(band[:], band_in).then_inc(s_ld, 16)
        p.sy(ld_band); ld.inc(16)

        for mt in range(2):
            war_v = v.n
            war_t = t.n
            def xl2(sy, mt=mt, war_v=war_v, war_t=war_t):
                if mt > 0:
                    sy.wait_ge(s_v, war_v)
                    sy.wait_ge(s_t, war_t)
                src = bass.AP(xT.tensor, mt * 512, [[TC, 128], [128 * TC, NCH], [1, 512]])
                sy.dma_start(xt[:], src).then_inc(s_ld, 16)
            p.sy(xl2); ld.inc(16)
            ld_t = ld.n
            p.s(lambda sc, ld_t=ld_t: sc.wait_ge(s_ld, ld_t))
            p.t(lambda te, ld_t=ld_t: te.wait_ge(s_ld, ld_t))
            p.v(lambda vv, ld_t=ld_t: vv.wait_ge(s_ld, ld_t))
            ln_tile_emit(p, sems, xt, harena, pp, C, tcw=512, xcolstride=512,
                         hcolstride=2048, xoff=0, hoff=mt * 512)
        v_hmy = v.n
        for h in range(NH):
            buf = h % 2
            war = t.n if h >= 2 else None
            def wql(g, h=h, buf=buf, war=war):
                if war is not None:
                    g.wait_ge(s_t, war)
                src = bass.AP(wq.tensor, h * 128, [[D, 128], [128 * D, NCH], [1, 128]])
                g.dma_start(wot[:, buf * 2048:(buf + 1) * 2048], src).then_inc(s_w, 16)
            p.g(wql); w.inc(16)
            w_t = w.n

            def qmm(te, h=h, buf=buf, w_t=w_t, v_hmy=v_hmy):
                te.wait_ge(s_w, w_t)
                te.wait_ge(s_v, v_hmy)
                li = None
                for th in range(2):
                    for kc in range(NCH):
                        li = te.matmul(pp[:, th * 512:(th + 1) * 512],
                                       wot[:, buf * 2048 + kc * 128: buf * 2048 + kc * 128 + 128],
                                       harena[:, kc * 2048 + th * 512: kc * 2048 + th * 512 + 512],
                                       start=(kc == 0), stop=(kc == NCH - 1))
                li.then_inc(s_t, 1)
            p.t(qmm); t.inc()
            t_q = t.n

            def qev(vv, h=h, t_q=t_q):
                vv.wait_ge(s_t, t_q)
                li = None
                for th in range(2):
                    li = vv.tensor_copy(q1[:, h * TC + th * 512: h * TC + th * 512 + 512],
                                        pp[:, th * 512:(th + 1) * 512])
                li.then_inc(s_v, 1)
            p.v(qev); v.inc()
            v_q = v.n
            p.t(lambda te, v_q=v_q: te.wait_ge(s_v, v_q))

        def ones_ms(vv):
            vv.memset(harena[:, 16384:16384 + 4160], 1.0).then_inc(s_v, 1)
        p.v(ones_ms); v.inc()
        v_ones = v.n

        def ones_init(sy, v_ones=v_ones):
            sy.wait_ge(s_v, v_ones)
            for h in range(NH):
                dst = bass.AP(v1d.tensor, h * 4096 * 130,
                              [[130, 128], [130 * 128, 32], [1, 130]])
                src = bass.AP(harena, 16384, [[32768, 128], [0, 32], [1, 130]])
                sy.dma_start(dst, src).then_inc(s_d, 16)
        p.sy(ones_init); sd.inc(16 * NH)
        sd_ones = sd.n
        p.s(lambda sc, sd_ones=sd_ones: sc.wait_ge(s_d, sd_ones))
        p.v(lambda vv, sd_ones=sd_ones: vv.wait_ge(s_d, sd_ones))

        for half in range(2):
            for kvt in range(4):
                war_v = v.n
                war_t = t.n
                def xfl(sy, half=half, kvt=kvt, war_v=war_v, war_t=war_t):
                    sy.wait_ge(s_v, war_v)
                    sy.wait_ge(s_t, war_t)
                    kt = half * 4 + kvt
                    if xful_blocks:
                        blkid, j0 = kt // 2, (kt % 2) * 512
                        src = bass.AP(xfT.tensor, blkid * D * TC + j0,
                                      [[TC, 128], [128 * TC, NCH], [1, 512]])
                    else:
                        src = bass.AP(xfT.tensor, kt * 512,
                                      [[4096, 128], [128 * 4096, NCH], [1, 512]])
                    sy.dma_start(xt[:], src).then_inc(s_ld, 16)
                p.sy(xfl); ld.inc(16)
                ld_t = ld.n
                p.s(lambda sc, ld_t=ld_t: sc.wait_ge(s_ld, ld_t))
                p.t(lambda te, ld_t=ld_t: te.wait_ge(s_ld, ld_t))
                p.v(lambda vv, ld_t=ld_t: vv.wait_ge(s_ld, ld_t))
                ln_tile_emit(p, sems, xt, harena, pp, C, tcw=512, xcolstride=512,
                             hcolstride=2048, xoff=0, hoff=kvt * 512)
            v_h = v.n
            for mat in range(2):
                wsrc = wk if mat == 0 else wv
                for ds in range(4):
                    war = t.n
                    def wsl_l(g, ds=ds, wsrc=wsrc, war=war):
                        g.wait_ge(s_t, war)
                        src = bass.AP(wsrc.tensor, ds * 512, [[D, 128], [128 * D, NCH], [1, 512]])
                        g.dma_start(wsl[:], src).then_inc(s_w, 16)
                    p.g(wsl_l); w.inc(16)
                    w_s = w.n
                    p.t(lambda te, w_s=w_s, v_h=v_h: (te.wait_ge(s_w, w_s), te.wait_ge(s_v, v_h)))
                    for kvt in range(4):
                        for blk4 in range(4):
                            def kvmm(te, mat=mat, ds=ds, kvt=kvt, blk4=blk4):
                                li = None
                                for kc in range(NCH):
                                    if mat == 0:
                                        li = te.matmul(pp[:, 0:512],
                                                       wsl[:, kc * 512 + blk4 * 128: kc * 512 + blk4 * 128 + 128],
                                                       harena[:, kc * 2048 + kvt * 512: kc * 2048 + kvt * 512 + 512],
                                                       start=(kc == 0), stop=(kc == NCH - 1))
                                    else:
                                        li = te.matmul(pp[:, 0:512],
                                                       harena[:, kc * 2048 + kvt * 512 + blk4 * 128: kc * 2048 + kvt * 512 + blk4 * 128 + 128],
                                                       wsl[:, kc * 512:(kc + 1) * 512],
                                                       start=(kc == 0), stop=(kc == NCH - 1))
                                li.then_inc(s_t, 1)
                            p.t(kvmm); t.inc()
                            t_kv = t.n

                            def kvev(vv, t_kv=t_kv):
                                vv.wait_ge(s_t, t_kv)
                                vv.tensor_copy(evb[:], pp[:, 0:512]).then_inc(s_v, 1)
                            p.v(kvev); v.inc()
                            v_ev = v.n
                            p.t(lambda te, v_ev=v_ev: te.wait_ge(s_v, v_ev))

                            def kvst(sy, mat=mat, ds=ds, kvt=kvt, blk4=blk4, half=half,
                                     v_ev=v_ev, sd_ones=sd_ones):
                                sy.wait_ge(s_v, v_ev)
                                if mat == 0:
                                    d0 = ds * 512 + blk4 * 128
                                    dst = bass.AP(k1d.tensor, d0 * 4096 + half * 2048 + kvt * 512,
                                                  [[4096, 128], [1, 512]])
                                    sy.dma_start(dst, evb[:]).then_inc(s_d, 16)
                                else:
                                    sy.wait_ge(s_d, sd_ones)
                                    t0 = half * 2048 + kvt * 512 + blk4 * 128
                                    base = (ds * 4) * 4096 * 130 + t0 * 130
                                    for hf in range(2):
                                        dst = bass.AP(v1d.tensor, base + hf * 65,
                                                      [[130, 128], [4096 * 130, 4], [1, 64]])
                                        srcap = bass.AP(evb, hf * 64,
                                                        [[512, 128], [128, 4], [1, 64]])
                                        sy.dma_start(dst, srcap).then_inc(s_d, 16)
                            p.sy(kvst); sd.inc(16 if mat == 0 else 32)
                            sd_cur = sd.n
                            p.v(lambda vv, sd_cur=sd_cur: vv.wait_ge(s_d, sd_cur))

        sd_all = sd.n
        p.sy(lambda sy, sd_all=sd_all: sy.wait_ge(s_d, sd_all))
        ld_b = 16
        for qb in range(2):
            for h in range(NH):
                war = t.n
                def ksl(sy, h=h, war=war):
                    sy.wait_ge(s_t, war)
                    src = bass.AP(k1d.tensor, h * 128 * 4096, [[4096, 128], [1, 4096]])
                    sy.dma_start(kslab[:], src).then_inc(s_ld, 16)
                p.sy(ksl); ld.inc(16)

                def vsl(sy, h=h, war=war):
                    sy.wait_ge(s_t, war)
                    src = bass.AP(v1d.tensor, h * 4096 * 130,
                                  [[130, 128], [130 * 128, 32], [1, 130]])
                    sy.dma_start(vslab[:], src).then_inc(s_ld, 16)
                p.sy(vsl); ld.inc(16)
                ld_slab = ld.n
                p.t(lambda te, ld_slab=ld_slab: te.wait_ge(s_ld, ld_slab))
                p.v(lambda vv, ld_slab=ld_slab: vv.wait_ge(s_ld, ld_slab))

                acts_done = {}
                mask_done = {}
                pv_done = {}
                pend_pv = []
                for tbl in range(32):
                    sb = tbl % 2
                    pb = tbl % 2
                    war_sb = mask_done.get(tbl - 2)
                    def qkmm(te, h=h, qb=qb, tbl=tbl, sb=sb, war_sb=war_sb):
                        if war_sb is not None:
                            te.wait_ge(s_v, war_sb)
                        te.matmul(pp[:, sb * 512:(sb + 1) * 512],
                                  kslab[:, tbl * 128:(tbl + 1) * 128],
                                  q1[:, h * TC + qb * 512: h * TC + qb * 512 + 512],
                                  start=True, stop=True).then_inc(s_t, 1)
                    p.t(qkmm); t.inc()
                    t_qk = t.n
                    if pend_pv:
                        ptbl, pf = pend_pv.pop(0)
                        p.t(pf); t.inc()
                        pv_done[ptbl] = t.n

                    war_tb = acts_done.get(tbl - 2)
                    def pmask(vv, qb=qb, tbl=tbl, sb=sb, pb=pb, t_qk=t_qk, war_tb=war_tb):
                        vv.wait_ge(s_t, t_qk)
                        if war_tb is not None:
                            vv.wait_ge(s_s, war_tb)
                        off = 3968 + qb * 512 - tbl * 128
                        vv.tensor_tensor(Tbuf[:, pb * 512:(pb + 1) * 512],
                                         pp[:, sb * 512:(sb + 1) * 512],
                                         band[:, off:off + 512], ALU.mult).then_inc(s_v, 1)
                    p.v(pmask); v.inc()
                    v_m = v.n
                    mask_done[tbl] = v_m

                    war_pbuf = pv_done.get(tbl - 2)
                    def pexp(sc, pb=pb, v_m=v_m, war_pbuf=war_pbuf):
                        sc.wait_ge(s_v, v_m)
                        if war_pbuf is not None:
                            sc.wait_ge(s_t, war_pbuf)
                        sc.activation(Pbuf[:, pb * 512:(pb + 1) * 512],
                                      Tbuf[:, pb * 512:(pb + 1) * 512],
                                      AF.Exp, bias=m64[:], scale=1.0 / 64).then_inc(s_s, 1)
                    p.s(pexp); s.inc()
                    s_e = s.n
                    acts_done[tbl] = s_e

                    def pv(te, tbl=tbl, pb=pb, s_e=s_e):
                        te.wait_ge(s_s, s_e)
                        te.matmul(pp[:, 2 * 512: 2 * 512 + 512][0:65, :],
                                  vslab[:, tbl * 130: tbl * 130 + 65],
                                  Pbuf[:, pb * 512:(pb + 1) * 512],
                                  start=(tbl == 0), stop=(tbl == 31))
                        te.matmul(pp[:, 3 * 512: 3 * 512 + 512][0:65, :],
                                  vslab[:, tbl * 130 + 65: tbl * 130 + 130],
                                  Pbuf[:, pb * 512:(pb + 1) * 512],
                                  start=(tbl == 0), stop=(tbl == 31)).then_inc(s_t, 1)
                    pend_pv.append((tbl, pv))
                for ptbl, pf in pend_pv:
                    p.t(pf); t.inc()
                    pv_done[ptbl] = t.n
                pend_pv = []

                t_pv = t.n

                def lsetup(vv, t_pv=t_pv):
                    vv.wait_ge(s_t, t_pv)
                    vv.tensor_copy(lrow[0:1, :], pp[64:65, 2 * 512: 2 * 512 + 512])
                    vv.reciprocal(lrow[0:1, :], lrow[0:1, :]).then_inc(s_v, 1)
                p.v(lsetup); v.inc()
                v_l = v.n

                def lbc(te, v_l=v_l):
                    te.wait_ge(s_v, v_l)
                    te.matmul(pp[0:64, 4 * 512: 4 * 512 + 512], C["ones_r_f32"][0:1, 0:64],
                              lrow[0:1, :], start=True, stop=True).then_inc(s_t, 1)
                p.t(lbc); t.inc()
                t_lb = t.n

                def cdiv(vv, h=h, qb=qb, t_lb=t_lb):
                    vv.wait_ge(s_t, t_lb)
                    vv.tensor_copy(lb[:], pp[0:64, 4 * 512: 4 * 512 + 512])
                    vv.scalar_tensor_tensor(
                        ctx_sb[0:64, h * TC + qb * 512: h * TC + qb * 512 + 512],
                        pp[0:64, 2 * 512: 2 * 512 + 512], 1.0, lb[:], ALU.mult, ALU.mult)
                    vv.scalar_tensor_tensor(
                        ctx_sb[64:128, h * TC + qb * 512: h * TC + qb * 512 + 512],
                        pp[0:64, 3 * 512: 3 * 512 + 512], 1.0, lb[:], ALU.mult, ALU.mult
                    ).then_inc(s_v, 1)
                p.v(cdiv); v.inc()
                v_c = v.n
                p.t(lambda te, v_c=v_c: te.wait_ge(s_v, v_c))

        for oc in range(NCH):
            buf = oc % 2
            war = t.n if oc >= 2 else None
            def wol(g, oc=oc, buf=buf, war=war):
                if war is not None:
                    g.wait_ge(s_t, war)
                src = bass.AP(wo.tensor, oc * 128, [[D, 128], [128 * D, NCH], [1, 128]])
                g.dma_start(wot[:, buf * 2048:(buf + 1) * 2048], src).then_inc(s_w, 16)
            p.g(wol); w.inc(16)
            w_t = w.n

            st_prev = st.n
            def xl4(sy, oc=oc, st_prev=st_prev):
                if oc > 0:
                    sy.wait_ge(s_st, st_prev)
                src = bass.AP(xT.tensor, oc * 128 * TC, [[TC, 128], [1, TC]])
                sy.dma_start(xnew[:], src).then_inc(s_ld, 16)
            p.sy(xl4); ld.inc(16)
            ld_xc = ld.n

            def omm(te, oc=oc, buf=buf, w_t=w_t):
                te.wait_ge(s_w, w_t)
                li = None
                for th in range(2):
                    for ic in range(NCH):
                        li = te.matmul(pp[:, (5 + th) * 512:(6 + th) * 512],
                                       wot[:, buf * 2048 + ic * 128: buf * 2048 + ic * 128 + 128],
                                       ctx_sb[:, ic * TC + th * 512: ic * TC + th * 512 + 512],
                                       start=(ic == 0), stop=(ic == NCH - 1))
                li.then_inc(s_t, 1)
            p.t(omm); t.inc()
            t_o = t.n

            def oev(vv, oc=oc, t_o=t_o, ld_xc=ld_xc):
                vv.wait_ge(s_t, t_o)
                vv.wait_ge(s_ld, ld_xc)
                li = None
                for th in range(2):
                    li = vv.tensor_tensor(xnew[:, th * 512:(th + 1) * 512],
                                          xnew[:, th * 512:(th + 1) * 512],
                                          pp[:, (5 + th) * 512:(6 + th) * 512], ALU.add)
                li.then_inc(s_v, 1)
            p.v(oev); v.inc()
            v_o = v.n
            p.t(lambda te, v_o=v_o: te.wait_ge(s_v, v_o))

            def ost(sy, oc=oc, v_o=v_o):
                sy.wait_ge(s_v, v_o)
                dst = bass.AP(xo.tensor, oc * 128 * TC, [[TC, 128], [1, TC]])
                sy.dma_start(dst, xnew[:]).then_inc(s_st, 16)
            p.sy(ost); st.inc(16)

        st_all = st.n
        p.sy(lambda sy, st_all=st_all: sy.wait_ge(s_st, st_all))

        p.emit(blk)


def build_l0attn(stages=99):
    nc = bass.Bass("TRN2", target_bir_lowering=False)
    xT = nc.dram_tensor("xT", [D, TC], F32, kind="ExternalInput").ap()
    xfT = nc.dram_tensor("xfT", [D, 4096], F32, kind="ExternalInput").ap()
    wq = nc.dram_tensor("wq", [D, D], F32, kind="ExternalInput").ap()
    wk = nc.dram_tensor("wk", [D, 512], F32, kind="ExternalInput").ap()
    wv = nc.dram_tensor("wv", [D, 512], F32, kind="ExternalInput").ap()
    wo = nc.dram_tensor("wo", [D, D], F32, kind="ExternalInput").ap()
    bo_in = nc.dram_tensor("bo", [128, 16], F32, kind="ExternalInput").ap()
    qbits_in = nc.dram_tensor("qbits", [12, 32 * TC], BF16, kind="ExternalInput").ap()
    cosq_in = nc.dram_tensor("cosq", [32, TC], BF16, kind="ExternalInput").ap()
    sinq_in = nc.dram_tensor("sinq", [32, TC], BF16, kind="ExternalInput").ap()
    cosk_in = nc.dram_tensor("cosk", [32, 4096], BF16, kind="ExternalInput").ap()
    sink_in = nc.dram_tensor("sink", [32, 4096], BF16, kind="ExternalInput").ap()
    tbias_in = nc.dram_tensor("tbias", [128, 1024], F32, kind="ExternalInput").ap()
    band_in = nc.dram_tensor("band", [128, 4992], BF16, kind="ExternalInput").ap()
    xo = nc.dram_tensor("xo", [D, TC], F32, kind="ExternalOutput").ap()
    with ExitStack() as octx:
        S = _mk_sems_fused(octx, nc)
        blk = octx.enter_context(nc.Block())
        emit_l0attn(nc, blk, S, xT, xfT, wq, wk, wv, wo, bo_in, qbits_in, cosq_in,
                    sinq_in, cosk_in, sink_in, tbias_in, band_in, xo)
    return nc


def emit_l0attn(nc, blk, S, xT, xfT, wq, wk, wv, wo, bo_in, qbits_in, cosq_in,
                sinq_in, cosk_in, sink_in, tbias_in, band_in, xo, stages=99):
    _next_phase()
    NH = 32
    TT = 256

    with ExitStack() as ctx:
        xt = ctx.enter_context(nc.sbuf_tensor(_pn("xt"), [128, NCH * TT], F32))
        ctxa = ctx.enter_context(nc.sbuf_tensor(_pn("ctxa"), [128, 16384], BF16))
        htile = ctx.enter_context(nc.sbuf_tensor(_pn("htile"), [128, NCH * TT], BF16))
        q1 = ctx.enter_context(nc.sbuf_tensor(_pn("q1"), [76, 16 * TC], BF16))
        k1h = ctx.enter_context(nc.sbuf_tensor(_pn("k1h"), [76, 4 * 4096], BF16))
        v1h = ctx.enter_context(nc.sbuf_tensor(_pn("v1h"), [128, 32 * 260], BF16))
        wkv = ctx.enter_context(nc.sbuf_tensor(_pn("wkv"), [128, NCH * 512], BF16))
        wot = ctx.enter_context(nc.sbuf_tensor(_pn("wot"), [128, 2 * 2048], BF16))
        Tbuf = ctx.enter_context(nc.sbuf_tensor(_pn("Tbuf"), [128, 512], F32))
        Pbuf = ctx.enter_context(nc.sbuf_tensor(_pn("Pbuf"), [128, 2 * 512], BF16))
        P2 = ctx.enter_context(nc.sbuf_tensor(_pn("P2"), [128, 2 * 512], BF16))
        band = ctx.enter_context(nc.sbuf_tensor(_pn("band_sb"), [128, 4992], BF16))
        tbias = ctx.enter_context(nc.sbuf_tensor(_pn("tbias_sb"), [128, 1024], F32))
        bo_sb = ctx.enter_context(nc.sbuf_tensor(_pn("bo_sb"), [128, 16], F32))
        m30 = ctx.enter_context(nc.sbuf_tensor(_pn("m30"), [128, 1], F32))
        lrow = ctx.enter_context(nc.sbuf_tensor(_pn("lrow"), [1, 512], F32))
        lb = ctx.enter_context(nc.sbuf_tensor(_pn("lb"), [64, 512], F32))
        xnew = ctx.enter_context(nc.sbuf_tensor(_pn("xnew"), [128, TC], F32))
        C = alloc_common(ctx, nc, roww=512, ftmpw=512)
        pp = ctx.enter_context(nc.psum_tensor(_pn("pp"), [128, 8 * 512], F32))

        p = P()
        ld, w, v, t, s, st = (S[k] for k in ("ld", "w", "v", "t", "s", "st"))
        cs = S["cs"]
        s_ld, s_w, s_v, s_t, s_s, s_st = (x.h for x in (ld, w, v, t, s, st))
        s_cs = cs.h
        sems = {"v": v, "t": t, "s": s}

        init_consts_emit(p, sems, C)

        def ic2(vv):
            vv.memset(m30[:], -30.0).then_inc(s_v, 1)
        p.v(ic2); v.inc()
        v_c0 = v.n
        p.t(lambda te, v_c0=v_c0: te.wait_ge(s_v, v_c0))
        p.s(lambda sc, v_c0=v_c0: sc.wait_ge(s_v, v_c0))

        def lconsts(sy):
            sy.dma_start(band[:], band_in).then_inc(s_ld, 16)
            sy.dma_start(tbias[:], tbias_in).then_inc(s_ld, 16)
            sy.dma_start(bo_sb[:], bo_in).then_inc(s_ld, 16)
        p.sy(lconsts); ld.inc(16 * 3)
        ld_c = ld.n
        p.t(lambda te, ld_c=ld_c: te.wait_ge(s_ld, ld_c))
        p.v(lambda vv, ld_c=ld_c: vv.wait_ge(s_ld, ld_c))
        p.s(lambda sc, ld_c=ld_c: sc.wait_ge(s_ld, ld_c))

        def rope_batch(colbase, buf, cos0, cos32, sin0, sin32, width):
            A = buf[0:32, colbase:colbase + width]
            B = buf[32:64, colbase:colbase + width]
            xtv = xt[:].bitcast(BF16)
            S1 = xtv[0:32, 0:width]
            S2 = xtv[32:64, 0:width]
            S3 = xtv[32:64, 4096:4096 + width]
            S4 = xtv[0:32, 4096:4096 + width]

            def rp(vv, A=A, B=B, S1=S1, S2=S2, S3=S3, S4=S4,
                   cos0=cos0, cos32=cos32, sin0=sin0, sin32=sin32):
                vv.tensor_tensor(S1, A, cos0, ALU.mult)
                vv.tensor_tensor(S2, A, sin0, ALU.mult)
                vv.tensor_tensor(S3, B, cos32, ALU.mult)
                vv.scalar_tensor_tensor(S4, B, -1.0, sin32, ALU.mult, ALU.mult)
                vv.tensor_tensor(A, S1, S4, ALU.add)
                vv.tensor_tensor(B, S3, S2, ALU.add).then_inc(s_v, 1)
            p.v(rp); v.inc()

        for hh in range(2):
            for mt in range(4):
                war_v = v.n
                war_t = t.n
                def xl(sy, mt=mt, war_v=war_v, war_t=war_t):
                    sy.wait_ge(s_v, war_v)
                    sy.wait_ge(s_t, war_t)
                    src = bass.AP(xT.tensor, mt * TT, [[TC, 128], [128 * TC, NCH], [1, TT]])
                    sy.dma_start(xt[:], src).then_inc(s_ld, 16)
                p.sy(xl); ld.inc(16)
                ld_t = ld.n
                p.s(lambda sc, ld_t=ld_t: sc.wait_ge(s_ld, ld_t))
                p.t(lambda te, ld_t=ld_t: te.wait_ge(s_ld, ld_t))
                p.v(lambda vv, ld_t=ld_t: vv.wait_ge(s_ld, ld_t))
                ln_tile_emit(p, sems, xt, htile, pp, C, tcw=TT, xcolstride=TT,
                             hcolstride=TT, xoff=0, hoff=0)
                v_h = v.n
                for hl in range(16):
                    h = hh * 16 + hl
                    buf = hl % 2
                    war = t.n if t.n >= 1 else None
                    def wql(g, h=h, buf=buf, war=war):
                        if war is not None:
                            g.wait_ge(s_t, war)
                        src = bass.AP(wq.tensor, h * 64, [[D, 128], [128 * D, NCH], [1, 64]])
                        g.dma_start(wot[:, buf * 2048: buf * 2048 + 1024], src).then_inc(s_w, 16)
                    p.g(wql); w.inc(16)
                    w_t = w.n

                    def qmm(te, hl=hl, buf=buf, w_t=w_t, v_h=v_h):
                        te.wait_ge(s_w, w_t)
                        te.wait_ge(s_v, v_h)
                        li = None
                        for kc in range(NCH):
                            li = te.matmul(pp[0:64, 0:TT],
                                           wot[:, buf * 2048 + kc * 64: buf * 2048 + kc * 64 + 64],
                                           htile[:, kc * TT: kc * TT + TT],
                                           start=(kc == 0), stop=(kc == NCH - 1))
                        li.then_inc(s_t, 1)
                    p.t(qmm); t.inc()
                    t_q = t.n

                    def qev(vv, hl=hl, mt=mt, t_q=t_q):
                        vv.wait_ge(s_t, t_q)
                        vv.tensor_copy(q1[0:64, hl * TC + mt * TT: hl * TC + mt * TT + TT],
                                       pp[0:64, 0:TT]).then_inc(s_v, 1)
                    p.v(qev); v.inc()
                    v_q = v.n
                    p.t(lambda te, v_q=v_q: te.wait_ge(s_v, v_q))
            war_qr = t.n
            def csql(sy, war_qr=war_qr):
                sy.wait_ge(s_t, war_qr)
                srcc = bass.AP(cosq_in.tensor, 0, [[0, 4], [TC, 32], [1, TC]])
                srcs = bass.AP(sinq_in.tensor, 0, [[0, 4], [TC, 32], [1, TC]])
                sy.dma_start(htile[:, 0:TC], srcc).then_inc(s_cs, 16)
                sy.dma_start(htile[:, TC:2 * TC], srcs).then_inc(s_cs, 16)
            p.sy(csql); cs.inc(32)
            ld_cs = cs.n
            p.v(lambda vv, ld_cs=ld_cs: vv.wait_ge(s_cs, ld_cs))
            RL = NCH * TT
            for b4 in range(4):
                cos0 = bass.AP(htile, 0 * RL, [[RL, 32], [0, 4], [1, TC]])
                cos32 = bass.AP(htile, 32 * RL, [[RL, 32], [0, 4], [1, TC]])
                sin0 = bass.AP(htile, 0 * RL + TC, [[RL, 32], [0, 4], [1, TC]])
                sin32 = bass.AP(htile, 32 * RL + TC, [[RL, 32], [0, 4], [1, TC]])
                rope_batch(b4 * 4 * TC, q1, cos0, cos32, sin0, sin32, 4 * TC)
            def qbl(sy, hh=hh):
                sy.dma_start(q1[64:76, :], qbits_in[:, hh * 16 * TC:(hh + 1) * 16 * TC]).then_inc(s_ld, 16)
            p.sy(qbl); ld.inc(16)
            ld_qb = ld.n
            p.t(lambda te, ld_qb=ld_qb: te.wait_ge(s_ld, ld_qb))

            war = t.n
            def wkvl(g, hh=hh, war=war):
                g.wait_ge(s_t, war)
                srck = bass.AP(wk.tensor, hh * 256, [[512, 128], [128 * 512, NCH], [1, 256]])
                dstk = bass.AP(wkv, 0, [[NCH * 512, 128], [512, NCH], [1, 256]])
                g.dma_start(dstk, srck).then_inc(s_w, 16)
                srcv = bass.AP(wv.tensor, hh * 256, [[512, 128], [128 * 512, NCH], [1, 256]])
                dstv = bass.AP(wkv, 256, [[NCH * 512, 128], [512, NCH], [1, 256]])
                g.dma_start(dstv, srcv).then_inc(s_w, 16)
            p.g(wkvl); w.inc(32)
            w_kv = w.n
            p.t(lambda te, w_kv=w_kv: te.wait_ge(s_w, w_kv))

            war_vv = t.n
            def vset(vv, war_vv=war_vv):
                vv.wait_ge(s_t, war_vv)
                vv.memset(k1h[64:76, :], 1.0)
                vv.memset(v1h[:], 1.0).then_inc(s_v, 1)
            p.v(vset); v.inc()
            v_set = v.n
            p.t(lambda te, v_set=v_set: te.wait_ge(s_v, v_set))

            for kt in range(16):
                war_v = v.n
                war_t = t.n
                def xfl(sy, kt=kt, war_v=war_v, war_t=war_t):
                    sy.wait_ge(s_v, war_v)
                    sy.wait_ge(s_t, war_t)
                    src = bass.AP(xfT.tensor, kt * TT, [[4096, 128], [128 * 4096, NCH], [1, TT]])
                    sy.dma_start(xt[:], src).then_inc(s_ld, 16)
                p.sy(xfl); ld.inc(16)
                ld_t = ld.n
                p.s(lambda sc, ld_t=ld_t: sc.wait_ge(s_ld, ld_t))
                p.t(lambda te, ld_t=ld_t: te.wait_ge(s_ld, ld_t))
                p.v(lambda vv, ld_t=ld_t: vv.wait_ge(s_ld, ld_t))
                ln_tile_emit(p, sems, xt, htile, pp, C, tcw=TT, xcolstride=TT,
                             hcolstride=TT, xoff=0, hoff=0)
                v_h = v.n
                def kmm(te, v_h=v_h):
                    te.wait_ge(s_v, v_h)
                    li = None
                    for ch in range(2):
                        for kc in range(NCH):
                            li = te.matmul(pp[:, ch * 512: ch * 512 + TT],
                                           wkv[:, kc * 512 + ch * 128: kc * 512 + ch * 128 + 128],
                                           htile[:, kc * TT: kc * TT + TT],
                                           start=(kc == 0), stop=(kc == NCH - 1))
                    for tb in range(2):
                        for kc in range(NCH):
                            li = te.matmul(pp[:, (2 + tb) * 512: (2 + tb) * 512 + 256],
                                           htile[:, kc * TT + tb * 128: kc * TT + tb * 128 + 128],
                                           wkv[:, kc * 512 + 256: kc * 512 + 512],
                                           start=(kc == 0), stop=(kc == NCH - 1))
                    li.then_inc(s_t, 1)
                p.t(kmm); t.inc()
                t_kv = t.n

                def kvev(vv, kt=kt, t_kv=t_kv):
                    vv.wait_ge(s_t, t_kv)
                    li = None
                    for ch in range(2):
                        for gh in range(2):
                            g_loc = ch * 2 + gh
                            li = vv.tensor_copy(
                                k1h[0:64, g_loc * 4096 + kt * TT: g_loc * 4096 + kt * TT + TT],
                                pp[gh * 64:(gh + 1) * 64, ch * 512: ch * 512 + TT])
                    for tb in range(2):
                        tt_idx = kt * 2 + tb
                        for g_loc in range(4):
                            li = vv.tensor_copy(
                                v1h[:, tt_idx * 260 + g_loc * 65: tt_idx * 260 + g_loc * 65 + 64],
                                pp[:, (2 + tb) * 512 + g_loc * 64: (2 + tb) * 512 + g_loc * 64 + 64])
                    li.then_inc(s_v, 1)
                p.v(kvev); v.inc()
                v_ev = v.n
                p.t(lambda te, v_ev=v_ev: te.wait_ge(s_v, v_ev))
            RL = NCH * TT
            for th2 in range(2):
                war_kr = t.n
                war_kv = v.n
                def cskl(sy, th2=th2, war_kr=war_kr, war_kv=war_kv):
                    sy.wait_ge(s_t, war_kr)
                    sy.wait_ge(s_v, war_kv)
                    srcc = bass.AP(cosk_in.tensor, th2 * 2048, [[0, 4], [4096, 32], [1, 2048]])
                    srcs = bass.AP(sink_in.tensor, th2 * 2048, [[0, 4], [4096, 32], [1, 2048]])
                    sy.dma_start(htile[:, 0:2048], srcc).then_inc(s_cs, 16)
                    sy.dma_start(htile[:, 2048:4096], srcs).then_inc(s_cs, 16)
                p.sy(cskl); cs.inc(32)
                ld_cs2 = cs.n
                p.v(lambda vv, ld_cs2=ld_cs2: vv.wait_ge(s_cs, ld_cs2))
                for g_loc in range(4):
                    rope_batch(g_loc * 4096 + th2 * 2048, k1h,
                               bass.AP(htile, 0 * RL, [[RL, 32], [1, 2048]]),
                               bass.AP(htile, 32 * RL, [[RL, 32], [1, 2048]]),
                               bass.AP(htile, 0 * RL + 2048, [[RL, 32], [1, 2048]]),
                               bass.AP(htile, 32 * RL + 2048, [[RL, 32], [1, 2048]]), 2048)
            v_krope = v.n
            p.t(lambda te, v_krope=v_krope: te.wait_ge(s_v, v_krope))

            for qb in range(2):
                for hl in range(16):
                    h = hh * 16 + hl
                    g_loc = hl // 4
                    acts_done = {}
                    mask_done = {}
                    pv_done = {}
                    pend_pv = []
                    for tbl in range(32):
                        sb = tbl % 2
                        war_s = acts_done.get(tbl - 2)
                        def qkmm(te, hl=hl, qb=qb, tbl=tbl, sb=sb, g_loc=g_loc, war_s=war_s):
                            if war_s is not None:
                                te.wait_ge(s_s, war_s)
                            te.matmul(pp[:, sb * 512:(sb + 1) * 512],
                                      k1h[0:76, g_loc * 4096 + tbl * 128: g_loc * 4096 + tbl * 128 + 128],
                                      q1[0:76, hl * TC + qb * 512: hl * TC + qb * 512 + 512],
                                      start=True, stop=True).then_inc(s_t, 1)
                        p.t(qkmm); t.inc()
                        t_qk = t.n
                        if pend_pv:
                            ptbl, pf = pend_pv.pop(0)
                            p.t(pf); t.inc()
                            pv_done[ptbl] = t.n

                        pb = tbl % 2
                        war_pb = mask_done.get(tbl - 2)
                        def acts(sc, h=h, tbl=tbl, sb=sb, pb=pb, t_qk=t_qk, war_pb=war_pb):
                            sc.wait_ge(s_t, t_qk)
                            if war_pb is not None:
                                sc.wait_ge(s_v, war_pb)
                            sc.activation(Tbuf[:], pp[:, sb * 512:(sb + 1) * 512],
                                          AF.Tanh, bias=tbias[:, h * 32 + tbl: h * 32 + tbl + 1],
                                          scale=1.0 / 240)
                            sc.activation(Pbuf[:, pb * 512:(pb + 1) * 512], Tbuf[:], AF.Exp,
                                          bias=m30[:], scale=30.0).then_inc(s_s, 1)
                        p.s(acts); s.inc()
                        s_e = s.n
                        acts_done[tbl] = s_e

                        war_p2 = pv_done.get(tbl - 2)
                        def pmask(vv, qb=qb, tbl=tbl, pb=pb, s_e=s_e, war_p2=war_p2):
                            vv.wait_ge(s_s, s_e)
                            if war_p2 is not None:
                                vv.wait_ge(s_t, war_p2)
                            off = 3968 + qb * 512 - tbl * 128
                            vv.tensor_tensor(P2[:, pb * 512:(pb + 1) * 512],
                                             Pbuf[:, pb * 512:(pb + 1) * 512],
                                             band[:, off:off + 512],
                                             ALU.mult).then_inc(s_v, 1)
                        p.v(pmask); v.inc()
                        v_m = v.n
                        mask_done[tbl] = v_m

                        def pv(te, tbl=tbl, hl=hl, pb=pb, v_m=v_m, g_loc=g_loc):
                            te.wait_ge(s_v, v_m)
                            te.matmul(pp[0:65, (2 + hl % 2) * 512:(2 + hl % 2) * 512 + 512],
                                      v1h[:, tbl * 260 + g_loc * 65: tbl * 260 + g_loc * 65 + 65],
                                      P2[:, pb * 512:(pb + 1) * 512],
                                      start=(tbl == 0), stop=(tbl == 31)).then_inc(s_t, 1)
                        pend_pv.append((tbl, pv))
                    for ptbl, pf in pend_pv:
                        p.t(pf); t.inc()
                        pv_done[ptbl] = t.n
                    pend_pv = []

                    t_pv = t.n
                    cb = (2 + hl % 2) * 512

                    def lsetup(vv, t_pv=t_pv, cb=cb):
                        vv.wait_ge(s_t, t_pv)
                        vv.tensor_copy(lrow[0:1, :], pp[64:65, cb:cb + 512])
                        vv.reciprocal(lrow[0:1, :], lrow[0:1, :]).then_inc(s_v, 1)
                    p.v(lsetup); v.inc()
                    v_l = v.n

                    def lbc(te, v_l=v_l):
                        te.wait_ge(s_v, v_l)
                        te.matmul(pp[0:64, 4 * 512: 4 * 512 + 512], C["ones_r_f32"][0:1, 0:64],
                                  lrow[0:1, :], start=True, stop=True).then_inc(s_t, 1)
                    p.t(lbc); t.inc()
                    t_lb = t.n

                    def cdiv(vv, h=h, qb=qb, cb=cb, t_lb=t_lb):
                        vv.wait_ge(s_t, t_lb)
                        vv.tensor_copy(lb[:], pp[0:64, 4 * 512: 4 * 512 + 512])
                        vv.scalar_tensor_tensor(
                            ctxa[(h % 2) * 64:(h % 2) * 64 + 64,
                                 (h // 2) * TC + qb * 512:(h // 2) * TC + qb * 512 + 512],
                            pp[0:64, cb:cb + 512], 1.0, lb[:], ALU.mult, ALU.mult
                        ).then_inc(s_v, 1)
                    p.v(cdiv); v.inc()
                    v_c = v.n
                    p.t(lambda te, v_c=v_c: te.wait_ge(s_v, v_c))

        for oc in range(NCH):
            buf = oc % 2
            war = t.n if oc >= 2 else None
            def wol(g, oc=oc, buf=buf, war=war):
                if war is not None:
                    g.wait_ge(s_t, war)
                src = bass.AP(wo.tensor, oc * 128, [[D, 128], [128 * D, NCH], [1, 128]])
                g.dma_start(wot[:, buf * 2048:(buf + 1) * 2048], src).then_inc(s_w, 16)
            p.g(wol); w.inc(16)
            w_t = w.n

            st_prev = st.n
            def xl4(sy, oc=oc, st_prev=st_prev):
                if oc > 0:
                    sy.wait_ge(s_st, st_prev)
                src = bass.AP(xT.tensor, oc * 128 * TC, [[TC, 128], [1, TC]])
                sy.dma_start(xnew[:], src).then_inc(s_ld, 16)
            p.sy(xl4); ld.inc(16)
            ld_xc = ld.n

            def omm(te, oc=oc, buf=buf, w_t=w_t):
                te.wait_ge(s_w, w_t)
                li = None
                for th in range(2):
                    for ic in range(NCH):
                        li = te.matmul(pp[:, (5 + th) * 512:(6 + th) * 512],
                                       wot[:, buf * 2048 + ic * 128: buf * 2048 + ic * 128 + 128],
                                       ctxa[:, ic * TC + th * 512: ic * TC + th * 512 + 512],
                                       start=(ic == 0), stop=(ic == NCH - 1))
                li.then_inc(s_t, 1)
            p.t(omm); t.inc()
            t_o = t.n

            def oev(vv, oc=oc, t_o=t_o, ld_xc=ld_xc):
                vv.wait_ge(s_t, t_o)
                vv.wait_ge(s_ld, ld_xc)
                li = None
                for th in range(2):
                    vv.tensor_scalar(C["ftmp"][:, 0:512], pp[:, (5 + th) * 512:(6 + th) * 512],
                                     bo_sb[:, oc:oc + 1], None, ALU.add)
                    li = vv.scalar_tensor_tensor(xnew[:, th * 512:(th + 1) * 512],
                                                 C["ftmp"][:, 0:512], 1.0,
                                                 xnew[:, th * 512:(th + 1) * 512],
                                                 ALU.mult, ALU.add)
                li.then_inc(s_v, 1)
            p.v(oev); v.inc()
            v_o = v.n
            p.t(lambda te, v_o=v_o: te.wait_ge(s_v, v_o))

            def ost(sy, oc=oc, v_o=v_o):
                sy.wait_ge(s_v, v_o)
                dst = bass.AP(xo.tensor, oc * 128 * TC, [[TC, 128], [1, TC]])
                sy.dma_start(dst, xnew[:]).then_inc(s_st, 16)
            p.sy(ost); st.inc(16)

        st_all = st.n
        p.sy(lambda sy, st_all=st_all: sy.wait_ge(s_st, st_all))

        p.emit(blk)


def _mk_sems_fused(ctx, nc):
    S = {}
    for name in ("ld", "w", "v", "t", "s", "st", "cs", "d", "ag"):
        h = ctx.enter_context(nc.semaphore("sF_" + name))
        S[name] = Sem(h)
    return S


def barrier_emit(blk, S):
    vals = [(sem.h, sem.n) for sem in S.values() if sem.n > 0]
    for eng in ("sync", "gpsimd", "vector", "scalar", "tensor"):
        def b(e, vals=vals):
            for h, n in vals:
                e.wait_ge(h, n)
        getattr(blk, eng)(b)


def build_fused():
    nc = bass.Bass("TRN2", target_bir_lowering=False)
    TT = 256
    xT = nc.dram_tensor("xT", [D, TC], F32, kind="ExternalInput").ap()
    xfT = nc.dram_tensor("xfT", [D, 4096], F32, kind="ExternalInput").ap()
    wq0 = nc.dram_tensor("wq0", [D, D], F32, kind="ExternalInput").ap()
    wk0 = nc.dram_tensor("wk0", [D, 512], F32, kind="ExternalInput").ap()
    wv0 = nc.dram_tensor("wv0", [D, 512], F32, kind="ExternalInput").ap()
    wo0 = nc.dram_tensor("wo0", [D, D], F32, kind="ExternalInput").ap()
    bo0 = nc.dram_tensor("bo0", [128, 16], F32, kind="ExternalInput").ap()
    qbits = nc.dram_tensor("qbits", [12, 32 * TC], BF16, kind="ExternalInput").ap()
    cosq = nc.dram_tensor("cosq", [32, TC], BF16, kind="ExternalInput").ap()
    sinq = nc.dram_tensor("sinq", [32, TC], BF16, kind="ExternalInput").ap()
    cosk = nc.dram_tensor("cosk", [32, 4096], BF16, kind="ExternalInput").ap()
    sink = nc.dram_tensor("sink", [32, 4096], BF16, kind="ExternalInput").ap()
    tbias = nc.dram_tensor("tbias", [128, 1024], F32, kind="ExternalInput").ap()
    band = nc.dram_tensor("band", [128, 4992], BF16, kind="ExternalInput").ap()
    pg8 = nc.dram_tensor("pg8", [1024, 16384], U8, kind="ExternalInput").ap()
    pu8 = nc.dram_tensor("pu8", [1024, 16384], U8, kind="ExternalInput").ap()
    pd8 = nc.dram_tensor("pd8", [8192, D], U8, kind="ExternalInput").ap()
    gscT = nc.dram_tensor("gscT", [64, 16384], F32, kind="ExternalInput").ap()
    uscT = nc.dram_tensor("uscT", [64, 16384], F32, kind="ExternalInput").ap()
    dscT = nc.dram_tensor("dscT", [512, D], F32, kind="ExternalInput").ap()
    wq1 = nc.dram_tensor("wq1", [D, D], F32, kind="ExternalInput").ap()
    wk1 = nc.dram_tensor("wk1", [D, D], F32, kind="ExternalInput").ap()
    wv1 = nc.dram_tensor("wv1", [D, D], F32, kind="ExternalInput").ap()
    wo1 = nc.dram_tensor("wo1", [D, D], F32, kind="ExternalInput").ap()
    wg8 = nc.dram_tensor("wg8", [D, 8192], U8, kind="ExternalInput").ap()
    wu8 = nc.dram_tensor("wu8", [D, 8192], U8, kind="ExternalInput").ap()
    wd8 = nc.dram_tensor("wd8", [8192, D], U8, kind="ExternalInput").ap()
    gsc1 = nc.dram_tensor("gsc1", [128, 64], F32, kind="ExternalInput").ap()
    usc1 = nc.dram_tensor("usc1", [128, 64], F32, kind="ExternalInput").ap()
    dsc1 = nc.dram_tensor("dsc1", [128, 16], F32, kind="ExternalInput").ap()
    xo = nc.dram_tensor("xo", [D, TC], F32, kind="ExternalOutput").ap()

    x_a = nc.dram_tensor("x_a", [D, TC], F32, kind="Internal").ap()
    x_b = nc.dram_tensor("x_b", [D, TC], F32, kind="Internal").ap()
    x_c = nc.dram_tensor("x_c", [D, TC], F32, kind="Internal").ap()
    xful = nc.dram_tensor("xful", [4 * D, TC], F32, kind="Internal").ap()

    with ExitStack() as octx:
        S = _mk_sems_fused(octx, nc)
        blk = octx.enter_context(nc.Block())

        emit_l0attn(nc, blk, S, xT, xfT, wq0, wk0, wv0, wo0, bo0, qbits, cosq,
                    sinq, cosk, sink, tbias, band, x_a)
        barrier_emit(blk, S)
        emit_l0ffn(nc, blk, S, x_a, pg8, pu8, pd8, gscT, uscT, dscT, x_b)
        barrier_emit(blk, S)

        ag = S["ag"]
        stn = S["st"].n

        def do_ag(g, stn=stn):
            g.wait_ge(S["st"].h, stn)
            g.collective_compute(
                "AllGather", ALU.bypass, [[0, 1, 2, 3], [4, 5, 6, 7]],
                ins=[x_b], outs=[xful]).then_inc(ag.h, 16)
        blk.gpsimd(do_ag)
        ag.inc(16)
        agn = ag.n
        blk.sync(lambda sy, agn=agn: sy.wait_ge(ag.h, agn))

        emit_l1attn(nc, blk, S, x_b, xful, wq1, wk1, wv1, wo1, band, x_c,
                    xful_blocks=True)
        barrier_emit(blk, S)
        emit_l1ffn(nc, blk, S, x_c, wg8, wu8, wd8, gsc1, usc1, dsc1, xo)
    return nc


def build_fused_a():
    nc = bass.Bass("TRN2", target_bir_lowering=False)
    TT = 256
    xT = nc.dram_tensor("xT", [D, TC], F32, kind="ExternalInput").ap()
    xfT = nc.dram_tensor("xfT", [D, 4096], F32, kind="ExternalInput").ap()
    wq0 = nc.dram_tensor("wq0", [D, D], F32, kind="ExternalInput").ap()
    wk0 = nc.dram_tensor("wk0", [D, 512], F32, kind="ExternalInput").ap()
    wv0 = nc.dram_tensor("wv0", [D, 512], F32, kind="ExternalInput").ap()
    wo0 = nc.dram_tensor("wo0", [D, D], F32, kind="ExternalInput").ap()
    bo0 = nc.dram_tensor("bo0", [128, 16], F32, kind="ExternalInput").ap()
    qbits = nc.dram_tensor("qbits", [12, 32 * TC], BF16, kind="ExternalInput").ap()
    cosq = nc.dram_tensor("cosq", [32, TC], BF16, kind="ExternalInput").ap()
    sinq = nc.dram_tensor("sinq", [32, TC], BF16, kind="ExternalInput").ap()
    cosk = nc.dram_tensor("cosk", [32, 4096], BF16, kind="ExternalInput").ap()
    sink = nc.dram_tensor("sink", [32, 4096], BF16, kind="ExternalInput").ap()
    tbias = nc.dram_tensor("tbias", [128, 1024], F32, kind="ExternalInput").ap()
    band = nc.dram_tensor("band", [128, 4992], BF16, kind="ExternalInput").ap()
    pg8 = nc.dram_tensor("pg8", [1024, 16384], U8, kind="ExternalInput").ap()
    pu8 = nc.dram_tensor("pu8", [1024, 16384], U8, kind="ExternalInput").ap()
    pd8 = nc.dram_tensor("pd8", [8192, D], U8, kind="ExternalInput").ap()
    gscT = nc.dram_tensor("gscT", [64, 16384], F32, kind="ExternalInput").ap()
    uscT = nc.dram_tensor("uscT", [64, 16384], F32, kind="ExternalInput").ap()
    dscT = nc.dram_tensor("dscT", [512, D], F32, kind="ExternalInput").ap()
    xo = nc.dram_tensor("xo", [D, TC], F32, kind="ExternalOutput").ap()
    x_a = nc.dram_tensor("x_a", [D, TC], F32, kind="Internal").ap()

    with ExitStack() as octx:
        S = _mk_sems_fused(octx, nc)
        blk = octx.enter_context(nc.Block())
        emit_l0attn(nc, blk, S, xT, xfT, wq0, wk0, wv0, wo0, bo0, qbits, cosq,
                    sinq, cosk, sink, tbias, band, x_a)
        barrier_emit(blk, S)
        emit_l0ffn(nc, blk, S, x_a, pg8, pu8, pd8, gscT, uscT, dscT, xo)
    return nc


def build_fused_b():
    nc = bass.Bass("TRN2", target_bir_lowering=False)
    xT = nc.dram_tensor("xT", [D, TC], F32, kind="ExternalInput").ap()
    xfT = nc.dram_tensor("xfT", [D, 4096], F32, kind="ExternalInput").ap()
    wq1 = nc.dram_tensor("wq1", [D, D], F32, kind="ExternalInput").ap()
    wk1 = nc.dram_tensor("wk1", [D, D], F32, kind="ExternalInput").ap()
    wv1 = nc.dram_tensor("wv1", [D, D], F32, kind="ExternalInput").ap()
    wo1 = nc.dram_tensor("wo1", [D, D], F32, kind="ExternalInput").ap()
    band = nc.dram_tensor("band", [128, 4992], BF16, kind="ExternalInput").ap()
    wg8 = nc.dram_tensor("wg8", [D, 8192], U8, kind="ExternalInput").ap()
    wu8 = nc.dram_tensor("wu8", [D, 8192], U8, kind="ExternalInput").ap()
    wd8 = nc.dram_tensor("wd8", [8192, D], U8, kind="ExternalInput").ap()
    gsc1 = nc.dram_tensor("gsc1", [128, 64], F32, kind="ExternalInput").ap()
    usc1 = nc.dram_tensor("usc1", [128, 64], F32, kind="ExternalInput").ap()
    dsc1 = nc.dram_tensor("dsc1", [128, 16], F32, kind="ExternalInput").ap()
    xo = nc.dram_tensor("xo", [D, TC], F32, kind="ExternalOutput").ap()
    x_c = nc.dram_tensor("x_c", [D, TC], F32, kind="Internal").ap()

    with ExitStack() as octx:
        S = _mk_sems_fused(octx, nc)
        blk = octx.enter_context(nc.Block())
        emit_l1attn(nc, blk, S, xT, xfT, wq1, wk1, wv1, wo1, band, x_c)
        barrier_emit(blk, S)
        emit_l1ffn(nc, blk, S, x_c, wg8, wu8, wd8, gsc1, usc1, dsc1, xo)
    return nc


from concourse.bass_utils import run_bass_kernel_spmd

_SIG = perm_evens_odds(2048)
_TAU = perm_evens_odds(16384)
_INV_SIG = np.argsort(_SIG)


def _u8view(a_int32):
    a = np.ascontiguousarray(a_int32.astype(np.int32))
    return a.view(np.uint8)[:, ::4]


def _stage_l0attn_consts(q0):
    NH, Dh = 32, 64
    slopes = 2.0 ** (-(np.arange(1, NH + 1) * 8.0 / NH))
    qpos = q0 + np.arange(1024)
    qbits = np.zeros((12, NH * 1024), dtype=np.float32)
    for b in range(12):
        bit = (qpos >> b) & 1
        for h in range(NH):
            qbits[b, h * 1024:(h + 1) * 1024] = -slopes[h] * (1 << b) * bit
    inv = 1.0 / 10000.0 ** (np.arange(0, Dh, 2) / Dh)
    frq = qpos[:, None] * inv[None, :]
    cosq = np.ascontiguousarray(np.cos(frq).T).astype(BF)
    sinq = np.ascontiguousarray(np.sin(frq).T).astype(BF)
    tpos = np.arange(4096)
    frk = tpos[:, None] * inv[None, :]
    cosk = np.ascontiguousarray(np.cos(frk).T).astype(BF)
    sink = np.ascontiguousarray(np.sin(frk).T).astype(BF)
    tbias = np.zeros((128, 1024), dtype=np.float32)
    pc = np.arange(128)
    for h in range(NH):
        for tbl in range(32):
            tbias[:, h * 32 + tbl] = slopes[h] * (tbl * 128 + pc) / 240.0
    u = np.arange(4992)[None, :]
    band = (u >= pc[:, None] + 3968 - q0).astype(np.float32).astype(BF)
    return {"qbits": qbits.astype(BF), "cosq": cosq, "sinq": sinq,
            "cosk": cosk, "sink": sink, "tbias": tbias, "band": band}


def kernel(L0_attn_hidden_states, L0_attn_Wq, L0_attn_Wk, L0_attn_Wv, L0_attn_Wo,
           L0_attn_bo, L0_ffn_gate_W_packed, L0_ffn_gate_scales, L0_ffn_up_W_packed,
           L0_ffn_up_scales, L0_ffn_down_W_packed, L0_ffn_down_scales,
           L1_attn_Wq, L1_attn_Wk, L1_attn_Wv, L1_attn_Wo,
           L1_ffn_gate_w, L1_ffn_gate_scale, L1_ffn_up_w, L1_ffn_up_scale,
           L1_ffn_down_w, L1_ffn_down_scale):
    x = np.asarray(L0_attn_hidden_states, dtype=np.float32)
    sig, tau, inv = _SIG, _TAU, _INV_SIG
    Dh = 64
    rho = np.concatenate([np.arange(0, Dh, 2), np.arange(1, Dh, 2)])
    colp_q = np.concatenate([h * Dh + rho for h in range(32)])
    colp_k = np.concatenate([g * Dh + rho for g in range(8)])

    wq0 = np.ascontiguousarray(np.asarray(L0_attn_Wq, np.float32)[sig][:, colp_q])
    wk0 = np.ascontiguousarray(np.asarray(L0_attn_Wk, np.float32)[sig][:, colp_k])
    wv0 = np.ascontiguousarray(np.asarray(L0_attn_Wv, np.float32)[sig])
    wo0 = np.ascontiguousarray(np.asarray(L0_attn_Wo, np.float32)[:, sig])
    bo0 = np.ascontiguousarray(np.asarray(L0_attn_bo, np.float32)[sig].reshape(16, 128).T)
    consts = [_stage_l0attn_consts(q0 * 1024) for q0 in range(4)]

    pg8 = np.ascontiguousarray(_u8view(np.asarray(L0_ffn_gate_W_packed)).T[:, tau])
    pu8 = np.ascontiguousarray(_u8view(np.asarray(L0_ffn_up_W_packed)).T[:, tau])
    pd8 = np.ascontiguousarray(_u8view(np.asarray(L0_ffn_down_W_packed)).T[:, sig])
    gscT = np.ascontiguousarray(np.asarray(L0_ffn_gate_scales, np.float32).T[:, tau])
    uscT = np.ascontiguousarray(np.asarray(L0_ffn_up_scales, np.float32).T[:, tau])
    dscT = np.ascontiguousarray(np.asarray(L0_ffn_down_scales, np.float32).T[:, sig])

    wq1 = np.ascontiguousarray(np.asarray(L1_attn_Wq, np.float32)[sig])
    wk1 = np.ascontiguousarray(np.asarray(L1_attn_Wk, np.float32)[sig])
    wv1 = np.ascontiguousarray(np.asarray(L1_attn_Wv, np.float32)[sig])
    wo1 = np.ascontiguousarray(np.asarray(L1_attn_Wo, np.float32)[:, sig])

    wg8 = np.ascontiguousarray(_u8view(np.asarray(L1_ffn_gate_w)).T[sig])
    wu8 = np.ascontiguousarray(_u8view(np.asarray(L1_ffn_up_w)).T[sig])
    wd8 = np.ascontiguousarray(_u8view(np.asarray(L1_ffn_down_w)).T[:, sig])
    gsc1 = np.ascontiguousarray(np.asarray(L1_ffn_gate_scale, np.float32).reshape(64, 128).T)
    usc1 = np.ascontiguousarray(np.asarray(L1_ffn_up_scale, np.float32).reshape(64, 128).T)
    dsc1 = np.ascontiguousarray(np.asarray(L1_ffn_down_scale, np.float32)[sig].reshape(16, 128).T)

    nc_a = build_fused_a()
    in_maps = []
    for c in range(8):
        b, qi = c // 4, c % 4
        xb = x[b]
        im = {"xT": np.ascontiguousarray(xb[qi * 1024:(qi + 1) * 1024].T[sig]),
              "xfT": np.ascontiguousarray(xb.T[sig]),
              "wq0": wq0, "wk0": wk0, "wv0": wv0, "wo0": wo0, "bo0": bo0,
              "qbits": consts[qi]["qbits"], "cosq": consts[qi]["cosq"],
              "sinq": consts[qi]["sinq"], "cosk": consts[qi]["cosk"],
              "sink": consts[qi]["sink"], "tbias": consts[qi]["tbias"],
              "band": consts[qi]["band"],
              "pg8": pg8, "pu8": pu8, "pd8": pd8, "gscT": gscT, "uscT": uscT,
              "dscT": dscT}
        in_maps.append(im)
    res = run_bass_kernel_spmd(nc_a, in_maps, core_ids=list(range(8)))
    shards = [res.results[c]["xo"] for c in range(8)]

    nc_b = build_fused_b()
    in_maps = []
    for c in range(8):
        b, qi = c // 4, c % 4
        xfT = np.ascontiguousarray(np.concatenate(shards[b * 4:(b + 1) * 4], axis=1))
        in_maps.append({"xT": np.ascontiguousarray(shards[c]), "xfT": xfT,
                        "wq1": wq1, "wk1": wk1, "wv1": wv1, "wo1": wo1,
                        "band": consts[qi]["band"],
                        "wg8": wg8, "wu8": wu8, "wd8": wd8, "gsc1": gsc1,
                        "usc1": usc1, "dsc1": dsc1})
    res = run_bass_kernel_spmd(nc_b, in_maps, core_ids=list(range(8)))
    out = np.empty((2, 4096, 2048), dtype=np.float32)
    for c in range(8):
        b, qi = c // 4, c % 4
        out[b, qi * 1024:(qi + 1) * 1024] = res.results[c]["xo"][inv].T
    return out
